# version 05
# baseline (speedup 1.0000x reference)
import numpy as np
from contextlib import ExitStack

import concourse.bass as bass
import concourse.mybir as mybir
from concourse.bass_utils import run_bass_kernel_spmd

F32 = mybir.dt.float32
BF16 = mybir.dt.bfloat16
I8 = mybir.dt.int8
AF = mybir.ActivationFunctionType
ALU = mybir.AluOpType

NCORES = 8
B = 2
L = 4096
D = 2048
TOK = B * L
TC = TOK // NCORES
NDT = D // 128
RK = 128
NST = 16
DS = D // NCORES
H = 16
HD = D // H
NEXP = 16
FI = 768
FF = 8192
EPS = 1e-5


class Ctr:
    __slots__ = ("n",)

    def __init__(self):
        self.n = 0

    def add(self, k=1):
        self.n += k
        return self.n


def dram_bcast_ap(dram_ap, row, col0, ncols, nrows=128):
    t = dram_ap.tensor
    rowlen = dram_ap.shape[-1]
    off = row * rowlen + col0
    return bass.AP(tensor=t, offset=off, ap=[[0, nrows], [1, ncols]])


def emit_idents(gp, idn, po, cpo):
    gp.memset(idn["ident128"][:], 1.0).then_inc(po, 1); cpo.add()
    gp.memset(idn["ident32"][:], 1.0).then_inc(po, 1); cpo.add()
    gp.memset(idn["ones128"][:], 1.0).then_inc(po, 1); cpo.add()
    gp.memset(idn["ones1r"][:], 1.0).then_inc(po, 1); cpo.add()
    gp.wait_ge(po, cpo.n)
    gp.affine_select(idn["ident128"][:], idn["ident128"][:], pattern=[[1, 128]],
                     compare_op=ALU.is_equal, fill=0.0, base=0,
                     channel_multiplier=-1).then_inc(po, 1); cpo.add()
    gp.affine_select(idn["ident32"][:], idn["ident32"][:], pattern=[[1, 32]],
                     compare_op=ALU.is_equal, fill=0.0, base=0,
                     channel_multiplier=-1).then_inc(po, 1); cpo.add()
    gp.memset(idn["eps1"][:], EPS).then_inc(po, 1); cpo.add()
    gp.memset(idn["one1"][:], 1.0).then_inc(po, 1); cpo.add()
    gp.memset(idn["ident16"][:], 1.0).then_inc(po, 1); cpo.add()
    gp.wait_ge(po, cpo.n)
    gp.affine_select(idn["ident16"][:], idn["ident16"][:], pattern=[[1, 16]],
                     compare_op=ALU.is_equal, fill=0.0, base=0,
                     channel_multiplier=-1).then_inc(po, 1); cpo.add()


def alloc_idents(nc, stack):
    sb = lambda n_, s, d: stack.enter_context(nc.sbuf_tensor(n_, s, d))
    return dict(ident128=sb("ident128", [128, 128], F32),
                ident32=sb("ident32", [32, 32], F32),
                ident16=sb("ident16", [16, 16], F32),
                ones128=sb("ones128", [128, 1], F32),
                ones1r=sb("ones1r", [1, 128], F32),
                eps1=sb("eps1", [1, 1], F32),
                one1=sb("one1", [1, 1], F32))



def build_launch_A(env=None):
    if env is None:
        nc = bass.Bass("TRN2", target_bir_lowering=False)
        xa = nc.dram_tensor("xa", [TC, D], F32, kind="ExternalInput").ap()
        xpw = nc.dram_tensor("xpw", [RK + 2 * NST, D], F32,
                             kind="ExternalInput").ap()
        xn_out = nc.dram_tensor("xn_out", [D, TC], F32,
                                kind="ExternalOutput").ap()
        xp_out = nc.dram_tensor("xp_out", [RK + 2 * NST, TC], F32,
                                kind="ExternalOutput").ap()
        stats_d = nc.dram_tensor("stats_d", [2, TC], F32).ap()
        mksem = None
    else:
        nc = env["nc"]
        xa = env["xa"]; xpw = env["xpw"]
        xn_out = env["xn_out"]; xp_out = env["xp_out"]
        stats_d = env["stats_d"]
        mksem = env["mksem"]

    stack = ExitStack()
    with stack:
        sb = lambda n_, s, d: stack.enter_context(nc.sbuf_tensor("A" + n_, s, d))
        _psn = [0]
        def ps(s):
            _psn[0] += 1
            return stack.enter_context(nc.psum_tensor(f"Aps{_psn[0]}", s, F32))
        if mksem is None:
            sem = lambda n_: stack.enter_context(nc.semaphore(n_))
        else:
            sem = lambda n_: mksem("A" + n_)

        xtm = [sb(f"xtm{i}", [128, D], F32) for i in range(2)]
        xfm = [sb(f"xfm{i}", [128, TC], F32) for i in range(NDT)]
        xnbf = [sb(f"xnbf{i}", [128, TC], BF16) for i in range(NDT)]
        sq = [sb(f"sq{i}", [128, TC], F32) for i in range(2)]
        xpw_lo = sb("xpw_lo", [128, D], F32)
        xpw_hi = sb("xpw_hi", [32, D], F32)
        xpwT = sb("xpwT", [128, NDT, 160], BF16)
        st_s = sb("st_s", [1, TC], F32)
        st_q = sb("st_q", [1, TC], F32)
        r_rstd = sb("r_rstd", [1, TC], F32)
        r_mrstd = sb("r_mrstd", [1, TC], F32)
        r_var = sb("r_var", [1, TC], F32)
        rstd_b = sb("rstd_b", [128, TC], F32)
        mrstd_b = sb("mrstd_b", [128, TC], F32)
        xnf = [sb(f"xnf{i}", [128, TC], F32) for i in range(2)]
        xp_lo = sb("xp_lo", [128, TC], F32)
        xp_hi = sb("xp_hi", [32, TC], F32)
        idn = env["idn"] if env is not None else alloc_idents(nc, stack)

        ptr = [ps([128, 128]) for _ in range(4)]
        stat_stack = ExitStack()
        pstat = stat_stack.enter_context(nc.psum_tensor("pstat", [1, TC], F32))
        pstatq = stat_stack.enter_context(nc.psum_tensor("pstatq", [1, TC], F32))

        ld = sem("ld"); mm = sem("mm"); ve = sem("ve"); po = sem("po")
        ac = sem("ac"); st = sem("st")
        lx2 = [sem("lxA"), sem("lxB")]
        sx2 = [sem("sxA"), sem("sxB")]
        csx = [Ctr(), Ctr()]
        cld = Ctr(); cmm = Ctr(); cve = Ctr(); cpo = Ctr()
        cac = Ctr(); cst = Ctr()

        with nc.Block() as b:
            @b.sync
            def _(sy):
                sy.dma_start(xpw_lo[:], xpw[0:128, :]).then_inc(ld, 16); cld.add()
                sy.dma_start(xpw_hi[:], xpw[128:160, :]).then_inc(ld, 16); cld.add()

            @b.gpsimd
            def _(gp):
                if env is None:
                    emit_idents(gp, idn, po, cpo)

        ld0, mm0, ve0 = cld.n, cmm.n, cve.n
        with nc.Block() as b:
            @b.sync
            def _(sy):
                for tt in range(8):
                    if tt >= 2:
                        sy.wait_ge(ve, ve0 + (tt - 1) * NDT)
                    sy.dma_start(xtm[tt % 2][:], xa[tt * 128:(tt + 1) * 128, :]
                                 ).then_inc(lx2[tt % 2], 16)

            @b.tensor
            def _(te):
                for tt in range(8):
                    te.wait_ge(lx2[tt % 2], 16 * (tt // 2 + 1))
                    for dt in range(NDT):
                        k = tt * NDT + dt
                        if k >= 4:
                            te.wait_ge(ve, ve0 + k - 3)
                        te.transpose(ptr[k % 4][:],
                                     xtm[tt % 2][:, dt * 128:(dt + 1) * 128],
                                     idn["ident128"][:]).then_inc(mm, 1); cmm.add()

            @b.vector
            def _(veg):
                for tt in range(8):
                    for dt in range(NDT):
                        k = tt * NDT + dt
                        veg.wait_ge(mm, mm0 + k + 1)
                        veg.tensor_copy(xfm[dt][:, tt * 128:(tt + 1) * 128],
                                        ptr[k % 4][:]).then_inc(ve, 1); cve.add()

        ac0, mm0 = cac.n, cmm.n
        with nc.Block() as b:
            @b.scalar
            def _(sc):
                for dt in range(NDT):
                    if dt >= 2:
                        sc.wait_ge(mm, mm0 + (dt - 1))
                    sc.activation(sq[dt % 2][:], xfm[dt][:], AF.Square
                                  ).then_inc(ac, 1); cac.add()

            @b.tensor
            def _(te):
                for dt in range(NDT):
                    te.wait_ge(ac, ac0 + dt + 1)
                    for tchunk in range(2):
                        t0 = tchunk * 512
                        te.matmul(pstat[:, t0:t0 + 512], idn["ones128"][:],
                                  xfm[dt][:, t0:t0 + 512],
                                  start=(dt == 0), stop=(dt == NDT - 1))
                        last = te.matmul(pstatq[:, t0:t0 + 512],
                                         idn["ones128"][:],
                                         sq[dt % 2][:, t0:t0 + 512],
                                         start=(dt == 0), stop=(dt == NDT - 1))
                    last.then_inc(mm, 1); cmm.add()

        ve0, ac0 = cve.n, cac.n
        with nc.Block() as b:
            @b.vector
            def _(veg):
                veg.tensor_scalar_mul(st_s[:], pstat[:], 1.0 / D
                                      ).then_inc(ve, 1); cve.add()
                veg.tensor_scalar_mul(st_q[:], pstatq[:], 1.0 / D
                                      ).then_inc(ve, 1); cve.add()
                veg.wait_ge(ve, ve0 + 2)
                veg.tensor_tensor(r_var[:], st_s[:], st_s[:],
                                  ALU.mult).then_inc(ve, 1); cve.add()
                veg.wait_ge(ve, ve0 + 3)
                veg.tensor_tensor(r_var[:], st_q[:], r_var[:],
                                  ALU.subtract).then_inc(ve, 1); cve.add()

            @b.scalar
            def _(sc):
                sc.wait_ge(ve, ve0 + 4)
                sc.activation(r_rstd[:], r_var[:], AF.Ln, bias=idn["eps1"][:]
                              ).then_inc(ac, 1); cac.add()
                sc.wait_ge(ac, ac0 + 1)
                sc.activation(r_rstd[:], r_rstd[:], AF.Exp, scale=-0.5
                              ).then_inc(ac, 1); cac.add()

        ve0, ld0 = cve.n, cld.n
        with nc.Block() as b:
            @b.vector
            def _(veg):
                veg.tensor_tensor(r_mrstd[:], st_s[:], r_rstd[:],
                                  ALU.mult).then_inc(ve, 1); cve.add()

            @b.sync
            def _(sy):
                sy.wait_ge(ve, ve0 + 1)
                sy.dma_start(stats_d[0:1, :], r_rstd[:]).then_inc(ld, 16); cld.add()
                sy.dma_start(stats_d[1:2, :], r_mrstd[:]).then_inc(ld, 16); cld.add()
                sy.wait_ge(ld, 16 * cld.n)
                sy.dma_start(rstd_b[:], dram_bcast_ap(stats_d, 0, 0, TC)
                             ).then_inc(ld, 16); cld.add()
                sy.dma_start(mrstd_b[:], dram_bcast_ap(stats_d, 1, 0, TC)
                             ).then_inc(ld, 16); cld.add()

        ve0, st0, mm0, po0 = cve.n, cst.n, cmm.n, cpo.n
        with nc.Block() as b:
            @b.vector
            def _(veg):
                veg.wait_ge(ld, 16 * cld.n)
                for dt in range(NDT):
                    if dt >= 2:
                        veg.wait_ge(sx2[dt % 2], 16 * (dt // 2))
                    veg.wait_ge(ve, cve.n)
                    veg.tensor_tensor(xnf[dt % 2][:], xfm[dt][:], rstd_b[:],
                                      ALU.mult).then_inc(ve, 1); cve.add()
                    veg.wait_ge(ve, cve.n)
                    veg.tensor_tensor(xnf[dt % 2][:], xnf[dt % 2][:],
                                      mrstd_b[:], ALU.subtract
                                      ).then_inc(ve, 1); cve.add()
                    veg.wait_ge(ve, cve.n)
                    veg.tensor_copy(xnbf[dt][:], xnf[dt % 2][:]
                                    ).then_inc(ve, 1); cve.add()

            @b.sync
            def _(sy):
                for dt in range(NDT):
                    sy.wait_ge(ve, ve0 + 3 * dt + 2)
                    sy.dma_start(xn_out[dt * 128:(dt + 1) * 128, :],
                                 xnf[dt % 2][:]).then_inc(sx2[dt % 2], 16)
                    csx[dt % 2].add()

            @b.tensor
            def _(te):
                for dt in range(NDT):
                    if dt >= 2:
                        te.wait_ge(po, po0 + 2 * (dt - 1))
                    te.transpose(ptr[(2 * dt) % 4][:],
                                 xpw_lo[:, dt * 128:(dt + 1) * 128],
                                 idn["ident128"][:]).then_inc(mm, 1); cmm.add()
                    te.transpose(ptr[(2 * dt + 1) % 4][0:128, 0:32],
                                 xpw_hi[:, dt * 128:(dt + 1) * 128],
                                 idn["ident32"][:]).then_inc(mm, 1); cmm.add()

            @b.scalar
            def _(sc):
                for dt in range(NDT):
                    sc.wait_ge(mm, mm0 + 2 * dt + 1)
                    sc.copy(xpwT[:, dt, 0:128], ptr[(2 * dt) % 4][:]
                            ).then_inc(po, 1); cpo.add()
                    sc.wait_ge(mm, mm0 + 2 * dt + 2)
                    sc.copy(xpwT[:, dt, 128:160],
                            ptr[(2 * dt + 1) % 4][0:128, 0:32]
                            ).then_inc(po, 1); cpo.add()

        stat_stack.close()
        pxp = [ps([128, 512]) for _ in range(2)]
        pxph = [ps([32, 512]) for _ in range(2)]
        mm0, ve0, st0 = cmm.n, cve.n, cst.n
        with nc.Block() as b:
            @b.tensor
            def _(te):
                for tchunk in range(2):
                    t0 = tchunk * 512
                    for dt in range(NDT):
                        te.matmul(pxp[tchunk][:], xpwT[:, dt, 0:128],
                                  xnbf[dt][:, t0:t0 + 512],
                                  start=(dt == 0), stop=(dt == NDT - 1))
                        last = te.matmul(pxph[tchunk][:], xpwT[:, dt, 128:160],
                                         xnbf[dt][:, t0:t0 + 512],
                                         start=(dt == 0), stop=(dt == NDT - 1))
                    last.then_inc(mm, 1); cmm.add()

            @b.vector
            def _(veg):
                for tchunk in range(2):
                    t0 = tchunk * 512
                    veg.wait_ge(mm, mm0 + tchunk + 1)
                    veg.tensor_copy(xp_lo[:, t0:t0 + 512], pxp[tchunk][:]
                                    ).then_inc(ve, 1); cve.add()
                    veg.tensor_copy(xp_hi[:, t0:t0 + 512], pxph[tchunk][:]
                                    ).then_inc(ve, 1); cve.add()

            @b.sync
            def _(sy):
                sy.wait_ge(ve, ve0 + 4)
                sy.dma_start(xp_out[0:128, :], xp_lo[:]).then_inc(st, 16); cst.add()
                sy.dma_start(xp_out[128:160, :], xp_hi[:]).then_inc(st, 16); cst.add()
                sy.wait_ge(st, 16 * cst.n)
                sy.wait_ge(sx2[0], 16 * csx[0].n)
                sy.wait_ge(sx2[1], 16 * csx[1].n)

    return nc





B_NST = NST
B_ACCUM = True


def build_launch_B(env=None):
    if env is None:
        nc = bass.Bass("TRN2", target_bir_lowering=False)
        xn_ds = nc.dram_tensor("xn_ds", [DS, TOK], F32, kind="ExternalInput").ap()
        xp_full = nc.dram_tensor("xp_full", [160, TOK], F32,
                                 kind="ExternalInput").ap()
        dtw_sl = nc.dram_tensor("dtw_sl", [DS, RK], F32, kind="ExternalInput").ap()
        dtb_sl = nc.dram_tensor("dtb_sl", [DS, 1], F32, kind="ExternalInput").ap()
        alog_sl = nc.dram_tensor("alog_sl", [DS, NST], F32,
                                 kind="ExternalInput").ap()
        dp_sl = nc.dram_tensor("dp_sl", [DS, 1], F32, kind="ExternalInput").ap()
        attn0_out = nc.dram_tensor("attn0_out", [DS, TOK], F32,
                                   kind="ExternalOutput").ap()
        y_dram = nc.dram_tensor("y_dram", [DS, TOK], F32).ap()
        mksem = None
        xn_a2a = xp_ag = attn0_stage = None
    else:
        nc = env["nc"]
        dtw_sl = env["dtw_sl"]; dtb_sl = env["dtb_sl"]
        alog_sl = env["alog_sl"]; dp_sl = env["dp_sl"]
        y_dram = env["y_dram"]
        xn_a2a = env["xn_a2a"]
        xp_ag = env["xp_ag"]
        attn0_stage = env["attn0_stage"]
        mksem = env["mksem"]
        xn_ds = xp_full = attn0_out = None
    xpbc_bf = nc.dram_tensor("xpbc_bf", [8 * 32, TC], BF16).ap()

    def xn_h_ap(dtile, bb):
        if env is None:
            return xn_ds[dtile * 128:(dtile + 1) * 128, bb * L:(bb + 1) * L]
        return bass.AP(tensor=xn_a2a.tensor,
                       offset=(bb * 4 * 256 + dtile * 128) * TC,
                       ap=[[TC, 128], [256 * TC, 4], [1, TC]])

    def dtl_ap():
        if env is None:
            return xp_full[0:128, :]
        return bass.AP(tensor=xp_ag.tensor, offset=0,
                       ap=[[TC, 128], [160 * TC, 8], [1, TC]])

    def bc_bcast_ap(row, bb):
        return bass.AP(tensor=xpbc_bf.tensor,
                       offset=(bb * 4 * 32 + (row - 128)) * TC,
                       ap=[[0, 128], [32 * TC, 4], [1, TC]])

    HB = L

    stack = ExitStack()
    with stack:
        sb = lambda n_, s, d: stack.enter_context(nc.sbuf_tensor("B" + n_, s, d))
        if mksem is None:
            sem = lambda n_: stack.enter_context(nc.semaphore(n_))
        else:
            sem = lambda n_: mksem("B" + n_)

        dtw_sb = sb("dtw_sb", [128, 2, RK], F32)
        dtwT = sb("dtwT", [128, 2, 128], BF16)
        dtb_sb = sb("dtb_sb", [128, 2], F32)
        a_sb = sb("a_sb", [128, 2, NST], F32)
        dp_sb = sb("dp_sb", [128, 2], F32)
        dtl_bf = sb("dtl_bf", [128, TOK], BF16)
        dt_h = sb("dt_h", [128, HB], F32)
        xn_h = sb("xn_h", [128, HB], F32)
        dtx = sb("dtx", [128, HB], BF16)
        bb = sb("bb", [128, HB], BF16)
        cb = sb("cb", [128, HB], BF16)
        ab = sb("ab", [128, HB], BF16)
        uu = sb("uu", [128, HB], BF16)
        hh = sb("hh", [128, HB], BF16)
        hc = [sb(f"hc{i}", [128, HB], F32) for i in range(2)]
        sp_t = sb("sp_t", [128, 512], F32)
        rl_t = sb("rl_t", [128, 512], F32)
        yrd = sb("yrd", [128, HB], F32)
        idn = env["idn"] if env is not None else alloc_idents(nc, stack)

        pdt = [stack.enter_context(nc.psum_tensor(f"Bpdt{i}", [128, 512], F32))
               for i in range(2)]
        ptA = stack.enter_context(nc.psum_tensor("BptA", [128, 128], F32))
        ptB = stack.enter_context(nc.psum_tensor("BptB", [128, 128], F32))

        ld = sem("ld"); mm = sem("mm"); ve = sem("ve"); po = sem("po")
        ac = sem("ac"); st = sem("st"); ay = sem("ay"); gld = sem("gld")
        lx = sem("lx"); lb = sem("lb"); lc = sem("lc"); ly = sem("ly")
        cld = Ctr(); cmm = Ctr(); cve = Ctr(); cpo = Ctr()
        cac = Ctr(); cst = Ctr(); cay = Ctr(); cgld = Ctr()
        clx = Ctr(); clb = Ctr(); clc = Ctr(); cly = Ctr()

        def vop(veg, fn, *waits):
            for (s, v) in waits:
                veg.wait_ge(s, v)
            veg.wait_ge(ve, cve.n)
            fn().then_inc(ve, 1)
            cve.add()

        def aop(sc, fn, *waits):
            for (s, v) in waits:
                sc.wait_ge(s, v)
            sc.wait_ge(ac, cac.n)
            fn().then_inc(ac, 1)
            cac.add()

        with nc.Block() as b:
            @b.sync
            def _(sy):
                sy.dma_start(dtw_sb[:].rearrange("p a r -> p a r"),
                             dtw_sl.rearrange("(a p) r -> p a r", p=128)
                             ).then_inc(ld, 16); cld.add()
                sy.dma_start(dtb_sb[:],
                             dtb_sl.rearrange("(a p) one -> p (a one)", p=128),
                             allow_slow_non_contiguous=True
                             ).then_inc(ld, 16); cld.add()
                sy.dma_start(a_sb[:],
                             alog_sl.rearrange("(a p) n -> p a n", p=128)
                             ).then_inc(ld, 16); cld.add()
                sy.dma_start(dp_sb[:],
                             dp_sl.rearrange("(a p) one -> p (a one)", p=128),
                             allow_slow_non_contiguous=True
                             ).then_inc(ld, 16); cld.add()

            @b.gpsimd
            def _(gp):
                gp.dma_start(dtl_bf[:], dtl_ap()).then_inc(gld, 16)
                cgld.add()
                if env is None:
                    bc_src = bass.AP(tensor=xp_full.tensor,
                                     offset=128 * TOK,
                                     ap=[[TC, 8], [TOK, 32], [1, TC]])
                else:
                    bc_src = bass.AP(tensor=xp_ag.tensor, offset=128 * TC,
                                     ap=[[160 * TC, 8], [TC, 32], [1, TC]])
                gp.dma_start(xpbc_bf[:], bc_src).then_inc(gld, 16)
                cgld.add()
                if env is None:
                    emit_idents(gp, idn, po, cpo)

        ac0 = cac.n
        with nc.Block() as b:
            @b.scalar
            def _(sc):
                sc.wait_ge(ld, 16 * cld.n)
                aop(sc, lambda: sc.activation(a_sb[:], a_sb[:], AF.Exp))

            @b.vector
            def _(veg):
                vop(veg, lambda: veg.tensor_scalar_mul(a_sb[:], a_sb[:], -1.0),
                    (ac, ac0 + 1))
                for dtile in range(2):
                    vop(veg, lambda dtile=dtile: veg.tensor_copy(
                        dtwT[:, dtile, :], [ptA, ptB][dtile][:]),
                        (mm, cmm.n + dtile + 1))

            @b.tensor
            def _(te):
                te.wait_ge(ld, 16 * cld.n)
                for dtile in range(2):
                    te.transpose([ptA, ptB][dtile][:], dtw_sb[:, dtile, :],
                                 idn["ident128"][:]).then_inc(mm, 1)
                    cmm.add()

        for dtile in range(2):
            for bb_i in range(B):
                t0 = bb_i * HB
                drow0 = dtile * 128
                mm0, ve0, ac0, ld0, st0 = cmm.n, cve.n, cac.n, cld.n, cst.n

                with nc.Block() as b:
                    @b.sync
                    def _(sy):
                        sy.dma_start(xn_h[:], xn_h_ap(dtile, bb_i)
                                     ).then_inc(lx, 16); clx.add()

                    @b.tensor
                    def _(te):
                        te.wait_ge(gld, 16 * cgld.n)
                        for ch in range(8):
                            if ch >= 2:
                                te.wait_ge(ac, ac0 + 3 * (ch - 1))
                                te.wait_ge(ve, ve0 + 2 * (ch - 1))
                            te.matmul(pdt[ch % 2][:],
                                      dtwT[:, dtile, :],
                                      dtl_bf[:, t0 + ch * 512:t0 + (ch + 1) * 512],
                                      start=True, stop=True).then_inc(mm, 1)
                            cmm.add()

                    @b.scalar
                    def _(sc):
                        for ch in range(8):
                            c0 = ch * 512
                            aop(sc, lambda ch=ch, c0=c0: sc.activation(
                                sp_t[:, 0:512], pdt[ch % 2][:], AF.Abs,
                                bias=dtb_sb[:, dtile:dtile + 1]),
                                (mm, mm0 + ch + 1))
                            aop(sc, lambda: sc.activation(
                                sp_t[:, 0:512], sp_t[:, 0:512], AF.Exp,
                                scale=-1.0))
                            aop(sc, lambda c0=c0: sc.activation(
                                dt_h[:, c0:c0 + 512], sp_t[:, 0:512], AF.Ln,
                                bias=idn["ones128"][:]))

                    @b.vector
                    def _(veg):
                        for ch in range(8):
                            c0 = ch * 512
                            vop(veg, lambda ch=ch, c0=c0: veg.tensor_scalar(
                                rl_t[:, 0:512], pdt[ch % 2][:],
                                dtb_sb[:, dtile:dtile + 1], 0.0,
                                ALU.add, ALU.max), (mm, mm0 + ch + 1))
                            vop(veg, lambda c0=c0: veg.tensor_tensor(
                                dt_h[:, c0:c0 + 512], dt_h[:, c0:c0 + 512],
                                rl_t[:, 0:512], ALU.add),
                                (ac, ac0 + 3 * (ch + 1)))

                mm0, ve0, ac0, ld0, st0, ay0 = (cmm.n, cve.n, cac.n, cld.n,
                                                cst.n, cay.n)
                lb0, lc0 = clb.n, clc.n
                with nc.Block() as b:
                    @b.vector
                    def _(veg):
                        veg.wait_ge(lx, 16 * clx.n)
                        vop(veg, lambda: veg.tensor_tensor(
                            dtx[:], dt_h[:], xn_h[:], ALU.mult))
                        for n in range(B_NST):
                            vop(veg, lambda n=n: veg.tensor_tensor(
                                uu[:], dtx[:], bb[:], ALU.mult),
                                (lb, 16 * (lb0 + n + 1)))
                            vop(veg, lambda: veg.tensor_tensor_scan(
                                hh[:], ab[:], uu[:], 0.0, ALU.mult, ALU.add),
                                (ac, ac0 + n + 1))
                            hc_waits = [(lc, 16 * (lc0 + n + 1)),
                                        (ay, 16 * (ay0 + 2 * max(0, n - 1)))]
                            if n == 0:
                                hc_waits.append((st, 16 * cst.n))
                            vop(veg, lambda n=n: veg.tensor_tensor(
                                hc[n % 2][:], hh[:], cb[:], ALU.mult),
                                *hc_waits)

                    @b.scalar
                    def _(sc):
                        for n in range(B_NST):
                            w = []
                            if n > 0:
                                w.append((ve, ve0 + 1 + 3 * (n - 1) + 2))
                            aop(sc, lambda n=n: sc.activation(
                                ab[:], dt_h[:], AF.Exp,
                                scale=a_sb[:, dtile, n:n + 1]), *w)

                    @b.gpsimd
                    def _(gp):
                        gp.wait_ge(gld, 16 * cgld.n)
                        def bcast(n):
                            if n >= B_NST:
                                return
                            if n > 0:
                                gp.wait_ge(ve, ve0 + 1 + 3 * (n - 1) + 1)
                            gp.dma_start(bb[:], bc_bcast_ap(128 + n, bb_i)
                                         ).then_inc(lb, 16)
                            clb.add()
                            if n > 0:
                                gp.wait_ge(ve, ve0 + 1 + 3 * (n - 1) + 3)
                            gp.dma_start(cb[:], bc_bcast_ap(144 + n, bb_i)
                                         ).then_inc(lc, 16)
                            clc.add()

                        bcast(0)
                        for n in range(B_NST):
                            bcast(n + 1)
                            gp.wait_ge(ve, ve0 + 1 + 3 * n + 3)
                            if n > 0:
                                gp.wait_ge(ay, 16 * (ay0 + 2 * n))
                            op = ALU.add if (n > 0 and B_ACCUM) else ALU.bypass
                            for half in range(2):
                                h0 = half * 2048
                                gp.dma_start(
                                    y_dram[drow0:drow0 + 128,
                                           t0 + h0:t0 + h0 + 2048],
                                    hc[n % 2][:, h0:h0 + 2048],
                                    accum_op=op).then_inc(ay, 16)
                                cay.add()

                ve0, ld0, st0 = cve.n, cld.n, cst.n
                with nc.Block() as b:
                    @b.sync
                    def _(sy):
                        sy.wait_ge(ay, 16 * cay.n)
                        sy.dma_start(yrd[:], y_dram[drow0:drow0 + 128,
                                                    t0:t0 + HB]
                                     ).then_inc(ly, 16); cly.add()

                    @b.vector
                    def _(veg):
                        vop(veg, lambda: veg.scalar_tensor_tensor(
                            hc[0][:], xn_h[:], dp_sb[:, dtile:dtile + 1],
                            yrd[:], ALU.mult, ALU.add),
                            (ly, 16 * cly.n))

                    @b.gpsimd
                    def _(gp):
                        gp.wait_ge(ve, cve.n)
                        if env is None:
                            gp.dma_start(attn0_out[drow0:drow0 + 128,
                                                   t0:t0 + HB],
                                         hc[0][:]).then_inc(st, 16)
                            cst.add()
                        else:
                            for q in range(4):
                                r0 = (bb_i * 4 + q) * 256 + dtile * 128
                                gp.dma_start(
                                    attn0_stage[r0:r0 + 128, :],
                                    hc[0][:, q * TC:(q + 1) * TC]
                                ).then_inc(st, 16)
                                cst.add()

        with nc.Block() as b:
            @b.sync
            def _(sy):
                sy.wait_ge(st, 16 * cst.n)

    return nc



GRP = 4
NGRP = (FF // 128) // GRP
C_NGRP = NGRP
C_NW = 4


def build_launch_C(env=None):
    if env is not None:
        nc = env["nc"]
        xa = env["xa"]
        attn0_t = env["attn0_t"]
        wg8 = env["wg8"]; wu8 = env["wu8"]; wd8 = env["wd8"]
        sg_l = env["sg_l"]; su_l = env["su_l"]; sd_l = env["sd_l"]
        wq = env["wq"]; wk = env["wk"]; wv = env["wv"]; wgt = env["wgt"]
        bq = env["bq"]; bk = env["bk"]; bv = env["bv"]; bg = env["bg"]
        x2_out = env["x2_out"]; q_out = env["q_out"]; k_out = env["k_out"]
        v_out = env["v_out"]; gate_out = env["gate_out"]
        stats_d = env["stats_d"]
        mksem = env["mksem"]
        return _launch_C_body(nc, env, mksem, xa, attn0_t, wg8, wu8, wd8,
                              sg_l, su_l, sd_l, wq, wk, wv, wgt, bq, bk, bv,
                              bg, x2_out, q_out, k_out, v_out, gate_out,
                              stats_d)
    nc = bass.Bass("TRN2", target_bir_lowering=False)
    xa = nc.dram_tensor("xa", [TC, D], F32, kind="ExternalInput").ap()
    attn0_t = nc.dram_tensor("attn0_t", [D, TC], F32, kind="ExternalInput").ap()
    wg8 = nc.dram_tensor("wg8", [D, FF], I8, kind="ExternalInput").ap()
    wu8 = nc.dram_tensor("wu8", [D, FF], I8, kind="ExternalInput").ap()
    wd8 = nc.dram_tensor("wd8", [FF, D], I8, kind="ExternalInput").ap()
    sg_l = nc.dram_tensor("sg_l", [128, 64], F32, kind="ExternalInput").ap()
    su_l = nc.dram_tensor("su_l", [128, 64], F32, kind="ExternalInput").ap()
    sd_l = nc.dram_tensor("sd_l", [128, 16], F32, kind="ExternalInput").ap()
    wq = nc.dram_tensor("wq", [D, D], F32, kind="ExternalInput").ap()
    wk = nc.dram_tensor("wk", [D, D], F32, kind="ExternalInput").ap()
    wv = nc.dram_tensor("wv", [D, D], F32, kind="ExternalInput").ap()
    wgt = nc.dram_tensor("wgt", [D, D], F32, kind="ExternalInput").ap()
    bq = nc.dram_tensor("bq", [128, 16], F32, kind="ExternalInput").ap()
    bk = nc.dram_tensor("bk", [128, 16], F32, kind="ExternalInput").ap()
    bv = nc.dram_tensor("bv", [128, 16], F32, kind="ExternalInput").ap()
    bg = nc.dram_tensor("bg", [128, 16], F32, kind="ExternalInput").ap()
    x2_out = nc.dram_tensor("x2_out", [D, TC], F32, kind="ExternalOutput").ap()
    q_out = nc.dram_tensor("q_out", [D, TC], F32, kind="ExternalOutput").ap()
    k_out = nc.dram_tensor("k_out", [D, TC], F32, kind="ExternalOutput").ap()
    v_out = nc.dram_tensor("v_out", [D, TC], F32, kind="ExternalOutput").ap()
    gate_out = nc.dram_tensor("gate_out", [D, TC], F32, kind="ExternalOutput").ap()
    stats_d = nc.dram_tensor("stats_d", [2, TC], F32).ap()
    return _launch_C_body(nc, None, None, xa, attn0_t, wg8, wu8, wd8, sg_l,
                          su_l, sd_l, wq, wk, wv, wgt, bq, bk, bv, bg,
                          x2_out, q_out, k_out, v_out, gate_out, stats_d)


def _launch_C_body(nc, env, mksem, xa, attn0_t, wg8, wu8, wd8, sg_l, su_l,
                   sd_l, wq, wk, wv, wgt, bq, bk, bv, bg, x2_out, q_out,
                   k_out, v_out, gate_out, stats_d):
    SCALING = float(HD) ** -0.5

    stack = ExitStack()
    with stack:
        sb = lambda n_, s, d: stack.enter_context(nc.sbuf_tensor("C" + n_, s, d))
        if mksem is None:
            sem = lambda n_: stack.enter_context(nc.semaphore(n_))
        else:
            sem = lambda n_: mksem("C" + n_)
        _psn = [0]

        def ps(s):
            _psn[0] += 1
            return stack.enter_context(nc.psum_tensor(f"Cps{_psn[0]}", s, F32))

        x1 = [sb(f"x1_{i}", [128, TC], F32) for i in range(NDT)]
        xnbf = [sb(f"xnbf{i}", [128, TC], BF16) for i in range(NDT)]
        sq = [sb("sq0", [128, TC], F32)]
        tscr = sb("tscr", [128, TC], F32)
        st_s = sb("st_s", [1, TC], F32)
        st_q = sb("st_q", [1, TC], F32)
        r_rstd = sb("r_rstd", [1, TC], F32)
        r_mrstd = sb("r_mrstd", [1, TC], F32)
        r_var = sb("r_var", [1, TC], F32)
        rstd_b = sb("rstd_b", [128, TC], F32)
        mrstd_b = sb("mrstd_b", [128, TC], F32)
        sg_sb = sb("sg_sb", [128, 64], F32)
        su_sb = sb("su_sb", [128, 64], F32)
        sd_sb = sb("sd_sb", [128, 16], F32)
        silu_t = sb("silu_t", [128, 512], BF16)
        up_t = sb("up_t", [128, 512], BF16)
        fuse = sb("fuse", [128, GRP, TC], BF16)
        bqs = sb("bqs", [128, 16], F32)
        bks = sb("bks", [128, 16], F32)
        bvs = sb("bvs", [128, 16], F32)
        bgs = sb("bgs", [128, 16], F32)
        ost = [sb(f"ost{i}", [128, 512], F32) for i in range(2)]
        idn = env["idn"] if env is not None else alloc_idents(nc, stack)

        ld = sem("ld"); mm = sem("mm"); ve = sem("ve"); po = sem("po")
        ac = sem("ac"); st = sem("st"); gld = sem("gld"); la = sem("la")
        lx2 = [sem("lxA"), sem("lxB")]
        la2 = [sem("laA"), sem("laB")]
        st2 = [sem("stA"), sem("stB")]
        cst2 = [Ctr(), Ctr()]
        cld = Ctr(); cmm = Ctr(); cve = Ctr(); cpo = Ctr()
        cac = Ctr(); cst = Ctr(); cgld = Ctr(); cla = Ctr(); clxa = Ctr()

        def vop(veg, fn, *waits):
            for (s, v) in waits:
                veg.wait_ge(s, v)
            veg.wait_ge(ve, cve.n)
            fn().then_inc(ve, 1)
            cve.add()

        def aop(sc, fn, *waits):
            for (s, v) in waits:
                sc.wait_ge(s, v)
            sc.wait_ge(ac, cac.n)
            fn().then_inc(ac, 1)
            cac.add()

        with nc.Block() as b:
            @b.sync
            def _(sy):
                sy.dma_start(sg_sb[:], sg_l[:]).then_inc(ld, 16); cld.add()
                sy.dma_start(su_sb[:], su_l[:]).then_inc(ld, 16); cld.add()
                sy.dma_start(sd_sb[:], sd_l[:]).then_inc(ld, 16); cld.add()
                sy.dma_start(bqs[:], bq[:]).then_inc(ld, 16); cld.add()
                sy.dma_start(bks[:], bk[:]).then_inc(ld, 16); cld.add()
                sy.dma_start(bvs[:], bv[:]).then_inc(ld, 16); cld.add()
                sy.dma_start(bgs[:], bg[:]).then_inc(ld, 16); cld.add()

            @b.gpsimd
            def _(gp):
                if env is None:
                    emit_idents(gp, idn, po, cpo)

        tmp_stack = ExitStack()
        xtm = [tmp_stack.enter_context(nc.sbuf_tensor(f"Cxtm{i}", [128, D], F32))
               for i in range(2)]
        a0t = [tmp_stack.enter_context(nc.sbuf_tensor(f"Ca0t{i}", [128, TC], F32))
               for i in range(2)]
        ptr = [tmp_stack.enter_context(nc.psum_tensor(f"Cptr{i}", [128, 128], F32))
               for i in range(4)]
        ld0, mm0, ve0, la0 = cld.n, cmm.n, cve.n, cla.n
        with nc.Block() as b:
            @b.sync
            def _(sy):
                for tt in range(8):
                    if tt >= 2:
                        sy.wait_ge(mm, mm0 + (tt - 1) * NDT)
                    sy.dma_start(xtm[tt % 2][:], xa[tt * 128:(tt + 1) * 128, :]
                                 ).then_inc(lx2[tt % 2], 16)

            @b.tensor
            def _(te):
                for tt in range(8):
                    te.wait_ge(lx2[tt % 2], 16 * (tt // 2 + 1))
                    for dt in range(NDT):
                        k = tt * NDT + dt
                        if k >= 4:
                            te.wait_ge(ve, ve0 + k - 3)
                        te.transpose(ptr[k % 4][:],
                                     xtm[tt % 2][:, dt * 128:(dt + 1) * 128],
                                     idn["ident128"][:]).then_inc(mm, 1)
                        cmm.add()

            @b.vector
            def _(veg):
                for tt in range(8):
                    for dt in range(NDT):
                        k = tt * NDT + dt
                        vop(veg, lambda dt=dt, tt=tt, k=k: veg.tensor_copy(
                            x1[dt][:, tt * 128:(tt + 1) * 128], ptr[k % 4][:]),
                            (mm, mm0 + k + 1))

        ve0, la0 = cve.n, cla.n
        with nc.Block() as b:
            @b.scalar
            def _(sc):
                for dt in range(NDT):
                    if dt >= 2:
                        sc.wait_ge(ve, ve0 + (dt - 1))
                    sc.dma_start(a0t[dt % 2][:],
                                 attn0_t[dt * 128:(dt + 1) * 128, :]
                                 ).then_inc(la2[dt % 2], 16)

            @b.vector
            def _(veg):
                for dt in range(NDT):
                    vop(veg, lambda dt=dt: veg.tensor_tensor(
                        x1[dt][:], x1[dt][:], a0t[dt % 2][:], ALU.add),
                        (la2[dt % 2], 16 * (dt // 2 + 1)))

        tmp_stack_pending = tmp_stack

        def emit_ln(src_tiles, dst_bf):
            pstat_stack = ExitStack()
            pstat = pstat_stack.enter_context(
                nc.psum_tensor(f"pstat{_psn[0]}", [1, TC], F32))
            _psn[0] += 1
            pstatq = pstat_stack.enter_context(
                nc.psum_tensor(f"pstatq{_psn[0]}", [1, TC], F32))
            _psn[0] += 1
            ac0, mm0, ve0, ld0 = cac.n, cmm.n, cve.n, cld.n
            with nc.Block() as b:
                @b.scalar
                def _(sc):
                    for dt in range(NDT):
                        if dt >= 1:
                            sc.wait_ge(mm, mm0 + dt)
                        aop(sc, lambda dt=dt: sc.activation(
                            sq[0][:], src_tiles[dt][:], AF.Square))

                @b.tensor
                def _(te):
                    for dt in range(NDT):
                        te.wait_ge(ac, ac0 + dt + 1)
                        for tchunk in range(2):
                            t0 = tchunk * 512
                            te.matmul(pstat[:, t0:t0 + 512], idn["ones128"][:],
                                      src_tiles[dt][:, t0:t0 + 512],
                                      start=(dt == 0), stop=(dt == NDT - 1))
                            last = te.matmul(pstatq[:, t0:t0 + 512],
                                             idn["ones128"][:],
                                             sq[0][:, t0:t0 + 512],
                                             start=(dt == 0),
                                             stop=(dt == NDT - 1))
                        last.then_inc(mm, 1); cmm.add()

            ve0, ac0 = cve.n, cac.n
            with nc.Block() as b:
                @b.vector
                def _(veg):
                    vop(veg, lambda: veg.tensor_scalar_mul(st_s[:], pstat[:],
                                                           1.0 / D))
                    vop(veg, lambda: veg.tensor_scalar_mul(st_q[:], pstatq[:],
                                                           1.0 / D))
                    vop(veg, lambda: veg.tensor_tensor(r_var[:], st_s[:],
                                                       st_s[:], ALU.mult))
                    vop(veg, lambda: veg.tensor_tensor(r_var[:], st_q[:],
                                                       r_var[:], ALU.subtract))

                @b.scalar
                def _(sc):
                    aop(sc, lambda: sc.activation(r_rstd[:], r_var[:], AF.Ln,
                                                  bias=idn["eps1"][:]),
                        (ve, ve0 + 4))
                    aop(sc, lambda: sc.activation(r_rstd[:], r_rstd[:], AF.Exp,
                                                  scale=-0.5))

            ve0, ld0 = cve.n, cld.n
            with nc.Block() as b:
                @b.vector
                def _(veg):
                    vop(veg, lambda: veg.tensor_tensor(r_mrstd[:], st_s[:],
                                                       r_rstd[:], ALU.mult))

                @b.sync
                def _(sy):
                    sy.wait_ge(ve, ve0 + 1)
                    sy.dma_start(stats_d[0:1, :], r_rstd[:]).then_inc(ld, 16)
                    cld.add()
                    sy.dma_start(stats_d[1:2, :], r_mrstd[:]).then_inc(ld, 16)
                    cld.add()
                    sy.wait_ge(ld, 16 * cld.n)
                    sy.dma_start(rstd_b[:], dram_bcast_ap(stats_d, 0, 0, TC)
                                 ).then_inc(ld, 16); cld.add()
                    sy.dma_start(mrstd_b[:], dram_bcast_ap(stats_d, 1, 0, TC)
                                 ).then_inc(ld, 16); cld.add()

            ve0 = cve.n
            with nc.Block() as b:
                @b.vector
                def _(veg):
                    veg.wait_ge(ld, 16 * cld.n)
                    for dt in range(NDT):
                        vop(veg, lambda dt=dt: veg.tensor_tensor(
                            tscr[:], src_tiles[dt][:], rstd_b[:], ALU.mult))
                        vop(veg, lambda dt=dt: veg.tensor_tensor(
                            dst_bf[dt][:], tscr[:], mrstd_b[:], ALU.subtract))
            pstat_stack.close()

        tmp_stack_pending.close()
        ffnw_stack = ExitStack()
        wgbf = ffnw_stack.enter_context(
            nc.sbuf_tensor("Cwgbf", [128, NDT, GRP * 128], BF16))
        wubf = ffnw_stack.enter_context(
            nc.sbuf_tensor("Cwubf", [128, NDT, GRP * 128], BF16))
        wdbf = ffnw_stack.enter_context(
            nc.sbuf_tensor("Cwdbf", [128, GRP, D // 2], BF16))
        emit_ln(x1, xnbf)

        ffn_psum_stack = ExitStack()
        def ps2(s):
            _psn[0] += 1
            return ffn_psum_stack.enter_context(
                nc.psum_tensor(f"Cps{_psn[0]}", s, F32))
        pga = [ps2([128, 512]) for i in range(2)]
        pgb = [ps2([128, 512]) for i in range(2)]
        pdn = [ps2([128, 512]) for i in range(2)]
        gfw = sem("gfw")
        gdw = sem("gdw")

        def wgwu_load(gp, g):
            f0_ = g * GRP * 128
            for (w8, dst) in ((wg8, wgbf), (wu8, wubf)):
                gp.dma_start(dst[:], bass.AP(
                    tensor=w8.tensor, offset=f0_,
                    ap=[[FF, 128], [FF * 128, NDT], [1, GRP * 128]])
                ).then_inc(gfw, 16)

        with nc.Block() as b:
            @b.gpsimd
            def _(gp):
                wgwu_load(gp, 0)

        mmGU = {}
        mmDN = {}
        veGU = {}
        veDN = {}
        acGU = {}
        mm0F, ve0F, ac0F = cmm.n, cve.n, cac.n

        for g in range(C_NGRP):
            base = mm0F + g * (GRP * 2 + 32)
            mmGU[g] = base + GRP * 2
            mmDN[(g, 0)] = mmGU[g] + 16
            mmDN[(g, 1)] = mmGU[g] + 32
            veGU[g] = ve0F + g * (GRP * 2 * 2 + 32) + GRP * 2 * 2
            veDN[g] = veGU[g] + 32
            acGU[g] = ac0F + (g + 1) * GRP * 2

        with nc.Block() as b:
            @b.gpsimd
            def _(gp):
                for g in range(C_NGRP):
                    f0 = g * GRP * 128
                    if g > 0:
                        gp.wait_ge(mm, mmDN[(g - 1, 1)])
                    gp.dma_start(wdbf[:], bass.AP(
                        tensor=wd8.tensor, offset=f0 * D,
                        ap=[[D, 128], [D * 128, GRP], [1, D // 2]])
                    ).then_inc(gdw, 16)
                    if g + 1 < C_NGRP:
                        gp.wait_ge(mm, mmGU[g])
                        wgwu_load(gp, g + 1)
                    gp.wait_ge(mm, mmDN[(g, 0)])
                    gp.dma_start(wdbf[:], bass.AP(
                        tensor=wd8.tensor, offset=f0 * D + D // 2,
                        ap=[[D, 128], [D * 128, GRP], [1, D // 2]])
                    ).then_inc(gdw, 16)

            @b.tensor
            def _(te):
                for g in range(C_NGRP):
                    te.wait_ge(gfw, 16 * 2 * (g + 1))
                    for fr in range(GRP):
                        for tchunk in range(2):
                            t0 = tchunk * 512
                            k = fr * 2 + tchunk
                            if k >= 2:
                                te.wait_ge(ac, ac0F + g * GRP * 2 + k - 1)
                                te.wait_ge(ve, ve0F + g * (GRP * 4 + 32)
                                           + 2 * (k - 2) + 1)
                            elif g > 0:
                                te.wait_ge(ac, acGU[g - 1])
                                te.wait_ge(ve, veGU[g - 1])
                            for dt in range(NDT):
                                te.matmul(pga[tchunk][:],
                                          wgbf[:, dt, fr * 128:(fr + 1) * 128],
                                          xnbf[dt][:, t0:t0 + 512],
                                          start=(dt == 0), stop=(dt == NDT - 1))
                                last = te.matmul(
                                    pgb[tchunk][:],
                                    wubf[:, dt, fr * 128:(fr + 1) * 128],
                                    xnbf[dt][:, t0:t0 + 512],
                                    start=(dt == 0), stop=(dt == NDT - 1))
                            last.then_inc(mm, 1); cmm.add()
                    for dh in range(2):
                        te.wait_ge(gdw, 16 * (2 * g + dh + 1))
                        te.wait_ge(ve, veGU[g] + dh * 16)
                        for odt in range(NDT // 2):
                            for tchunk in range(2):
                                t0 = tchunk * 512
                                k = odt * 2 + tchunk
                                if k >= 2:
                                    te.wait_ge(ve, veGU[g] + dh * 16 + k - 1)
                                for fr in range(GRP):
                                    last = te.matmul(
                                        pdn[k % 2][:],
                                        wdbf[:, fr, odt * 128:(odt + 1) * 128],
                                        fuse[:, fr, t0:t0 + 512],
                                        start=(fr == 0), stop=(fr == GRP - 1))
                                last.then_inc(mm, 1); cmm.add()

            @b.scalar
            def _(sc):
                for g in range(C_NGRP):
                    for fr in range(GRP):
                        gf = g * GRP + fr
                        for tchunk in range(2):
                            k = fr * 2 + tchunk
                            aop(sc, lambda gf=gf, tchunk=tchunk:
                                sc.activation(silu_t[:], pga[tchunk][:],
                                              AF.Silu,
                                              scale=sg_sb[:, gf:gf + 1]),
                                (mm, mm0F + g * (GRP * 2 + 32) + k + 1),
                                (ve, ve0F + g * (GRP * 4 + 32) + 2 * k))

            @b.vector
            def _(veg):
                for g in range(C_NGRP):
                    mgu = mm0F + g * (GRP * 2 + 32)
                    for fr in range(GRP):
                        gf = g * GRP + fr
                        for tchunk in range(2):
                            t0 = tchunk * 512
                            k = fr * 2 + tchunk
                            waits = [(mm, mgu + k + 1)]
                            if k == 0 and g > 0:
                                waits.append((mm, mmDN[(g - 1, 1)]))
                            vop(veg, lambda gf=gf, tchunk=tchunk:
                                veg.tensor_scalar(up_t[:], pgb[tchunk][:],
                                                  su_sb[:, gf:gf + 1], None,
                                                  ALU.mult), *waits)
                            vop(veg, lambda fr=fr, t0=t0: veg.tensor_tensor(
                                fuse[:, fr, t0:t0 + 512], silu_t[:], up_t[:],
                                ALU.mult),
                                (ac, ac0F + g * GRP * 2 + k + 1))
                    for dh in range(2):
                        for odt in range(NDT // 2):
                            godt = dh * (NDT // 2) + odt
                            for tchunk in range(2):
                                t0 = tchunk * 512
                                k = odt * 2 + tchunk
                                vop(veg, lambda godt=godt, t0=t0, k=k:
                                    veg.scalar_tensor_tensor(
                                        x1[godt][:, t0:t0 + 512], pdn[k % 2][:],
                                        sd_sb[:, godt:godt + 1],
                                        x1[godt][:, t0:t0 + 512],
                                        ALU.mult, ALU.add),
                                    (mm, mmGU[g] + dh * 16 + k + 1))

        ffn_psum_stack.close()
        ffnw_stack.close()
        emit_ln(x1, xnbf)

        pqk = [ps([128, 512]) for i in range(2)]
        wbig2 = [sb(f"wbig{i}", [128, NDT, 512], BF16) for i in range(2)]
        qw2 = [sem("qwA"), sem("qwB")]
        wseq = [(wi_, h_) for wi_ in range(C_NW) for h_ in range(4)]
        wsrcs = [wq, wk, wv, wgt]

        def wbig_load(gp, s):
            wi_, h_ = wseq[s]
            gp.dma_start(wbig2[s % 2][:], bass.AP(
                tensor=wsrcs[wi_].tensor, offset=h_ * 512,
                ap=[[D, 128], [D * 128, NDT], [1, 512]])
            ).then_inc(qw2[s % 2], 16)

        if wseq:
            with nc.Block() as b:
                @b.gpsimd
                def _(gp):
                    wbig_load(gp, 0)

        for wi, (wsrc, bias_sb, dst, mode) in enumerate((
                (wq, bqs, q_out, "q"), (wk, bks, k_out, "k"),
                (wv, bvs, v_out, "k"), (wgt, bgs, gate_out, "g"))[:C_NW]):
            for half in range(4):
                c0 = half * 512
                s = wi * 4 + half
                wbig = wbig2[s % 2]
                gld0, mm0, ve0, ac0 = cgld.n, cmm.n, cve.n, cac.n
                sb0 = [cst2[0].n, cst2[1].n]
                with nc.Block() as b:
                    @b.gpsimd
                    def _(gp):
                        if s + 1 < len(wseq):
                            wbig_load(gp, s + 1)

                    @b.tensor
                    def _(te):
                        te.wait_ge(qw2[s % 2], 16 * (s // 2 + 1))
                        for ft in range(4):
                            for tchunk in range(2):
                                t0 = tchunk * 512
                                k = ft * 2 + tchunk
                                if k >= 2:
                                    if mode == "g":
                                        te.wait_ge(ac, ac0 + k - 1)
                                    else:
                                        te.wait_ge(ve, ve0 + k - 1)
                                for dt in range(NDT):
                                    last = te.matmul(
                                        pqk[k % 2][:],
                                        wbig[:, dt, ft * 128:(ft + 1) * 128],
                                        xnbf[dt][:, t0:t0 + 512],
                                        start=(dt == 0), stop=(dt == NDT - 1))
                                last.then_inc(mm, 1); cmm.add()

                    if mode != "g":
                        @b.vector
                        def _(veg):
                            for ft in range(4):
                                fcol = (c0 + ft * 128) // 128
                                for tchunk in range(2):
                                    t0 = tchunk * 512
                                    k = ft * 2 + tchunk
                                    if mode == "q":
                                        vop(veg, lambda fcol=fcol, k=k:
                                            veg.tensor_scalar(
                                                ost[k % 2][:], pqk[k % 2][:],
                                                bias_sb[:, fcol:fcol + 1],
                                                SCALING, ALU.add, ALU.mult),
                                            (mm, mm0 + k + 1),
                                            (st2[k % 2],
                                             16 * (sb0[k % 2] + k // 2)))
                                    else:
                                        vop(veg, lambda fcol=fcol, k=k:
                                            veg.tensor_scalar(
                                                ost[k % 2][:], pqk[k % 2][:],
                                                bias_sb[:, fcol:fcol + 1],
                                                None, ALU.add),
                                            (mm, mm0 + k + 1),
                                            (st2[k % 2],
                                             16 * (sb0[k % 2] + k // 2)))
                    else:
                        @b.scalar
                        def _(sc):
                            for ft in range(4):
                                fcol = (c0 + ft * 128) // 128
                                for tchunk in range(2):
                                    k = ft * 2 + tchunk
                                    aop(sc, lambda fcol=fcol, k=k:
                                        sc.activation(
                                            ost[k % 2][:], pqk[k % 2][:],
                                            AF.Silu,
                                            bias=bias_sb[:, fcol:fcol + 1]),
                                        (mm, mm0 + k + 1),
                                        (st2[k % 2],
                                         16 * (sb0[k % 2] + k // 2)))

                    @b.sync
                    def _(sy):
                        for ft in range(4):
                            for tchunk in range(2):
                                t0 = tchunk * 512
                                k = ft * 2 + tchunk
                                if mode == "g":
                                    sy.wait_ge(ac, ac0 + k + 1)
                                else:
                                    sy.wait_ge(ve, ve0 + k + 1)
                                sy.dma_start(
                                    dst[c0 + ft * 128:c0 + (ft + 1) * 128,
                                        t0:t0 + 512],
                                    ost[k % 2][:]).then_inc(st2[k % 2], 16)
                                cst2[k % 2].add()

        st0 = cst.n
        with nc.Block() as b:
            @b.sync
            def _(sy):
                for dt in range(NDT):
                    sy.dma_start(x2_out[dt * 128:(dt + 1) * 128, :],
                                 x1[dt][:]).then_inc(st, 16)
                    cst.add()
                sy.wait_ge(st, 16 * cst.n)
                sy.wait_ge(st2[0], 16 * cst2[0].n)
                sy.wait_ge(st2[1], 16 * cst2[1].n)

    return nc



DW = 4992


def build_launch_D(env=None):
    if env is None:
        nc = bass.Bass("TRN2", target_bir_lowering=False)
        qT = nc.dram_tensor("qT", [4, 128, L], F32, kind="ExternalInput").ap()
        kT = nc.dram_tensor("kT", [4, 128, L], F32, kind="ExternalInput").ap()
        vtm = nc.dram_tensor("vtm", [4, L, HD], F32, kind="ExternalInput").ap()
        decay_d = nc.dram_tensor("decay_d", [1, 2], F32,
                                 kind="ExternalInput").ap()
        gnw_d = nc.dram_tensor("gnw_d", [128, 2], F32, kind="ExternalInput").ap()
        gnb_d = nc.dram_tensor("gnb_d", [128, 2], F32, kind="ExternalInput").ap()
        g_out = nc.dram_tensor("g_out", [4, 128, L], F32,
                               kind="ExternalOutput").ap()
        scr = nc.dram_tensor("scr", [4, L], F32).ap()
        mksem = None
        q_a2a = k_a2a = v_a2a = g_stage = None
    else:
        nc = env["nc"]
        decay_d = env["decay_d"]; gnw_d = env["gnw_d"]; gnb_d = env["gnb_d"]
        q_a2a = env["q_a2a"]; k_a2a = env["k_a2a"]; v_a2a = env["v_a2a"]
        g_stage = env["g_stage"]
        scr = env["scr"]
        mksem = env["mksem"]
        qT = kT = vtm = g_out = None

    def pair_ap(src_a2a, pair):
        b, hloc = pair % 2, pair // 2
        return bass.AP(tensor=src_a2a.tensor,
                       offset=(b * 4 * 256 + hloc * 128) * TC,
                       ap=[[TC, 128], [256 * TC, 4], [1, TC]])

    stack = ExitStack()
    with stack:
        sb = lambda n_, s, d: stack.enter_context(nc.sbuf_tensor("D" + n_, s, d))
        if mksem is None:
            sem = lambda n_: stack.enter_context(nc.semaphore(n_))
        else:
            sem = lambda n_: mksem("D" + n_)
        _psn = [0]

        def ps(shape):
            _psn[0] += 1
            return stack.enter_context(nc.psum_tensor(f"Dps{_psn[0]}", shape, F32))

        dfull = [sb(f"dfull{i}", [128, DW], F32) for i in range(2)]
        iot = sb("iot", [128, DW], F32)
        qp = sb("qp", [128, L], F32)
        kp = sb("kp", [128, L], F32)
        qs_bf = sb("qs_bf", [128, L], BF16)
        kbf = sb("kbf", [128, L], BF16)
        vt_bf = sb("vt_bf", [128, 32, HD], BF16)
        ww = sb("ww", [128, L], F32)
        dect = sb("dect", [128, L], F32)
        dcol = sb("dcol", [128, 1], F32)
        ed = sb("ed", [128, 1], F32)
        rsum = sb("rsum", [1, L], F32)
        rb = sb("rb", [128, L], F32)
        vfm = rb
        retbf = [sb(f"retbf{i}", [128, 512], BF16) for i in range(2)]
        at_f = sb("at_f", [128, 512], F32)
        sq2 = sb("sq2", [128, 512], F32)
        s_m = sb("s_m", [1, 512], F32)
        s_q = sb("s_q", [1, 512], F32)
        s_v = sb("s_v", [1, 512], F32)
        mrb = sb("mrb", [128, 512], F32)
        rrb = sb("rrb", [128, 512], F32)
        gt = [sb(f"gt{i}", [128, 512], F32) for i in range(2)]
        gnw_sb = sb("gnw_sb", [128, 2], F32)
        gnb_sb = sb("gnb_sb", [128, 2], F32)
        idn = env["idn"] if env is not None else alloc_idents(nc, stack)

        psA = [ps([128, 512]) for _ in range(2)]
        psB = [ps([128, 512]) for _ in range(2)]
        psC = ps([1, 512])
        psD = ps([1, 512])
        pbc1 = ps([128, 512])
        pbc2 = ps([128, 512])
        acT = {}
        veT = {}

        ld = sem("ld"); mm = sem("mm"); ve = sem("ve"); po = sem("po")
        ac = sem("ac"); gld = sem("gld"); ly = sem("ly")
        st2 = [sem("stA"), sem("stB")]
        cst2 = [Ctr(), Ctr()]
        cld = Ctr(); cmm = Ctr(); cve = Ctr(); cpo = Ctr()
        cac = Ctr(); cgld = Ctr(); cly = Ctr()

        def vop(veg, fn, *waits):
            for (s, v) in waits:
                veg.wait_ge(s, v)
            veg.wait_ge(ve, cve.n)
            fn().then_inc(ve, 1)
            cve.add()

        def aop(sc, fn, *waits):
            for (s, v) in waits:
                sc.wait_ge(s, v)
            sc.wait_ge(ac, cac.n)
            fn().then_inc(ac, 1)
            cac.add()

        with nc.Block() as b:
            @b.gpsimd
            def _(gp):
                if env is None:
                    emit_idents(gp, idn, po, cpo)
                gp.iota(iot[:], pattern=[[1, DW]], base=-384,
                        channel_multiplier=-1,
                        allow_small_or_imprecise_dtypes=True
                        ).then_inc(po, 1); cpo.add()

            @b.sync
            def _(sy):
                sy.dma_start(gnw_sb[:], gnw_d[:]).then_inc(ld, 16); cld.add()
                sy.dma_start(gnb_sb[:], gnb_d[:]).then_inc(ld, 16); cld.add()

        for pair in range(4):
            hloc = pair // 2
            newhead = (pair % 2 == 0)

            ld0, ve0, ac0, gld0 = cld.n, cve.n, cac.n, cgld.n
            with nc.Block() as b:
                @b.sync
                def _(sy):
                    if env is None:
                        sy.dma_start(qp[:], qT[pair]).then_inc(ld, 16)
                        cld.add()
                        sy.dma_start(kp[:], kT[pair]).then_inc(ld, 16)
                        cld.add()
                    else:
                        sy.dma_start(qp[:], pair_ap(q_a2a, pair)
                                     ).then_inc(ld, 16); cld.add()
                        sy.dma_start(kp[:], pair_ap(k_a2a, pair)
                                     ).then_inc(ld, 16); cld.add()
                        sy.dma_start(vfm[:], pair_ap(v_a2a, pair)
                                     ).then_inc(ld, 16); cld.add()
                    if newhead:
                        sy.dma_start(dcol[:], dram_bcast_ap(decay_d, 0, hloc, 1)
                                     ).then_inc(ld, 16); cld.add()

                @b.gpsimd
                def _(gp):
                    if env is None:
                        src = bass.AP(tensor=vtm.tensor, offset=pair * L * HD,
                                      ap=[[HD, 128], [HD * 128, 32], [1, HD]])
                        gp.dma_start(vt_bf[:], src).then_inc(gld, 16)
                        cgld.add()
                        src2 = bass.AP(tensor=kT.tensor, offset=pair * 128 * L,
                                       ap=[[L, 128], [1, L]])
                        gp.dma_start(kbf[:], src2).then_inc(gld, 16)
                        cgld.add()
                    else:
                        gp.dma_start(kbf[:], pair_ap(k_a2a, pair)
                                     ).then_inc(gld, 16); cgld.add()

                @b.vector
                def _(veg):
                    if newhead:
                        vop(veg, lambda: veg.tensor_scalar(
                            dfull[hloc][:], iot[:], dcol[:], None, ALU.mult),
                            (ld, 16 * cld.n))
                    vop(veg, lambda: veg.tensor_tensor_scan(
                        ww[:], dect[:], kp[:], 0.0, ALU.mult, ALU.add),
                        (ac, ac0 + 2))
                    vop(veg, lambda: veg.tensor_tensor(
                        ww[:], qp[:], ww[:], ALU.mult))

                @b.scalar
                def _(sc):
                    if newhead:
                        aop(sc, lambda: sc.activation(ed[:], dcol[:], AF.Exp),
                            (ld, 16 * cld.n))
                    else:
                        aop(sc, lambda: sc.activation(ed[:], dcol[:], AF.Exp))
                    aop(sc, lambda: sc.activation(dect[:], qp[:], AF.Abs,
                                                  bias=ed[:], scale=0.0),
                        (ld, 16 * cld.n))

            if env is not None:
                mmv0 = cmm.n
                vev0 = cve.n
                with nc.Block() as b:
                    @b.tensor
                    def _(te):
                        te.wait_ge(ld, 16 * cld.n)
                        for sj in range(32):
                            if sj >= 2:
                                te.wait_ge(ve, vev0 + sj - 1)
                            te.transpose(
                                psA[sj % 2][0:128, 0:128],
                                vfm[:, sj * 128:(sj + 1) * 128],
                                idn["ident128"][:]).then_inc(mm, 1)
                            cmm.add()

                    @b.vector
                    def _(veg):
                        for sj in range(32):
                            vop(veg, lambda sj=sj: veg.tensor_copy(
                                vt_bf[:, sj, :], psA[sj % 2][0:128, 0:128]),
                                (mm, mmv0 + sj + 1))

            if newhead:
                ve0, po0 = cve.n, cpo.n
                with nc.Block() as b:
                    @b.gpsimd
                    def _(gp):
                        gp.affine_select(
                            dfull[hloc][:], dfull[hloc][:],
                            pattern=[[1, DW]], compare_op=ALU.is_ge,
                            fill=-1e38, base=-384, channel_multiplier=-1
                        ).then_inc(po, 1); cpo.add()

                    @b.scalar
                    def _(sc):
                        aop(sc, lambda: sc.activation(
                            dfull[hloc][:], dfull[hloc][:], AF.Exp),
                            (po, po0 + 1))

            mm0, ve0, ac0, ly0 = cmm.n, cve.n, cac.n, cly.n
            with nc.Block() as b:
                @b.tensor
                def _(te):
                    for ch in range(8):
                        if ch >= 2:
                            te.wait_ge(ve, ve0 + ch - 1)
                        te.matmul(psC[:] if ch % 2 == 0 else psD[:],
                                  idn["ones128"][:],
                                  ww[:, ch * 512:(ch + 1) * 512],
                                  start=True, stop=True).then_inc(mm, 1)
                        cmm.add()

                @b.vector
                def _(veg):
                    for ch in range(8):
                        vop(veg, lambda ch=ch: veg.tensor_copy(
                            rsum[:, ch * 512:(ch + 1) * 512],
                            (psC[:] if ch % 2 == 0 else psD[:])),
                            (mm, mm0 + ch + 1))
                    vop(veg, lambda: veg.tensor_scalar(
                        rb[0:1, :], rsum[:], -1.0, None, ALU.mult))
                    vop(veg, lambda: veg.tensor_tensor(
                        rsum[:], rsum[:], rb[0:1, :], ALU.max))
                    vop(veg, lambda: veg.tensor_scalar(
                        rsum[:], rsum[:], 1.0, None, ALU.max))
                    vop(veg, lambda: veg.reciprocal(rsum[:], rsum[:]))

                @b.sync
                def _(sy):
                    sy.wait_ge(ve, ve0 + 12)
                    sy.dma_start(scr[0:1, :], rsum[:]).then_inc(ly, 16)
                    cly.add()
                    sy.wait_ge(ly, 16 * cly.n)
                    sy.dma_start(rb[:], dram_bcast_ap(scr, 0, 0, L)
                                 ).then_inc(ly, 16); cly.add()

            ve0, ly0 = cve.n, cly.n
            with nc.Block() as b:
                @b.vector
                def _(veg):
                    vop(veg, lambda: veg.tensor_tensor(
                        qs_bf[:], qp[:], rb[:], ALU.mult), (ly, 16 * cly.n))

            for tich in range(8):
                t0 = tich * 512
                nsj = 4 * (tich + 1)
                mm0, ve0, ac0 = cmm.n, cve.n, cac.n
                sb0 = [cst2[0].n, cst2[1].n]
                acT[(pair, tich)] = ac0
                veT[(pair, tich)] = ve0
                with nc.Block() as b:
                    @b.tensor
                    def _(te):
                        te.wait_ge(gld, 16 * cgld.n)
                        te.wait_ge(ve, ve0)
                        te.wait_ge(ac, ac0)
                        for sjb in range(nsj):
                            if sjb >= 2:
                                te.wait_ge(ve, ve0 + sjb - 1)
                            te.matmul(psA[sjb % 2][:],
                                      kbf[:, sjb * 128:(sjb + 1) * 128],
                                      qs_bf[:, t0:t0 + 512],
                                      start=True, stop=True).then_inc(mm, 1)
                            cmm.add()
                            te.wait_ge(ve, ve0 + sjb + 1)
                            te.matmul(psB[tich % 2][:],
                                      vt_bf[:, sjb, :], retbf[sjb % 2][:],
                                      start=(sjb == 0), stop=(sjb == nsj - 1)
                                      ).then_inc(mm, 1)
                            cmm.add()
                        te.wait_ge(ac, ac0 + 2)
                        te.matmul(psC[:], idn["ones128"][:], at_f[:],
                                  start=True, stop=True).then_inc(mm, 1)
                        cmm.add()
                        te.matmul(psD[:], idn["ones128"][:], sq2[:],
                                  start=True, stop=True).then_inc(mm, 1)
                        cmm.add()
                        te.wait_ge(ac, ac0 + 4)
                        te.matmul(pbc1[:], idn["ones1r"][:], s_v[:],
                                  start=True, stop=True).then_inc(mm, 1)
                        cmm.add()
                        te.wait_ge(ve, ve0 + nsj + 5)
                        te.matmul(pbc2[:], idn["ones1r"][:], s_m[:],
                                  start=True, stop=True).then_inc(mm, 1)
                        cmm.add()

                    @b.scalar
                    def _(sc):
                        w = [(mm, mm0 + 2 * nsj)]
                        if tich >= 1:
                            pnsj = 4 * tich
                            w.append((ve, veT[(pair, tich - 1)] + pnsj + 8))
                        aop(sc, lambda: sc.activation(at_f[:],
                                                      psB[tich % 2][:],
                                                      AF.Copy), *w)
                        aop(sc, lambda: sc.activation(sq2[:], at_f[:],
                                                      AF.Square))
                        aop(sc, lambda: sc.activation(s_v[:], s_q[:], AF.Ln,
                                                      bias=idn["eps1"][:]),
                            (ve, ve0 + nsj + 4))
                        aop(sc, lambda: sc.activation(s_v[:], s_v[:], AF.Exp,
                                                      scale=-0.5))

                    @b.vector
                    def _(veg):
                        for sjb in range(nsj):
                            doff = 384 + t0 - sjb * 128
                            vop(veg, lambda sjb=sjb, doff=doff:
                                veg.tensor_tensor(
                                    retbf[sjb % 2][:], psA[sjb % 2][:],
                                    dfull[hloc][:, doff:doff + 512], ALU.mult),
                                (mm, mm0 + 2 * sjb + 1))
                        vop(veg, lambda: veg.tensor_scalar_mul(
                            s_m[:], psC[:], 1.0 / HD),
                            (mm, mm0 + 2 * nsj + 1))
                        vop(veg, lambda: veg.tensor_scalar_mul(
                            s_q[:], psD[:], 1.0 / HD),
                            (mm, mm0 + 2 * nsj + 2))
                        vop(veg, lambda: veg.tensor_tensor(
                            rb[0:1, 0:512], s_m[:], s_m[:], ALU.mult))
                        vop(veg, lambda: veg.tensor_tensor(
                            s_q[:], s_q[:], rb[0:1, 0:512], ALU.subtract))
                        vop(veg, lambda: veg.tensor_tensor(
                            s_m[:], s_m[:], s_v[:], ALU.mult), (ac, ac0 + 4))
                        kbuf = tich % 2
                        vop(veg, lambda kbuf=kbuf: veg.tensor_tensor(
                            gt[kbuf][:], at_f[:], pbc1[:], ALU.mult),
                            (mm, mm0 + 2 * nsj + 3),
                            (st2[kbuf], 16 * cst2[kbuf].n))
                        vop(veg, lambda kbuf=kbuf: veg.tensor_tensor(
                            gt[kbuf][:], gt[kbuf][:], pbc2[:], ALU.subtract),
                            (mm, mm0 + 2 * nsj + 4))
                        vop(veg, lambda kbuf=kbuf: veg.tensor_scalar(
                            gt[kbuf][:], gt[kbuf][:],
                            gnw_sb[:, hloc:hloc + 1],
                            gnb_sb[:, hloc:hloc + 1],
                            ALU.mult, ALU.add))

                    @b.sync
                    def _(sy):
                        kbuf = tich % 2
                        sy.wait_ge(ve, cve.n)
                        if env is None:
                            tgt = bass.AP(tensor=g_out.tensor,
                                          offset=pair * 128 * L + t0,
                                          ap=[[L, 128], [1, 512]])
                        else:
                            b_, hloc_ = pair % 2, pair // 2
                            r0 = (b_ * 4 + tich // 2) * 256 + hloc_ * 128
                            tgt = bass.AP(tensor=g_stage.tensor,
                                          offset=r0 * TC + (tich % 2) * 512,
                                          ap=[[TC, 128], [1, 512]])
                        sy.dma_start(tgt, gt[kbuf][:]).then_inc(st2[kbuf], 16)
                        cst2[kbuf].add()

        with nc.Block() as b:
            @b.sync
            def _(sy):
                sy.wait_ge(st2[0], 16 * cst2[0].n)
                sy.wait_ge(st2[1], 16 * cst2[1].n)

    return nc



def build_launch_E(env=None):
    if env is None:
        nc = bass.Bass("TRN2", target_bir_lowering=False)
        x2_t = nc.dram_tensor("x2_t", [D, TC], F32, kind="ExternalInput").ap()
        g_t = nc.dram_tensor("g_t", [D, TC], F32, kind="ExternalInput").ap()
        gate_t = nc.dram_tensor("gate_t", [D, TC], F32,
                                kind="ExternalInput").ap()
        wo = nc.dram_tensor("wo", [D, D], F32, kind="ExternalInput").ap()
        bo_l = nc.dram_tensor("bo_l", [128, 16], F32, kind="ExternalInput").ap()
        rw = nc.dram_tensor("rw", [NEXP, D], F32, kind="ExternalInput").ap()
        gup = nc.dram_tensor("gup", [NEXP, D, 2 * FI], F32,
                             kind="ExternalInput").ap()
        dwn = nc.dram_tensor("dwn", [NEXP, FI, D], F32,
                             kind="ExternalInput").ap()
        out_t = nc.dram_tensor("out_t", [TC, D], F32, kind="ExternalOutput").ap()
        stats_d = nc.dram_tensor("stats_d", [2, TC], F32).ap()
        comb_d = nc.dram_tensor("comb_d", [NEXP, TC], F32).ap()
        mksem = None
    else:
        nc = env["nc"]
        x2_t = env["x2_t"]; g_t = env["g_t"]; gate_t = env["gate_t"]
        wo = env["wo"]; bo_l = env["bo_l"]; rw = env["rw"]
        gup = env["gup"]; dwn = env["dwn"]; out_t = env["out_t"]
        stats_d = env["stats_d"]; comb_d = env["comb_d"]
        mksem = env["mksem"]

    def g_row(h):
        if env is None:
            return h * 128
        return (h // 2) * 256 + (h % 2) * 128

    stack = ExitStack()
    with stack:
        sb = lambda n_, s, d: stack.enter_context(nc.sbuf_tensor("E" + n_, s, d))
        if mksem is None:
            sem = lambda n_: stack.enter_context(nc.semaphore(n_))
        else:
            sem = lambda n_: mksem("E" + n_)
        _psn = [0]

        def ps_on(st_, shape):
            _psn[0] += 1
            return st_.enter_context(nc.psum_tensor(f"Eps{_psn[0]}", shape, F32))

        x3 = [sb(f"x3_{i}", [128, TC], F32) for i in range(NDT)]
        xnbf = [sb(f"xnbf{i}", [128, TC], BF16) for i in range(NDT)]
        bos = sb("bos", [128, 16], F32)
        idn = env["idn"] if env is not None else alloc_idents(nc, stack)

        ld = sem("ld"); mm = sem("mm"); ve = sem("ve"); po = sem("po")
        ac = sem("ac"); gld = sem("gld"); ly = sem("ly")
        lg2 = [sem("lgA"), sem("lgB")]
        lt2 = [sem("ltA"), sem("ltB")]
        st2 = [sem("stA"), sem("stB")]
        cst2 = [Ctr(), Ctr()]
        cld = Ctr(); cmm = Ctr(); cve = Ctr(); cpo = Ctr()
        cac = Ctr(); cgld = Ctr(); cly = Ctr()

        def vop(veg, fn, *waits):
            for (s, v) in waits:
                veg.wait_ge(s, v)
            veg.wait_ge(ve, cve.n)
            fn().then_inc(ve, 1)
            cve.add()

        def aop(sc, fn, *waits):
            for (s, v) in waits:
                sc.wait_ge(s, v)
            sc.wait_ge(ac, cac.n)
            fn().then_inc(ac, 1)
            cac.add()

        gg_stack = ExitStack()
        gated = [gg_stack.enter_context(
            nc.sbuf_tensor(f"Egated{i}", [128, TC], BF16)) for i in range(NDT)]
        gg = [gg_stack.enter_context(nc.sbuf_tensor(f"Egg{i}", [128, TC], F32))
              for i in range(2)]
        gt2 = [gg_stack.enter_context(nc.sbuf_tensor(f"Egt2{i}", [128, TC], F32))
               for i in range(2)]
        with nc.Block() as b:
            @b.sync
            def _(sy):
                sy.dma_start(bos[:], bo_l[:]).then_inc(ld, 16); cld.add()

            @b.gpsimd
            def _(gp):
                if env is None:
                    emit_idents(gp, idn, po, cpo)

        ve0 = cve.n
        with nc.Block() as b:
            @b.sync
            def _(sy):
                for dt in range(NDT):
                    if dt >= 2:
                        sy.wait_ge(ve, ve0 + dt - 1)
                    sy.dma_start(gg[dt % 2][:],
                                 g_t[g_row(dt):g_row(dt) + 128, :]
                                 ).then_inc(lg2[dt % 2], 16)

            @b.scalar
            def _(sc):
                for dt in range(NDT):
                    if dt >= 2:
                        sc.wait_ge(ve, ve0 + dt - 1)
                    sc.dma_start(gt2[dt % 2][:],
                                 gate_t[dt * 128:(dt + 1) * 128, :]
                                 ).then_inc(lt2[dt % 2], 16)

            @b.gpsimd
            def _(gp):
                for dt in range(NDT):
                    gp.dma_start(x3[dt][:], x2_t[dt * 128:(dt + 1) * 128, :]
                                 ).then_inc(gld, 16); cgld.add()

            @b.vector
            def _(veg):
                for dt in range(NDT):
                    vop(veg, lambda dt=dt: veg.tensor_tensor(
                        gated[dt][:], gt2[dt % 2][:], gg[dt % 2][:], ALU.mult),
                        (lg2[dt % 2], 16 * (dt // 2 + 1)),
                        (lt2[dt % 2], 16 * (dt // 2 + 1)))

        wo_stack = ExitStack()
        wbig2 = [wo_stack.enter_context(
            nc.sbuf_tensor(f"Ewbig{i}", [128, NDT, 512], BF16))
            for i in range(2)]
        pqk = [ps_on(wo_stack, [128, 512]) for _ in range(2)]
        wq2 = [sem("EwqA"), sem("EwqB")]

        def wo_load(gp, q_):
            gp.dma_start(wbig2[q_ % 2][:], bass.AP(
                tensor=wo.tensor, offset=q_ * 512,
                ap=[[D, 128], [D * 128, NDT], [1, 512]])
            ).then_inc(wq2[q_ % 2], 16)

        with nc.Block() as b:
            @b.gpsimd
            def _(gp):
                wo_load(gp, 0)

        for quarter in range(4):
            c0 = quarter * 512
            wbig = wbig2[quarter % 2]
            gld0, mm0, ve0 = cgld.n, cmm.n, cve.n
            with nc.Block() as b:
                @b.gpsimd
                def _(gp):
                    if quarter + 1 < 4:
                        wo_load(gp, quarter + 1)

                @b.tensor
                def _(te):
                    te.wait_ge(wq2[quarter % 2], 16 * (quarter // 2 + 1))
                    for ft in range(4):
                        for tchunk in range(2):
                            t0 = tchunk * 512
                            k = ft * 2 + tchunk
                            if k >= 2:
                                te.wait_ge(ve, ve0 + k - 1)
                            for dt in range(NDT):
                                last = te.matmul(
                                    pqk[k % 2][:],
                                    wbig[:, dt, ft * 128:(ft + 1) * 128],
                                    gated[dt][:, t0:t0 + 512],
                                    start=(dt == 0), stop=(dt == NDT - 1))
                            last.then_inc(mm, 1); cmm.add()

                @b.vector
                def _(veg):
                    for ft in range(4):
                        odt = (c0 + ft * 128) // 128
                        for tchunk in range(2):
                            t0 = tchunk * 512
                            k = ft * 2 + tchunk
                            vop(veg, lambda odt=odt, t0=t0, k=k:
                                veg.scalar_tensor_tensor(
                                    x3[odt][:, t0:t0 + 512], pqk[k % 2][:],
                                    bos[:, odt:odt + 1],
                                    x3[odt][:, t0:t0 + 512],
                                    ALU.add, ALU.add),
                                (mm, mm0 + k + 1),
                                (ld, 16 * cld.n),
                                (gld, 16 * cgld.n))
        wo_stack.close()
        gg_stack.close()

        ln_stack = ExitStack()
        lnsb = lambda n_, s: ln_stack.enter_context(
            nc.sbuf_tensor("E" + n_, s, F32))
        tscr = lnsb("tscr", [128, TC])
        st_s = lnsb("st_s", [1, TC])
        st_q = lnsb("st_q", [1, TC])
        r_rstd = lnsb("r_rstd", [1, TC])
        r_mrstd = lnsb("r_mrstd", [1, TC])
        r_var = lnsb("r_var", [1, TC])
        rstd_b = lnsb("rstd_b", [128, TC])
        mrstd_b = lnsb("mrstd_b", [128, TC])

        def emit_ln(src_tiles, dst_bf):
            pst_stack = ExitStack()
            pstat = ps_on(pst_stack, [1, TC])
            pstatq = ps_on(pst_stack, [1, TC])
            ac0, mm0, ve0 = cac.n, cmm.n, cve.n
            with nc.Block() as b:
                @b.scalar
                def _(sc):
                    for dt in range(NDT):
                        if dt >= 1:
                            sc.wait_ge(mm, mm0 + dt)
                        aop(sc, lambda dt=dt: sc.activation(
                            tscr[:], src_tiles[dt][:], AF.Square))

                @b.tensor
                def _(te):
                    for dt in range(NDT):
                        te.wait_ge(ac, ac0 + dt + 1)
                        for tchunk in range(2):
                            t0 = tchunk * 512
                            te.matmul(pstat[:, t0:t0 + 512], idn["ones128"][:],
                                      src_tiles[dt][:, t0:t0 + 512],
                                      start=(dt == 0), stop=(dt == NDT - 1))
                            last = te.matmul(pstatq[:, t0:t0 + 512],
                                             idn["ones128"][:],
                                             tscr[:, t0:t0 + 512],
                                             start=(dt == 0),
                                             stop=(dt == NDT - 1))
                        last.then_inc(mm, 1); cmm.add()

            ve0, ac0 = cve.n, cac.n
            with nc.Block() as b:
                @b.vector
                def _(veg):
                    vop(veg, lambda: veg.tensor_scalar_mul(st_s[:], pstat[:],
                                                           1.0 / D))
                    vop(veg, lambda: veg.tensor_scalar_mul(st_q[:], pstatq[:],
                                                           1.0 / D))
                    vop(veg, lambda: veg.tensor_tensor(r_var[:], st_s[:],
                                                       st_s[:], ALU.mult))
                    vop(veg, lambda: veg.tensor_tensor(r_var[:], st_q[:],
                                                       r_var[:], ALU.subtract))

                @b.scalar
                def _(sc):
                    aop(sc, lambda: sc.activation(r_rstd[:], r_var[:], AF.Ln,
                                                  bias=idn["eps1"][:]),
                        (ve, ve0 + 4))
                    aop(sc, lambda: sc.activation(r_rstd[:], r_rstd[:],
                                                  AF.Exp, scale=-0.5))

            ve0 = cve.n
            with nc.Block() as b:
                @b.vector
                def _(veg):
                    vop(veg, lambda: veg.tensor_tensor(r_mrstd[:], st_s[:],
                                                       r_rstd[:], ALU.mult))

                @b.sync
                def _(sy):
                    sy.wait_ge(ve, ve0 + 1)
                    sy.dma_start(stats_d[0:1, :], r_rstd[:]).then_inc(ld, 16)
                    cld.add()
                    sy.dma_start(stats_d[1:2, :], r_mrstd[:]).then_inc(ld, 16)
                    cld.add()
                    sy.wait_ge(ld, 16 * cld.n)
                    sy.dma_start(rstd_b[:], dram_bcast_ap(stats_d, 0, 0, TC)
                                 ).then_inc(ld, 16); cld.add()
                    sy.dma_start(mrstd_b[:], dram_bcast_ap(stats_d, 1, 0, TC)
                                 ).then_inc(ld, 16); cld.add()

            with nc.Block() as b:
                @b.vector
                def _(veg):
                    veg.wait_ge(ld, 16 * cld.n)
                    for dt in range(NDT):
                        vop(veg, lambda dt=dt: veg.tensor_tensor(
                            tscr[:], src_tiles[dt][:], rstd_b[:], ALU.mult))
                        vop(veg, lambda dt=dt: veg.tensor_tensor(
                            dst_bf[dt][:], tscr[:], mrstd_b[:], ALU.subtract))
            pst_stack.close()

        emit_ln(x3, xnbf)
        ln_stack.close()

        rt_stack = ExitStack()
        rws = rt_stack.enter_context(nc.sbuf_tensor("Erws", [NEXP, D], F32))
        rwT = rt_stack.enter_context(
            nc.sbuf_tensor("ErwT", [128, NDT, NEXP], BF16))
        lgits = rt_stack.enter_context(
            nc.sbuf_tensor("Elgits", [128, NEXP], F32))
        pr_m = rt_stack.enter_context(nc.sbuf_tensor("Epr_m", [128, 1], F32))
        pr_s = rt_stack.enter_context(nc.sbuf_tensor("Epr_s", [128, 1], F32))
        probs = rt_stack.enter_context(nc.sbuf_tensor("Eprobs", [128, NEXP], F32))
        m8 = rt_stack.enter_context(nc.sbuf_tensor("Em8", [128, 8], F32))
        mrep = rt_stack.enter_context(nc.sbuf_tensor("Emrep", [128, NEXP], F32))
        cmb = rt_stack.enter_context(nc.sbuf_tensor("Ecmb", [128, NEXP], F32))
        combT = rt_stack.enter_context(nc.sbuf_tensor("EcombT", [16, TC], F32))
        ptr16 = ps_on(rt_stack, [128, 16])
        ptc = ps_on(rt_stack, [16, 128])
        prr = [ps_on(rt_stack, [128, NEXP]) for _ in range(2)]

        ld0, mm0, ve0, ac0 = cld.n, cmm.n, cve.n, cac.n
        with nc.Block() as b:
            @b.sync
            def _(sy):
                sy.dma_start(rws[:], rw[:]).then_inc(ld, 16); cld.add()

            @b.tensor
            def _(te):
                te.wait_ge(ld, 16 * cld.n)
                for dt in range(NDT):
                    if dt >= 1:
                        te.wait_ge(ve, ve0 + dt)
                    te.transpose(ptr16[:], rws[:, dt * 128:(dt + 1) * 128],
                                 idn["ident16"][:]).then_inc(mm, 1); cmm.add()

            @b.vector
            def _(veg):
                for dt in range(NDT):
                    vop(veg, lambda dt=dt: veg.tensor_copy(
                        rwT[:, dt, :], ptr16[:]), (mm, mm0 + dt + 1))

        for tb in range(8):
            tb0 = tb * 128
            mm0, ve0, ac0, ly0 = cmm.n, cve.n, cac.n, cly.n
            with nc.Block() as b:
                @b.tensor
                def _(te):
                    for dt in range(NDT):
                        last = te.matmul(prr[tb % 2][:], xnbf[dt][:, tb0:tb0 + 128],
                                         rwT[:, dt, :],
                                         start=(dt == 0), stop=(dt == NDT - 1))
                    last.then_inc(mm, 1); cmm.add()

                @b.vector
                def _(veg):
                    vop(veg, lambda: veg.reduce_max(pr_m[:], prr[tb % 2][:],
                                                    axis=mybir.AxisListType.X,
                                                    negate=True),
                        (mm, mm0 + 1))
                    vop(veg, lambda: veg.tensor_scalar(
                        lgits[:], prr[tb % 2][:], pr_m[:], None, ALU.add))

                @b.scalar
                def _(sc):
                    aop(sc, lambda: sc.activation(probs[:], lgits[:], AF.Exp),
                        (ve, ve0 + 2))

                @b.vector
                def _(veg):
                    vop(veg, lambda: veg.reduce_sum(pr_s[:], probs[:],
                                                    axis=mybir.AxisListType.X),
                        (ac, ac0 + 1))
                    vop(veg, lambda: veg.reciprocal(pr_s[:], pr_s[:]))
                    vop(veg, lambda: veg.tensor_scalar(
                        probs[:], probs[:], pr_s[:], None, ALU.mult))
                    vop(veg, lambda: veg.max(out=m8[:], in_=probs[:]))
                    vop(veg, lambda: veg.memset(m8[:, 4:8], -1e30))
                    vop(veg, lambda: veg.match_replace(
                        out=mrep[:], in_to_replace=m8[:], in_values=probs[:],
                        imm_value=-7.0))
                    vop(veg, lambda: veg.tensor_scalar(
                        mrep[:], mrep[:], -7.0, None, ALU.is_equal))
                    vop(veg, lambda: veg.tensor_tensor(
                        cmb[:], probs[:], mrep[:], ALU.mult))
                    vop(veg, lambda: veg.reduce_sum(pr_s[:], m8[:, 0:4],
                                                    axis=mybir.AxisListType.X))
                    vop(veg, lambda: veg.reciprocal(pr_s[:], pr_s[:]))
                    vop(veg, lambda: veg.tensor_scalar(
                        cmb[:], cmb[:], pr_s[:], None, ALU.mult))

            mm0, ve0 = cmm.n, cve.n
            with nc.Block() as b:
                @b.tensor
                def _(te):
                    te.transpose(ptc[:], cmb[:], idn["ident128"][:]
                                 ).then_inc(mm, 1); cmm.add()

                @b.vector
                def _(veg):
                    vop(veg, lambda: veg.tensor_copy(
                        combT[:, tb0:tb0 + 128], ptc[:]), (mm, mm0 + 1))

        ve0, ly0 = cve.n, cly.n
        with nc.Block() as b:
            @b.sync
            def _(sy):
                sy.dma_start(comb_d[:], combT[:]).then_inc(ly, 16); cly.add()
        comb_store_cnt = cly.n
        rt_stack.close()

        def gup_src(e):
            return bass.AP(tensor=gup.tensor, offset=e * D * 2 * FI,
                           ap=[[2 * FI, 128], [2 * FI * 128, NDT],
                               [1, 2 * FI]])

        def dwn_src(e):
            return bass.AP(tensor=dwn.tensor, offset=e * FI * D,
                           ap=[[D, 128], [D * 128, NDT_FI], [1, D]])

        moe_stack = ExitStack()
        gupbf = moe_stack.enter_context(
            nc.sbuf_tensor("Egupbf", [128, NDT, 2 * FI], BF16))
        dwnbf = moe_stack.enter_context(
            nc.sbuf_tensor("Edwnbf", [128, FI // 128, D], BF16))
        comb_b = moe_stack.enter_context(
            nc.sbuf_tensor("Ecomb_b", [128, TC], F32))
        silu_t = moe_stack.enter_context(
            nc.sbuf_tensor("Esilu_t", [128, 512], BF16))
        up_t = moe_stack.enter_context(
            nc.sbuf_tensor("Eup_t", [128, 512], BF16))
        fuse_e = moe_stack.enter_context(
            nc.sbuf_tensor("Efuse_e", [128, FI // 128, TC], BF16))
        pg = [ps_on(moe_stack, [128, 512]) for _ in range(2)]
        pu = [ps_on(moe_stack, [128, 512]) for _ in range(2)]
        pdn = [ps_on(moe_stack, [128, 512]) for _ in range(2)]
        NFI = FI // 128
        NDT_FI = NFI

        gldG = sem("gldG"); gldD = sem("gldD")
        with nc.Block() as b:
            @b.gpsimd
            def _(gp):
                gp.wait_ge(ly, 16 * comb_store_cnt)
                gp.dma_start(gupbf[:], gup_src(0)).then_inc(gldG, 16)

        mm0E, ve0E, ac0E, ly0E = cmm.n, cve.n, cac.n, cly.n
        mmGU_E = {}
        mmDN_E = {}
        veGU_E = {}
        for e in range(NEXP):
            mmGU_E[e] = mm0E + e * 44 + 12
            mmDN_E[e] = mm0E + (e + 1) * 44
            veGU_E[e] = ve0E + e * 56 + 24
        with nc.Block() as b:
            @b.gpsimd
            def _(gp):
                for e in range(NEXP):
                    if e > 0:
                        gp.wait_ge(mm, mmDN_E[e - 1])
                    gp.dma_start(dwnbf[:], dwn_src(e)).then_inc(gldD, 16)
                    if e + 1 < NEXP:
                        gp.wait_ge(mm, mmGU_E[e])
                        gp.dma_start(gupbf[:], gup_src(e + 1)).then_inc(gldG, 16)

            @b.sync
            def _(sy):
                for e in range(NEXP):
                    if e > 0:
                        sy.wait_ge(ve, veGU_E[e - 1])
                    sy.dma_start(comb_b[:], dram_bcast_ap(comb_d, e, 0, TC)
                                 ).then_inc(ly, 16); cly.add()

            @b.tensor
            def _(te):
                for e in range(NEXP):
                    te.wait_ge(gldG, 16 * (e + 1))
                    for fr in range(NFI):
                        for tchunk in range(2):
                            t0 = tchunk * 512
                            k = fr * 2 + tchunk
                            if k >= 2:
                                te.wait_ge(ve, ve0E + e * 56 + 2 * (k - 2) + 2)
                                te.wait_ge(ac, ac0E + e * 12 + k - 1)
                            elif e > 0:
                                te.wait_ge(ve, veGU_E[e - 1])
                                te.wait_ge(ac, ac0E + e * 12)
                            for dt in range(NDT):
                                te.matmul(pg[tchunk][:],
                                          gupbf[:, dt, fr * 128:(fr + 1) * 128],
                                          xnbf[dt][:, t0:t0 + 512],
                                          start=(dt == 0), stop=(dt == NDT - 1))
                                last = te.matmul(
                                    pu[tchunk][:],
                                    gupbf[:, dt,
                                          FI + fr * 128:FI + (fr + 1) * 128],
                                    xnbf[dt][:, t0:t0 + 512],
                                    start=(dt == 0), stop=(dt == NDT - 1))
                            last.then_inc(mm, 1); cmm.add()
                    te.wait_ge(gldD, 16 * (e + 1))
                    te.wait_ge(ve, veGU_E[e])
                    for odt in range(NDT):
                        for tchunk in range(2):
                            t0 = tchunk * 512
                            k = odt * 2 + tchunk
                            if k >= 2:
                                te.wait_ge(ve, veGU_E[e] + k - 1)
                            for fr in range(NFI):
                                last = te.matmul(
                                    pdn[k % 2][:],
                                    dwnbf[:, fr, odt * 128:(odt + 1) * 128],
                                    fuse_e[:, fr, t0:t0 + 512],
                                    start=(fr == 0), stop=(fr == NFI - 1))
                            last.then_inc(mm, 1); cmm.add()

            @b.scalar
            def _(sc):
                for e in range(NEXP):
                    for fr in range(NFI):
                        for tchunk in range(2):
                            k = fr * 2 + tchunk
                            aop(sc, lambda tchunk=tchunk: sc.activation(
                                silu_t[:], pg[tchunk][:], AF.Silu),
                                (mm, mm0E + e * 44 + k + 1),
                                (ve, ve0E + e * 56 + 2 * k))

            @b.vector
            def _(veg):
                for e in range(NEXP):
                    for fr in range(NFI):
                        for tchunk in range(2):
                            t0 = tchunk * 512
                            k = fr * 2 + tchunk
                            waits = [(mm, mm0E + e * 44 + k + 1),
                                     (ly, 16 * (ly0E + e + 1))]
                            if k == 0 and e > 0:
                                waits.append((mm, mmDN_E[e - 1]))
                            vop(veg, lambda tchunk=tchunk, t0=t0:
                                veg.tensor_tensor(
                                    up_t[:], pu[tchunk][:],
                                    comb_b[:, t0:t0 + 512], ALU.mult),
                                *waits)
                            vop(veg, lambda fr=fr, t0=t0: veg.tensor_tensor(
                                fuse_e[:, fr, t0:t0 + 512], silu_t[:],
                                up_t[:], ALU.mult),
                                (ac, ac0E + e * 12 + k + 1))
                    for odt in range(NDT):
                        for tchunk in range(2):
                            t0 = tchunk * 512
                            k = odt * 2 + tchunk
                            vop(veg, lambda odt=odt, t0=t0, k=k:
                                veg.tensor_tensor(
                                    x3[odt][:, t0:t0 + 512], pdn[k % 2][:],
                                    x3[odt][:, t0:t0 + 512], ALU.add),
                                (mm, mmGU_E[e] + k + 1))

        moe_stack.close()

        fin_stack = ExitStack()
        otile = [fin_stack.enter_context(
            nc.sbuf_tensor(f"Eotile{i}", [128, D], F32)) for i in range(2)]
        ptro = [ps_on(fin_stack, [128, 128]) for _ in range(4)]
        mm0, ve0 = cmm.n, cve.n
        with nc.Block() as b:
            @b.tensor
            def _(te):
                for tt in range(8):
                    for dt in range(NDT):
                        k = tt * NDT + dt
                        if k >= 4:
                            te.wait_ge(ve, ve0 + k - 3)
                        te.transpose(ptro[k % 4][:],
                                     x3[dt][:, tt * 128:(tt + 1) * 128],
                                     idn["ident128"][:]).then_inc(mm, 1)
                        cmm.add()

            @b.vector
            def _(veg):
                for tt in range(8):
                    for dt in range(NDT):
                        k = tt * NDT + dt
                        w = [(mm, mm0 + k + 1)]
                        if dt == 0 and tt >= 2:
                            w.append((st2[tt % 2], 16 * (tt // 2)))
                        vop(veg, lambda tt=tt, dt=dt, k=k: veg.tensor_copy(
                            otile[tt % 2][:, dt * 128:(dt + 1) * 128],
                            ptro[k % 4][:]), *w)

            @b.sync
            def _(sy):
                for tt in range(8):
                    sy.wait_ge(ve, ve0 + (tt + 1) * NDT)
                    sy.dma_start(out_t[tt * 128:(tt + 1) * 128, :],
                                 otile[tt % 2][:]).then_inc(st2[tt % 2], 16)
                    cst2[tt % 2].add()
                sy.wait_ge(st2[0], 16 * cst2[0].n)
                sy.wait_ge(st2[1], 16 * cst2[1].n)
        fin_stack.close()

    return nc



def _col16(v):
    return np.ascontiguousarray(v.reshape(16, 128).T)


def _col64(v):
    return np.ascontiguousarray(v.reshape(64, 128).T)


def kernel_unfused(**inp):

    cores = list(range(NCORES))
    x = np.ascontiguousarray(inp["L0_attn_x"].reshape(TOK, D))

    nc = build_launch_A()
    xpw = np.ascontiguousarray(inp["L0_attn_x_proj_w"])
    maps = [dict(xa=np.ascontiguousarray(x[c * TC:(c + 1) * TC]), xpw=xpw)
            for c in cores]
    res = run_bass_kernel_spmd(nc, maps, core_ids=cores)
    xn_full = np.concatenate([res.results[c]["xn_out"] for c in cores], axis=1)
    xp_full = np.concatenate([res.results[c]["xp_out"] for c in cores], axis=1)

    nc = build_launch_B()
    dtw = inp["L0_attn_dt_proj_w"]
    dtb = inp["L0_attn_dt_proj_b"].reshape(-1, 1)
    alog = inp["L0_attn_A_log"]
    dp = inp["L0_attn_D_param"].reshape(-1, 1)
    maps = []
    for c in cores:
        ds = slice(c * DS, (c + 1) * DS)
        maps.append(dict(
            xn_ds=np.ascontiguousarray(xn_full[ds]), xp_full=xp_full,
            dtw_sl=np.ascontiguousarray(dtw[ds]),
            dtb_sl=np.ascontiguousarray(dtb[ds]),
            alog_sl=np.ascontiguousarray(alog[ds]),
            dp_sl=np.ascontiguousarray(dp[ds])))
    res = run_bass_kernel_spmd(nc, maps, core_ids=cores)
    attn0_full = np.concatenate([res.results[c]["attn0_out"] for c in cores],
                                axis=0)

    nc = build_launch_C()
    common = dict(
        wg8=inp["L0_ffn_w_gate"], wu8=inp["L0_ffn_w_up"],
        wd8=inp["L0_ffn_w_down"],
        sg_l=_col64(inp["L0_ffn_s_gate"]), su_l=_col64(inp["L0_ffn_s_up"]),
        sd_l=_col16(inp["L0_ffn_s_down"]),
        wq=inp["L1_attn_Wq"], wk=inp["L1_attn_Wk"], wv=inp["L1_attn_Wv"],
        wgt=inp["L1_attn_Wg"],
        bq=_col16(inp["L1_attn_bq"]), bk=_col16(inp["L1_attn_bk"]),
        bv=_col16(inp["L1_attn_bv"]), bg=_col16(inp["L1_attn_bg"]))
    maps = []
    for c in cores:
        ts = slice(c * TC, (c + 1) * TC)
        m = dict(common)
        m["xa"] = np.ascontiguousarray(x[ts])
        m["attn0_t"] = np.ascontiguousarray(attn0_full[:, ts])
        maps.append(m)
    res = run_bass_kernel_spmd(nc, maps, core_ids=cores)
    x2_full = np.concatenate([res.results[c]["x2_out"] for c in cores], axis=1)
    q_full = np.concatenate([res.results[c]["q_out"] for c in cores], axis=1)
    k_full = np.concatenate([res.results[c]["k_out"] for c in cores], axis=1)
    v_full = np.concatenate([res.results[c]["v_out"] for c in cores], axis=1)
    gate_full = np.concatenate([res.results[c]["gate_out"] for c in cores],
                               axis=1)

    nc = build_launch_D()
    decay = inp["L1_attn_decay"]
    gnw = inp["L1_attn_gn_w"].reshape(H, HD)
    gnb = inp["L1_attn_gn_b"].reshape(H, HD)
    maps = []
    pair_lists = []
    for c in cores:
        hA, hB = 2 * c, 2 * c + 1
        pairs = [(0, hA), (1, hA), (0, hB), (1, hB)]
        pair_lists.append(pairs)
        qT = np.stack([q_full[h * HD:(h + 1) * HD, b * L:(b + 1) * L]
                       for (b, h) in pairs])
        kT = np.stack([k_full[h * HD:(h + 1) * HD, b * L:(b + 1) * L]
                       for (b, h) in pairs])
        vtm = np.stack([v_full[h * HD:(h + 1) * HD, b * L:(b + 1) * L].T
                        for (b, h) in pairs])
        maps.append(dict(
            qT=np.ascontiguousarray(qT), kT=np.ascontiguousarray(kT),
            vtm=np.ascontiguousarray(vtm),
            decay_d=np.ascontiguousarray(decay[[hA, hB]].reshape(1, 2)),
            gnw_d=np.ascontiguousarray(gnw[[hA, hB]].T),
            gnb_d=np.ascontiguousarray(gnb[[hA, hB]].T)))
    res = run_bass_kernel_spmd(nc, maps, core_ids=cores)
    g_full = np.empty((D, TOK), np.float32)
    for c in cores:
        go = res.results[c]["g_out"].reshape(4, HD, L)
        for pi, (b, h) in enumerate(pair_lists[c]):
            g_full[h * HD:(h + 1) * HD, b * L:(b + 1) * L] = go[pi]

    nc = build_launch_E()
    common = dict(
        wo=inp["L1_attn_Wo"], bo_l=_col16(inp["L1_attn_bo"]),
        rw=inp["L1_ffn_router_weight"],
        gup=inp["L1_ffn_gate_up_proj"], dwn=inp["L1_ffn_down_proj"])
    maps = []
    for c in cores:
        ts = slice(c * TC, (c + 1) * TC)
        m = dict(common)
        m["x2_t"] = np.ascontiguousarray(x2_full[:, ts])
        m["g_t"] = np.ascontiguousarray(g_full[:, ts])
        m["gate_t"] = np.ascontiguousarray(gate_full[:, ts])
        maps.append(m)
    res = run_bass_kernel_spmd(nc, maps, core_ids=cores)
    out = np.concatenate([res.results[c]["out_t"] for c in cores], axis=0)
    return out.reshape(B, L, D)



def build_fused():
    import os as _os
    _os.environ["TRN_ALLOW_ICI_ALL_TO_ALL"] = "1"
    nc = bass.Bass("TRN2", target_bir_lowering=False)
    ein = lambda n_, s, d: nc.dram_tensor(n_, s, d, kind="ExternalInput").ap()
    intl = lambda n_, s, d=F32: nc.dram_tensor(n_, s, d).ap()

    xa = ein("xa", [TC, D], F32)
    xpw = ein("xpw", [RK + 2 * NST, D], F32)
    dtw_sl = ein("dtw_sl", [DS, RK], F32)
    dtb_sl = ein("dtb_sl", [DS, 1], F32)
    alog_sl = ein("alog_sl", [DS, NST], F32)
    dp_sl = ein("dp_sl", [DS, 1], F32)
    wg8 = ein("wg8", [D, FF], I8)
    wu8 = ein("wu8", [D, FF], I8)
    wd8 = ein("wd8", [FF, D], I8)
    sg_l = ein("sg_l", [128, 64], F32)
    su_l = ein("su_l", [128, 64], F32)
    sd_l = ein("sd_l", [128, 16], F32)
    wq = ein("wq", [D, D], F32)
    wk = ein("wk", [D, D], F32)
    wv = ein("wv", [D, D], F32)
    wgt = ein("wgt", [D, D], F32)
    bq = ein("bq", [128, 16], F32)
    bk = ein("bk", [128, 16], F32)
    bv = ein("bv", [128, 16], F32)
    bg = ein("bg", [128, 16], F32)
    decay_d = ein("decay_d", [1, 2], F32)
    gnw_d = ein("gnw_d", [128, 2], F32)
    gnb_d = ein("gnb_d", [128, 2], F32)
    wo = ein("wo", [D, D], F32)
    bo_l = ein("bo_l", [128, 16], F32)
    rw = ein("rw", [NEXP, D], F32)
    gup = ein("gup", [NEXP, D, 2 * FI], F32)
    dwn = ein("dwn", [NEXP, FI, D], F32)
    out_t = nc.dram_tensor("out_t", [TC, D], F32, kind="ExternalOutput").ap()

    xn_loc = intl("xn_loc", [D, TC])
    xn_a2a = intl("xn_a2a", [D, TC])
    xp_loc = intl("xp_loc", [160, TC])
    xp_ag = intl("xp_ag", [8 * 160, TC])
    attn0_pre = intl("attn0_pre", [D, TC])
    attn0_t = intl("attn0_t2", [D, TC])
    y_dram = intl("y_dram", [DS, TOK])
    q_loc = intl("q_loc", [D, TC])
    k_loc = intl("k_loc", [D, TC])
    v_loc = intl("v_loc", [D, TC])
    q_a2a = intl("q_a2a", [D, TC])
    k_a2a = intl("k_a2a", [D, TC])
    v_a2a = intl("v_a2a", [D, TC])
    x2_loc = intl("x2_loc", [D, TC])
    gate_loc = intl("gate_loc", [D, TC])
    g_stage = intl("g_stage", [D, TC])
    g_a2a = intl("g_a2a", [D, TC])
    stats_dA = intl("stats_dA", [2, TC])
    stats_dC = intl("stats_dC", [2, TC])
    stats_dE = intl("stats_dE", [2, TC])
    comb_d = intl("comb_d", [NEXP, TC])
    scrD = intl("scrD", [4, L])

    RGRP = [list(range(NCORES))]

    gstack = ExitStack()
    with gstack:
        idn = alloc_idents(nc, gstack)
        semnames = {}

        def mksem(name):
            assert name not in semnames
            s = gstack.enter_context(nc.semaphore(name))
            semnames[name] = s
            return s

        ccsem = mksem("ccsem")
        ncc = [0]

        def collective(kind, pairs_list):
            with nc.Block() as b:
                @b.gpsimd
                def _(gp):
                    for (src, dst) in pairs_list:
                        gp.collective_compute(
                            kind, ALU.bypass, replica_groups=RGRP,
                            ins=[src.opt()], outs=[dst.opt()],
                        ).then_inc(ccsem)
                        ncc[0] += 1
                    gp.wait_ge(ccsem, ncc[0])

        with nc.Block() as b:
            @b.gpsimd
            def _(gp):
                emit_idents(gp, idn, mksem("identpo"), Ctr())

        env = dict(nc=nc, mksem=mksem, idn=idn)

        with nc.named_scope("phA"):
            build_launch_A(dict(env, xa=xa, xpw=xpw, xn_out=xn_loc,
                                xp_out=xp_loc, stats_d=stats_dA))
        with nc.named_scope("ccA"):
            with nc.Block() as b:
                @b.gpsimd
                def _(gp):
                    gp.collective_compute(
                        "AllToAll", ALU.bypass, replica_groups=RGRP,
                        ins=[xn_loc.opt()], outs=[xn_a2a.opt()],
                    ).then_inc(ccsem)
                    ncc[0] += 1
                    gp.collective_compute(
                        "AllGather", ALU.bypass, replica_groups=RGRP,
                        ins=[xp_loc.opt()], outs=[xp_ag.opt()],
                    ).then_inc(ccsem)
                    ncc[0] += 1
                    gp.wait_ge(ccsem, ncc[0])

        with nc.named_scope("phB"):
            build_launch_B(dict(env, dtw_sl=dtw_sl, dtb_sl=dtb_sl,
                                alog_sl=alog_sl, dp_sl=dp_sl, y_dram=y_dram,
                                xn_a2a=xn_a2a, xp_ag=xp_ag,
                                attn0_stage=attn0_pre))
        with nc.named_scope("ccB"):
            collective("AllToAll", [(attn0_pre, attn0_t)])

        with nc.named_scope("phC"):
            build_launch_C(dict(env, xa=xa, attn0_t=attn0_t, wg8=wg8, wu8=wu8,
                                wd8=wd8, sg_l=sg_l, su_l=su_l, sd_l=sd_l,
                                wq=wq, wk=wk, wv=wv, wgt=wgt, bq=bq, bk=bk,
                                bv=bv, bg=bg, x2_out=x2_loc, q_out=q_loc,
                                k_out=k_loc, v_out=v_loc, gate_out=gate_loc,
                                stats_d=stats_dC))
        with nc.named_scope("ccC"):
            collective("AllToAll", [(q_loc, q_a2a), (k_loc, k_a2a),
                                    (v_loc, v_a2a)])

        with nc.named_scope("phD"):
            build_launch_D(dict(env, decay_d=decay_d, gnw_d=gnw_d, gnb_d=gnb_d,
                                q_a2a=q_a2a, k_a2a=k_a2a, v_a2a=v_a2a,
                                g_stage=g_stage, scr=scrD))
        with nc.named_scope("ccD"):
            collective("AllToAll", [(g_stage, g_a2a)])

        with nc.named_scope("phE"):
            build_launch_E(dict(env, x2_t=x2_loc, g_t=g_a2a, gate_t=gate_loc,
                                wo=wo, bo_l=bo_l, rw=rw, gup=gup, dwn=dwn,
                                out_t=out_t, stats_d=stats_dE, comb_d=comb_d))

    return nc


def kernel(**inp):
    import os as _os
    _os.environ["TRN_ALLOW_ICI_ALL_TO_ALL"] = "1"
    cores = list(range(NCORES))
    x = np.ascontiguousarray(inp["L0_attn_x"].reshape(TOK, D))
    dtw = inp["L0_attn_dt_proj_w"]
    dtb = inp["L0_attn_dt_proj_b"].reshape(-1, 1)
    alog = inp["L0_attn_A_log"]
    dp = inp["L0_attn_D_param"].reshape(-1, 1)
    decay = inp["L1_attn_decay"]
    gnw = inp["L1_attn_gn_w"].reshape(H, HD)
    gnb = inp["L1_attn_gn_b"].reshape(H, HD)
    common = dict(
        xpw=np.ascontiguousarray(inp["L0_attn_x_proj_w"]),
        wg8=inp["L0_ffn_w_gate"], wu8=inp["L0_ffn_w_up"],
        wd8=inp["L0_ffn_w_down"],
        sg_l=_col64(inp["L0_ffn_s_gate"]), su_l=_col64(inp["L0_ffn_s_up"]),
        sd_l=_col16(inp["L0_ffn_s_down"]),
        wq=inp["L1_attn_Wq"], wk=inp["L1_attn_Wk"], wv=inp["L1_attn_Wv"],
        wgt=inp["L1_attn_Wg"],
        bq=_col16(inp["L1_attn_bq"]), bk=_col16(inp["L1_attn_bk"]),
        bv=_col16(inp["L1_attn_bv"]), bg=_col16(inp["L1_attn_bg"]),
        wo=inp["L1_attn_Wo"], bo_l=_col16(inp["L1_attn_bo"]),
        rw=inp["L1_ffn_router_weight"],
        gup=inp["L1_ffn_gate_up_proj"], dwn=inp["L1_ffn_down_proj"])
    nc = build_fused()
    maps = []
    for c in cores:
        ds = slice(c * DS, (c + 1) * DS)
        hA, hB = 2 * c, 2 * c + 1
        m = dict(common)
        m["xa"] = np.ascontiguousarray(x[c * TC:(c + 1) * TC])
        m["dtw_sl"] = np.ascontiguousarray(dtw[ds])
        m["dtb_sl"] = np.ascontiguousarray(dtb[ds])
        m["alog_sl"] = np.ascontiguousarray(alog[ds])
        m["dp_sl"] = np.ascontiguousarray(dp[ds])
        m["decay_d"] = np.ascontiguousarray(decay[[hA, hB]].reshape(1, 2))
        m["gnw_d"] = np.ascontiguousarray(gnw[[hA, hB]].T)
        m["gnb_d"] = np.ascontiguousarray(gnb[[hA, hB]].T)
        maps.append(m)
    res = run_bass_kernel_spmd(nc, maps, core_ids=cores)
    out = np.concatenate([res.results[c]["out_t"] for c in cores], axis=0)
    return out.reshape(B, L, D)




# version 06 (final)
# speedup vs baseline: 1.0628x; 1.0628x over previous
import numpy as np
from contextlib import ExitStack

import concourse.bass as bass
import concourse.mybir as mybir
from concourse.bass_utils import run_bass_kernel_spmd

F32 = mybir.dt.float32
BF16 = mybir.dt.bfloat16
I8 = mybir.dt.int8
AF = mybir.ActivationFunctionType
ALU = mybir.AluOpType

NCORES = 8
B = 2
L = 4096
D = 2048
TOK = B * L
TC = TOK // NCORES
NDT = D // 128
RK = 128
NST = 16
DS = D // NCORES
H = 16
HD = D // H
NEXP = 16
FI = 768
FF = 8192
EPS = 1e-5


class Ctr:
    __slots__ = ("n",)

    def __init__(self):
        self.n = 0

    def add(self, k=1):
        self.n += k
        return self.n


def dram_bcast_ap(dram_ap, row, col0, ncols, nrows=128):
    t = dram_ap.tensor
    rowlen = dram_ap.shape[-1]
    off = row * rowlen + col0
    return bass.AP(tensor=t, offset=off, ap=[[0, nrows], [1, ncols]])


def emit_idents(gp, idn, po, cpo):
    gp.memset(idn["ident128"][:], 1.0).then_inc(po, 1); cpo.add()
    gp.memset(idn["ident32"][:], 1.0).then_inc(po, 1); cpo.add()
    gp.memset(idn["ones128"][:], 1.0).then_inc(po, 1); cpo.add()
    gp.memset(idn["ones1r"][:], 1.0).then_inc(po, 1); cpo.add()
    gp.wait_ge(po, cpo.n)
    gp.affine_select(idn["ident128"][:], idn["ident128"][:], pattern=[[1, 128]],
                     compare_op=ALU.is_equal, fill=0.0, base=0,
                     channel_multiplier=-1).then_inc(po, 1); cpo.add()
    gp.affine_select(idn["ident32"][:], idn["ident32"][:], pattern=[[1, 32]],
                     compare_op=ALU.is_equal, fill=0.0, base=0,
                     channel_multiplier=-1).then_inc(po, 1); cpo.add()
    gp.memset(idn["eps1"][:], EPS).then_inc(po, 1); cpo.add()
    gp.memset(idn["one1"][:], 1.0).then_inc(po, 1); cpo.add()
    gp.memset(idn["ident16"][:], 1.0).then_inc(po, 1); cpo.add()
    gp.wait_ge(po, cpo.n)
    gp.affine_select(idn["ident16"][:], idn["ident16"][:], pattern=[[1, 16]],
                     compare_op=ALU.is_equal, fill=0.0, base=0,
                     channel_multiplier=-1).then_inc(po, 1); cpo.add()


def alloc_idents(nc, stack):
    sb = lambda n_, s, d: stack.enter_context(nc.sbuf_tensor(n_, s, d))
    return dict(ident128=sb("ident128", [128, 128], F32),
                ident32=sb("ident32", [32, 32], F32),
                ident16=sb("ident16", [16, 16], F32),
                ones128=sb("ones128", [128, 1], F32),
                ones1r=sb("ones1r", [1, 128], F32),
                eps1=sb("eps1", [1, 1], F32),
                one1=sb("one1", [1, 1], F32))



def build_launch_A(env=None):
    if env is None:
        nc = bass.Bass("TRN2", target_bir_lowering=False)
        xa = nc.dram_tensor("xa", [TC, D], F32, kind="ExternalInput").ap()
        xpw = nc.dram_tensor("xpw", [RK + 2 * NST, D], F32,
                             kind="ExternalInput").ap()
        xn_out = nc.dram_tensor("xn_out", [D, TC], F32,
                                kind="ExternalOutput").ap()
        xp_out = nc.dram_tensor("xp_out", [RK + 2 * NST, TC], F32,
                                kind="ExternalOutput").ap()
        stats_d = nc.dram_tensor("stats_d", [2, TC], F32).ap()
        mksem = None
    else:
        nc = env["nc"]
        xa = env["xa"]; xpw = env["xpw"]
        xn_out = env["xn_out"]; xp_out = env["xp_out"]
        stats_d = env["stats_d"]
        mksem = env["mksem"]

    stack = ExitStack()
    with stack:
        sb = lambda n_, s, d: stack.enter_context(nc.sbuf_tensor("A" + n_, s, d))
        _psn = [0]
        def ps(s):
            _psn[0] += 1
            return stack.enter_context(nc.psum_tensor(f"Aps{_psn[0]}", s, F32))
        if mksem is None:
            sem = lambda n_: stack.enter_context(nc.semaphore(n_))
        else:
            sem = lambda n_: mksem("A" + n_)

        xtm = [sb(f"xtm{i}", [128, D], F32) for i in range(2)]
        xfm = [sb(f"xfm{i}", [128, TC], F32) for i in range(NDT)]
        xnbf = [sb(f"xnbf{i}", [128, TC], BF16) for i in range(NDT)]
        sq = [sb(f"sq{i}", [128, TC], F32) for i in range(2)]
        xpw_lo = sb("xpw_lo", [128, D], F32)
        xpw_hi = sb("xpw_hi", [32, D], F32)
        xpwT = sb("xpwT", [128, NDT, 160], BF16)
        st_s = sb("st_s", [1, TC], F32)
        st_q = sb("st_q", [1, TC], F32)
        r_rstd = sb("r_rstd", [1, TC], F32)
        r_mrstd = sb("r_mrstd", [1, TC], F32)
        r_var = sb("r_var", [1, TC], F32)
        rstd_b = sb("rstd_b", [128, TC], F32)
        mrstd_b = sb("mrstd_b", [128, TC], F32)
        xnf = [sb(f"xnf{i}", [128, TC], F32) for i in range(2)]
        xp_lo = sb("xp_lo", [128, TC], F32)
        xp_hi = sb("xp_hi", [32, TC], F32)
        idn = env["idn"] if env is not None else alloc_idents(nc, stack)

        ptr = [ps([128, 128]) for _ in range(4)]
        stat_stack = ExitStack()
        pstat = stat_stack.enter_context(nc.psum_tensor("pstat", [1, TC], F32))
        pstatq = stat_stack.enter_context(nc.psum_tensor("pstatq", [1, TC], F32))

        ld = sem("ld"); mm = sem("mm"); ve = sem("ve"); po = sem("po")
        ac = sem("ac"); st = sem("st")
        lx2 = [sem("lxA"), sem("lxB")]
        sx2 = [sem("sxA"), sem("sxB")]
        csx = [Ctr(), Ctr()]
        cld = Ctr(); cmm = Ctr(); cve = Ctr(); cpo = Ctr()
        cac = Ctr(); cst = Ctr()

        with nc.Block() as b:
            @b.sync
            def _(sy):
                sy.dma_start(xpw_lo[:], xpw[0:128, :]).then_inc(ld, 16); cld.add()
                sy.dma_start(xpw_hi[:], xpw[128:160, :]).then_inc(ld, 16); cld.add()

            @b.gpsimd
            def _(gp):
                if env is None:
                    emit_idents(gp, idn, po, cpo)

        ld0, mm0, ve0 = cld.n, cmm.n, cve.n
        with nc.Block() as b:
            @b.sync
            def _(sy):
                for tt in range(8):
                    if tt >= 2:
                        sy.wait_ge(ve, ve0 + (tt - 1) * NDT)
                    sy.dma_start(xtm[tt % 2][:], xa[tt * 128:(tt + 1) * 128, :]
                                 ).then_inc(lx2[tt % 2], 16)

            @b.tensor
            def _(te):
                for tt in range(8):
                    te.wait_ge(lx2[tt % 2], 16 * (tt // 2 + 1))
                    for dt in range(NDT):
                        k = tt * NDT + dt
                        if k >= 4:
                            te.wait_ge(ve, ve0 + k - 3)
                        te.transpose(ptr[k % 4][:],
                                     xtm[tt % 2][:, dt * 128:(dt + 1) * 128],
                                     idn["ident128"][:]).then_inc(mm, 1); cmm.add()

            @b.vector
            def _(veg):
                for tt in range(8):
                    for dt in range(NDT):
                        k = tt * NDT + dt
                        veg.wait_ge(mm, mm0 + k + 1)
                        veg.tensor_copy(xfm[dt][:, tt * 128:(tt + 1) * 128],
                                        ptr[k % 4][:]).then_inc(ve, 1); cve.add()

        ac0, mm0 = cac.n, cmm.n
        with nc.Block() as b:
            @b.scalar
            def _(sc):
                for dt in range(NDT):
                    if dt >= 2:
                        sc.wait_ge(mm, mm0 + (dt - 1))
                    sc.activation(sq[dt % 2][:], xfm[dt][:], AF.Square
                                  ).then_inc(ac, 1); cac.add()

            @b.tensor
            def _(te):
                for dt in range(NDT):
                    te.wait_ge(ac, ac0 + dt + 1)
                    for tchunk in range(2):
                        t0 = tchunk * 512
                        te.matmul(pstat[:, t0:t0 + 512], idn["ones128"][:],
                                  xfm[dt][:, t0:t0 + 512],
                                  start=(dt == 0), stop=(dt == NDT - 1))
                        last = te.matmul(pstatq[:, t0:t0 + 512],
                                         idn["ones128"][:],
                                         sq[dt % 2][:, t0:t0 + 512],
                                         start=(dt == 0), stop=(dt == NDT - 1))
                    last.then_inc(mm, 1); cmm.add()

        ve0, ac0 = cve.n, cac.n
        with nc.Block() as b:
            @b.vector
            def _(veg):
                veg.tensor_scalar_mul(st_s[:], pstat[:], 1.0 / D
                                      ).then_inc(ve, 1); cve.add()
                veg.tensor_scalar_mul(st_q[:], pstatq[:], 1.0 / D
                                      ).then_inc(ve, 1); cve.add()
                veg.wait_ge(ve, ve0 + 2)
                veg.tensor_tensor(r_var[:], st_s[:], st_s[:],
                                  ALU.mult).then_inc(ve, 1); cve.add()
                veg.wait_ge(ve, ve0 + 3)
                veg.tensor_tensor(r_var[:], st_q[:], r_var[:],
                                  ALU.subtract).then_inc(ve, 1); cve.add()

            @b.scalar
            def _(sc):
                sc.wait_ge(ve, ve0 + 4)
                sc.activation(r_rstd[:], r_var[:], AF.Ln, bias=idn["eps1"][:]
                              ).then_inc(ac, 1); cac.add()
                sc.wait_ge(ac, ac0 + 1)
                sc.activation(r_rstd[:], r_rstd[:], AF.Exp, scale=-0.5
                              ).then_inc(ac, 1); cac.add()

        ve0, ld0 = cve.n, cld.n
        with nc.Block() as b:
            @b.vector
            def _(veg):
                veg.tensor_tensor(r_mrstd[:], st_s[:], r_rstd[:],
                                  ALU.mult).then_inc(ve, 1); cve.add()

            @b.sync
            def _(sy):
                sy.wait_ge(ve, ve0 + 1)
                sy.dma_start(stats_d[0:1, :], r_rstd[:]).then_inc(ld, 16); cld.add()
                sy.dma_start(stats_d[1:2, :], r_mrstd[:]).then_inc(ld, 16); cld.add()
                sy.wait_ge(ld, 16 * cld.n)
                sy.dma_start(rstd_b[:], dram_bcast_ap(stats_d, 0, 0, TC)
                             ).then_inc(ld, 16); cld.add()
                sy.dma_start(mrstd_b[:], dram_bcast_ap(stats_d, 1, 0, TC)
                             ).then_inc(ld, 16); cld.add()

        ve0, st0, mm0, po0 = cve.n, cst.n, cmm.n, cpo.n
        with nc.Block() as b:
            @b.vector
            def _(veg):
                veg.wait_ge(ld, 16 * cld.n)
                for dt in range(NDT):
                    if dt >= 2:
                        veg.wait_ge(sx2[dt % 2], 16 * (dt // 2))
                    veg.wait_ge(ve, cve.n)
                    veg.tensor_tensor(xnf[dt % 2][:], xfm[dt][:], rstd_b[:],
                                      ALU.mult).then_inc(ve, 1); cve.add()
                    veg.wait_ge(ve, cve.n)
                    veg.tensor_tensor(xnf[dt % 2][:], xnf[dt % 2][:],
                                      mrstd_b[:], ALU.subtract
                                      ).then_inc(ve, 1); cve.add()
                    veg.wait_ge(ve, cve.n)
                    veg.tensor_copy(xnbf[dt][:], xnf[dt % 2][:]
                                    ).then_inc(ve, 1); cve.add()

            @b.sync
            def _(sy):
                for dt in range(NDT):
                    sy.wait_ge(ve, ve0 + 3 * dt + 2)
                    sy.dma_start(xn_out[dt * 128:(dt + 1) * 128, :],
                                 xnf[dt % 2][:]).then_inc(sx2[dt % 2], 16)
                    csx[dt % 2].add()

            @b.tensor
            def _(te):
                for dt in range(NDT):
                    if dt >= 2:
                        te.wait_ge(po, po0 + 2 * (dt - 1))
                    te.transpose(ptr[(2 * dt) % 4][:],
                                 xpw_lo[:, dt * 128:(dt + 1) * 128],
                                 idn["ident128"][:]).then_inc(mm, 1); cmm.add()
                    te.transpose(ptr[(2 * dt + 1) % 4][0:128, 0:32],
                                 xpw_hi[:, dt * 128:(dt + 1) * 128],
                                 idn["ident32"][:]).then_inc(mm, 1); cmm.add()

            @b.scalar
            def _(sc):
                for dt in range(NDT):
                    sc.wait_ge(mm, mm0 + 2 * dt + 1)
                    sc.copy(xpwT[:, dt, 0:128], ptr[(2 * dt) % 4][:]
                            ).then_inc(po, 1); cpo.add()
                    sc.wait_ge(mm, mm0 + 2 * dt + 2)
                    sc.copy(xpwT[:, dt, 128:160],
                            ptr[(2 * dt + 1) % 4][0:128, 0:32]
                            ).then_inc(po, 1); cpo.add()

        stat_stack.close()
        pxp = [ps([128, 512]) for _ in range(2)]
        pxph = [ps([32, 512]) for _ in range(2)]
        mm0, ve0, st0 = cmm.n, cve.n, cst.n
        with nc.Block() as b:
            @b.tensor
            def _(te):
                for tchunk in range(2):
                    t0 = tchunk * 512
                    for dt in range(NDT):
                        te.matmul(pxp[tchunk][:], xpwT[:, dt, 0:128],
                                  xnbf[dt][:, t0:t0 + 512],
                                  start=(dt == 0), stop=(dt == NDT - 1))
                        last = te.matmul(pxph[tchunk][:], xpwT[:, dt, 128:160],
                                         xnbf[dt][:, t0:t0 + 512],
                                         start=(dt == 0), stop=(dt == NDT - 1))
                    last.then_inc(mm, 1); cmm.add()

            @b.vector
            def _(veg):
                for tchunk in range(2):
                    t0 = tchunk * 512
                    veg.wait_ge(mm, mm0 + tchunk + 1)
                    veg.tensor_copy(xp_lo[:, t0:t0 + 512], pxp[tchunk][:]
                                    ).then_inc(ve, 1); cve.add()
                    veg.tensor_copy(xp_hi[:, t0:t0 + 512], pxph[tchunk][:]
                                    ).then_inc(ve, 1); cve.add()

            @b.sync
            def _(sy):
                sy.wait_ge(ve, ve0 + 4)
                sy.dma_start(xp_out[0:128, :], xp_lo[:]).then_inc(st, 16); cst.add()
                sy.dma_start(xp_out[128:160, :], xp_hi[:]).then_inc(st, 16); cst.add()
                sy.wait_ge(st, 16 * cst.n)
                sy.wait_ge(sx2[0], 16 * csx[0].n)
                sy.wait_ge(sx2[1], 16 * csx[1].n)

    return nc





B_NST = NST
B_ACCUM = True


def build_launch_B(env=None):
    if env is None:
        nc = bass.Bass("TRN2", target_bir_lowering=False)
        xn_ds = nc.dram_tensor("xn_ds", [DS, TOK], F32, kind="ExternalInput").ap()
        xp_full = nc.dram_tensor("xp_full", [160, TOK], F32,
                                 kind="ExternalInput").ap()
        dtw_sl = nc.dram_tensor("dtw_sl", [DS, RK], F32, kind="ExternalInput").ap()
        dtb_sl = nc.dram_tensor("dtb_sl", [DS, 1], F32, kind="ExternalInput").ap()
        alog_sl = nc.dram_tensor("alog_sl", [DS, NST], F32,
                                 kind="ExternalInput").ap()
        dp_sl = nc.dram_tensor("dp_sl", [DS, 1], F32, kind="ExternalInput").ap()
        attn0_out = nc.dram_tensor("attn0_out", [DS, TOK], F32,
                                   kind="ExternalOutput").ap()
        y_dram = nc.dram_tensor("y_dram", [DS, TOK], F32).ap()
        mksem = None
        xn_a2a = xp_ag = attn0_stage = None
    else:
        nc = env["nc"]
        dtw_sl = env["dtw_sl"]; dtb_sl = env["dtb_sl"]
        alog_sl = env["alog_sl"]; dp_sl = env["dp_sl"]
        y_dram = env["y_dram"]
        xn_a2a = env["xn_a2a"]
        xp_ag = env["xp_ag"]
        attn0_stage = env["attn0_stage"]
        mksem = env["mksem"]
        xn_ds = xp_full = attn0_out = None
    xpbc_bf = nc.dram_tensor("xpbc_bf", [8 * 32, TC], BF16).ap()

    def xn_h_ap(dtile, bb):
        if env is None:
            return xn_ds[dtile * 128:(dtile + 1) * 128, bb * L:(bb + 1) * L]
        return bass.AP(tensor=xn_a2a.tensor,
                       offset=(bb * 4 * 256 + dtile * 128) * TC,
                       ap=[[TC, 128], [256 * TC, 4], [1, TC]])

    def dtl_ap():
        if env is None:
            return xp_full[0:128, :]
        return bass.AP(tensor=xp_ag.tensor, offset=0,
                       ap=[[TC, 128], [160 * TC, 8], [1, TC]])

    def bc_bcast_ap(row, bb):
        return bass.AP(tensor=xpbc_bf.tensor,
                       offset=(bb * 4 * 32 + (row - 128)) * TC,
                       ap=[[0, 128], [32 * TC, 4], [1, TC]])

    HB = L

    stack = ExitStack()
    with stack:
        sb = lambda n_, s, d: stack.enter_context(nc.sbuf_tensor("B" + n_, s, d))
        if mksem is None:
            sem = lambda n_: stack.enter_context(nc.semaphore(n_))
        else:
            sem = lambda n_: mksem("B" + n_)

        dtw_sb = sb("dtw_sb", [128, 2, RK], F32)
        dtwT = sb("dtwT", [128, 2, 128], BF16)
        dtb_sb = sb("dtb_sb", [128, 2], F32)
        a_sb = sb("a_sb", [128, 2, NST], F32)
        dp_sb = sb("dp_sb", [128, 2], F32)
        dtl_bf = sb("dtl_bf", [128, TOK], BF16)
        dt_h = sb("dt_h", [128, HB], F32)
        xn_h = sb("xn_h", [128, HB], F32)
        dtx = sb("dtx", [128, HB], BF16)
        brow = [sb(f"brow{i}", [1, HB], BF16) for i in range(2)]
        crow = [sb(f"crow{i}", [1, HB], BF16) for i in range(2)]
        ones1b = sb("ones1b", [1, 128], BF16)
        ab = sb("ab", [128, HB], BF16)
        uu = sb("uu", [128, HB], BF16)
        hh = sb("hh", [128, HB], BF16)
        hc = [sb(f"hc{i}", [128, HB], F32) for i in range(2)]
        sp_t = sb("sp_t", [128, 512], F32)
        rl_t = sb("rl_t", [128, 512], F32)
        yrd = sb("yrd", [128, HB], F32)
        idn = env["idn"] if env is not None else alloc_idents(nc, stack)

        p4a = stack.enter_context(nc.psum_tensor("Bp4a", [128, 2048], F32))
        p4b = stack.enter_context(nc.psum_tensor("Bp4b", [128, 2048], F32))
        pdt = [p4a[0:128, 0:512], p4a[0:128, 512:1024]]
        ptA = p4b[0:128, 0:128]
        ptB = p4b[0:128, 512:640]

        ld = sem("ld"); mm = sem("mm"); ve = sem("ve"); po = sem("po")
        ac = sem("ac"); st = sem("st"); ay = sem("ay"); gld = sem("gld")
        lx = sem("lx"); ly = sem("ly")
        lb2 = [sem("lbA"), sem("lbB")]
        lc2 = [sem("lcA"), sem("lcB")]
        cld = Ctr(); cmm = Ctr(); cve = Ctr(); cpo = Ctr()
        cac = Ctr(); cst = Ctr(); cay = Ctr(); cgld = Ctr()
        clx = Ctr(); cly = Ctr()

        def vop(veg, fn, *waits):
            for (s, v) in waits:
                veg.wait_ge(s, v)
            veg.wait_ge(ve, cve.n)
            fn().then_inc(ve, 1)
            cve.add()

        def aop(sc, fn, *waits):
            for (s, v) in waits:
                sc.wait_ge(s, v)
            sc.wait_ge(ac, cac.n)
            fn().then_inc(ac, 1)
            cac.add()

        with nc.Block() as b:
            @b.sync
            def _(sy):
                sy.dma_start(dtw_sb[:].rearrange("p a r -> p a r"),
                             dtw_sl.rearrange("(a p) r -> p a r", p=128)
                             ).then_inc(ld, 16); cld.add()
                sy.dma_start(dtb_sb[:],
                             dtb_sl.rearrange("(a p) one -> p (a one)", p=128),
                             allow_slow_non_contiguous=True
                             ).then_inc(ld, 16); cld.add()
                sy.dma_start(a_sb[:],
                             alog_sl.rearrange("(a p) n -> p a n", p=128)
                             ).then_inc(ld, 16); cld.add()
                sy.dma_start(dp_sb[:],
                             dp_sl.rearrange("(a p) one -> p (a one)", p=128),
                             allow_slow_non_contiguous=True
                             ).then_inc(ld, 16); cld.add()

            @b.gpsimd
            def _(gp):
                gp.memset(ones1b[:], 1.0).then_inc(po, 1); cpo.add()
                gp.dma_start(dtl_bf[:], dtl_ap()).then_inc(gld, 16)
                cgld.add()
                if env is None:
                    bc_src = bass.AP(tensor=xp_full.tensor,
                                     offset=128 * TOK,
                                     ap=[[TC, 8], [TOK, 32], [1, TC]])
                else:
                    bc_src = bass.AP(tensor=xp_ag.tensor, offset=128 * TC,
                                     ap=[[160 * TC, 8], [TC, 32], [1, TC]])
                gp.dma_start(xpbc_bf[:], bc_src).then_inc(gld, 16)
                cgld.add()
                if env is None:
                    emit_idents(gp, idn, po, cpo)

        ac0 = cac.n
        with nc.Block() as b:
            @b.scalar
            def _(sc):
                sc.wait_ge(ld, 16 * cld.n)
                aop(sc, lambda: sc.activation(a_sb[:], a_sb[:], AF.Exp))

            @b.vector
            def _(veg):
                vop(veg, lambda: veg.tensor_scalar_mul(a_sb[:], a_sb[:], -1.0),
                    (ac, ac0 + 1))
                for dtile in range(2):
                    vop(veg, lambda dtile=dtile: veg.tensor_copy(
                        dtwT[:, dtile, :], [ptA, ptB][dtile][:]),
                        (mm, cmm.n + dtile + 1))

            @b.tensor
            def _(te):
                te.wait_ge(ld, 16 * cld.n)
                for dtile in range(2):
                    te.transpose([ptA, ptB][dtile][:], dtw_sb[:, dtile, :],
                                 idn["ident128"][:]).then_inc(mm, 1)
                    cmm.add()

        for dtile in range(2):
            for bb_i in range(B):
                BGRP_I = dtile * B + bb_i
                t0 = bb_i * HB
                drow0 = dtile * 128
                mm0, ve0, ac0, ld0, st0 = cmm.n, cve.n, cac.n, cld.n, cst.n

                with nc.Block() as b:
                    @b.sync
                    def _(sy):
                        sy.dma_start(xn_h[:], xn_h_ap(dtile, bb_i)
                                     ).then_inc(lx, 16); clx.add()

                    @b.tensor
                    def _(te):
                        te.wait_ge(gld, 16 * cgld.n)
                        for ch in range(8):
                            if ch >= 2:
                                te.wait_ge(ac, ac0 + 3 * (ch - 1))
                                te.wait_ge(ve, ve0 + 2 * (ch - 1))
                            te.matmul(pdt[ch % 2][:],
                                      dtwT[:, dtile, :],
                                      dtl_bf[:, t0 + ch * 512:t0 + (ch + 1) * 512],
                                      start=True, stop=True).then_inc(mm, 1)
                            cmm.add()

                    @b.scalar
                    def _(sc):
                        for ch in range(8):
                            c0 = ch * 512
                            aop(sc, lambda ch=ch, c0=c0: sc.activation(
                                sp_t[:, 0:512], pdt[ch % 2][:], AF.Abs,
                                bias=dtb_sb[:, dtile:dtile + 1]),
                                (mm, mm0 + ch + 1))
                            aop(sc, lambda: sc.activation(
                                sp_t[:, 0:512], sp_t[:, 0:512], AF.Exp,
                                scale=-1.0))
                            aop(sc, lambda c0=c0: sc.activation(
                                dt_h[:, c0:c0 + 512], sp_t[:, 0:512], AF.Ln,
                                bias=idn["ones128"][:]))

                    @b.vector
                    def _(veg):
                        for ch in range(8):
                            c0 = ch * 512
                            vop(veg, lambda ch=ch, c0=c0: veg.tensor_scalar(
                                rl_t[:, 0:512], pdt[ch % 2][:],
                                dtb_sb[:, dtile:dtile + 1], 0.0,
                                ALU.add, ALU.max), (mm, mm0 + ch + 1))
                            vop(veg, lambda c0=c0: veg.tensor_tensor(
                                dt_h[:, c0:c0 + 512], dt_h[:, c0:c0 + 512],
                                rl_t[:, 0:512], ALU.add),
                                (ac, ac0 + 3 * (ch + 1)))

                mm0, ve0, ac0, ld0, st0, ay0 = (cmm.n, cve.n, cac.n, cld.n,
                                                cst.n, cay.n)
                with nc.Block() as b:
                    @b.sync
                    def _(sy):
                        for n in range(B_NST):
                            if n >= 2:
                                sy.wait_ge(mm, mm0 + 16 * (n - 1))
                            sy.dma_start(brow[n % 2][:], bass.AP(
                                tensor=xpbc_bf.tensor,
                                offset=(bb_i * 4 * 32 + n) * TC,
                                ap=[[0, 1], [32 * TC, 4], [1, TC]])
                            ).then_inc(lb2[n % 2], 16)
                            sy.dma_start(crow[n % 2][:], bass.AP(
                                tensor=xpbc_bf.tensor,
                                offset=(bb_i * 4 * 32 + 16 + n) * TC,
                                ap=[[0, 1], [32 * TC, 4], [1, TC]])
                            ).then_inc(lc2[n % 2], 16)

                    @b.tensor
                    def _(te):
                        for n in range(B_NST):
                            te.wait_ge(lb2[n % 2], 16 * (BGRP_I * 8 + n // 2 + 1))
                            if n >= 1:
                                te.wait_ge(ve, ve0 + 1 + 5 * (n - 1) + 5)
                            for half in range(2):
                                tgt = [p4a, p4b][half]
                                for ch in range(4):
                                    te.matmul(tgt[0:128, ch * 512:(ch + 1) * 512],
                                              ones1b[:],
                                              brow[n % 2][0:1,
                                                          (half * 4 + ch) * 512:
                                                          (half * 4 + ch + 1) * 512],
                                              start=True, stop=True
                                              ).then_inc(mm, 1)
                                    cmm.add()
                            te.wait_ge(lc2[n % 2], 16 * (BGRP_I * 8 + n // 2 + 1))
                            te.wait_ge(ve, ve0 + 1 + 5 * n + 2)
                            for half in range(2):
                                tgt = [p4a, p4b][half]
                                for ch in range(4):
                                    te.matmul(tgt[0:128, ch * 512:(ch + 1) * 512],
                                              ones1b[:],
                                              crow[n % 2][0:1,
                                                          (half * 4 + ch) * 512:
                                                          (half * 4 + ch + 1) * 512],
                                              start=True, stop=True
                                              ).then_inc(mm, 1)
                                    cmm.add()

                    @b.vector
                    def _(veg):
                        veg.wait_ge(lx, 16 * clx.n)
                        vop(veg, lambda: veg.tensor_tensor(
                            dtx[:], dt_h[:], xn_h[:], ALU.mult))
                        for n in range(B_NST):
                            vop(veg, lambda: veg.tensor_tensor(
                                uu[:, 0:2048], dtx[:, 0:2048], p4a[:],
                                ALU.mult), (mm, mm0 + 16 * n + 4))
                            vop(veg, lambda: veg.tensor_tensor(
                                uu[:, 2048:HB], dtx[:, 2048:HB], p4b[:],
                                ALU.mult), (mm, mm0 + 16 * n + 8))
                            vop(veg, lambda: veg.tensor_tensor_scan(
                                hh[:], ab[:], uu[:], 0.0, ALU.mult, ALU.add),
                                (ac, ac0 + n + 1))
                            hw1 = [(mm, mm0 + 16 * n + 12),
                                   (ay, 16 * (ay0 + 2 * max(0, n - 1)))]
                            if n == 0:
                                hw1.append((st, 16 * cst.n))
                            vop(veg, lambda n=n: veg.tensor_tensor(
                                hc[n % 2][:, 0:2048], hh[:, 0:2048], p4a[:],
                                ALU.mult), *hw1)
                            vop(veg, lambda n=n: veg.tensor_tensor(
                                hc[n % 2][:, 2048:HB], hh[:, 2048:HB], p4b[:],
                                ALU.mult), (mm, mm0 + 16 * n + 16))

                    @b.scalar
                    def _(sc):
                        for n in range(B_NST):
                            w = []
                            if n > 0:
                                w.append((ve, ve0 + 1 + 5 * (n - 1) + 3))
                            aop(sc, lambda n=n: sc.activation(
                                ab[:], dt_h[:], AF.Exp,
                                scale=a_sb[:, dtile, n:n + 1]), *w)

                    @b.gpsimd
                    def _(gp):
                        gp.wait_ge(gld, 16 * cgld.n)
                        for n in range(B_NST):
                            gp.wait_ge(ve, ve0 + 1 + 5 * n + 5)
                            if n > 0:
                                gp.wait_ge(ay, 16 * (ay0 + 2 * n))
                            op = ALU.add if (n > 0 and B_ACCUM) else ALU.bypass
                            for half in range(2):
                                h0 = half * 2048
                                gp.dma_start(
                                    y_dram[drow0:drow0 + 128,
                                           t0 + h0:t0 + h0 + 2048],
                                    hc[n % 2][:, h0:h0 + 2048],
                                    accum_op=op).then_inc(ay, 16)
                                cay.add()

                ve0, ld0, st0 = cve.n, cld.n, cst.n
                with nc.Block() as b:
                    @b.sync
                    def _(sy):
                        sy.wait_ge(ay, 16 * cay.n)
                        sy.dma_start(yrd[:], y_dram[drow0:drow0 + 128,
                                                    t0:t0 + HB]
                                     ).then_inc(ly, 16); cly.add()

                    @b.vector
                    def _(veg):
                        vop(veg, lambda: veg.scalar_tensor_tensor(
                            hc[0][:], xn_h[:], dp_sb[:, dtile:dtile + 1],
                            yrd[:], ALU.mult, ALU.add),
                            (ly, 16 * cly.n))

                    @b.gpsimd
                    def _(gp):
                        gp.wait_ge(ve, cve.n)
                        if env is None:
                            gp.dma_start(attn0_out[drow0:drow0 + 128,
                                                   t0:t0 + HB],
                                         hc[0][:]).then_inc(st, 16)
                            cst.add()
                        else:
                            for q in range(4):
                                r0 = (bb_i * 4 + q) * 256 + dtile * 128
                                gp.dma_start(
                                    attn0_stage[r0:r0 + 128, :],
                                    hc[0][:, q * TC:(q + 1) * TC]
                                ).then_inc(st, 16)
                                cst.add()

        with nc.Block() as b:
            @b.sync
            def _(sy):
                sy.wait_ge(st, 16 * cst.n)

    return nc



GRP = 4
NGRP = (FF // 128) // GRP
C_NGRP = NGRP
C_NW = 4


def build_launch_C(env=None):
    if env is not None:
        nc = env["nc"]
        xa = env["xa"]
        attn0_t = env["attn0_t"]
        wg8 = env["wg8"]; wu8 = env["wu8"]; wd8 = env["wd8"]
        sg_l = env["sg_l"]; su_l = env["su_l"]; sd_l = env["sd_l"]
        wq = env["wq"]; wk = env["wk"]; wv = env["wv"]; wgt = env["wgt"]
        bq = env["bq"]; bk = env["bk"]; bv = env["bv"]; bg = env["bg"]
        x2_out = env["x2_out"]; q_out = env["q_out"]; k_out = env["k_out"]
        v_out = env["v_out"]; gate_out = env["gate_out"]
        stats_d = env["stats_d"]
        mksem = env["mksem"]
        return _launch_C_body(nc, env, mksem, xa, attn0_t, wg8, wu8, wd8,
                              sg_l, su_l, sd_l, wq, wk, wv, wgt, bq, bk, bv,
                              bg, x2_out, q_out, k_out, v_out, gate_out,
                              stats_d)
    nc = bass.Bass("TRN2", target_bir_lowering=False)
    xa = nc.dram_tensor("xa", [TC, D], F32, kind="ExternalInput").ap()
    attn0_t = nc.dram_tensor("attn0_t", [D, TC], F32, kind="ExternalInput").ap()
    wg8 = nc.dram_tensor("wg8", [D, FF], I8, kind="ExternalInput").ap()
    wu8 = nc.dram_tensor("wu8", [D, FF], I8, kind="ExternalInput").ap()
    wd8 = nc.dram_tensor("wd8", [FF, D], I8, kind="ExternalInput").ap()
    sg_l = nc.dram_tensor("sg_l", [128, 64], F32, kind="ExternalInput").ap()
    su_l = nc.dram_tensor("su_l", [128, 64], F32, kind="ExternalInput").ap()
    sd_l = nc.dram_tensor("sd_l", [128, 16], F32, kind="ExternalInput").ap()
    wq = nc.dram_tensor("wq", [D, D], F32, kind="ExternalInput").ap()
    wk = nc.dram_tensor("wk", [D, D], F32, kind="ExternalInput").ap()
    wv = nc.dram_tensor("wv", [D, D], F32, kind="ExternalInput").ap()
    wgt = nc.dram_tensor("wgt", [D, D], F32, kind="ExternalInput").ap()
    bq = nc.dram_tensor("bq", [128, 16], F32, kind="ExternalInput").ap()
    bk = nc.dram_tensor("bk", [128, 16], F32, kind="ExternalInput").ap()
    bv = nc.dram_tensor("bv", [128, 16], F32, kind="ExternalInput").ap()
    bg = nc.dram_tensor("bg", [128, 16], F32, kind="ExternalInput").ap()
    x2_out = nc.dram_tensor("x2_out", [D, TC], F32, kind="ExternalOutput").ap()
    q_out = nc.dram_tensor("q_out", [D, TC], F32, kind="ExternalOutput").ap()
    k_out = nc.dram_tensor("k_out", [D, TC], F32, kind="ExternalOutput").ap()
    v_out = nc.dram_tensor("v_out", [D, TC], F32, kind="ExternalOutput").ap()
    gate_out = nc.dram_tensor("gate_out", [D, TC], F32, kind="ExternalOutput").ap()
    stats_d = nc.dram_tensor("stats_d", [2, TC], F32).ap()
    return _launch_C_body(nc, None, None, xa, attn0_t, wg8, wu8, wd8, sg_l,
                          su_l, sd_l, wq, wk, wv, wgt, bq, bk, bv, bg,
                          x2_out, q_out, k_out, v_out, gate_out, stats_d)


def _launch_C_body(nc, env, mksem, xa, attn0_t, wg8, wu8, wd8, sg_l, su_l,
                   sd_l, wq, wk, wv, wgt, bq, bk, bv, bg, x2_out, q_out,
                   k_out, v_out, gate_out, stats_d):
    SCALING = float(HD) ** -0.5

    stack = ExitStack()
    with stack:
        sb = lambda n_, s, d: stack.enter_context(nc.sbuf_tensor("C" + n_, s, d))
        if mksem is None:
            sem = lambda n_: stack.enter_context(nc.semaphore(n_))
        else:
            sem = lambda n_: mksem("C" + n_)
        _psn = [0]

        def ps(s):
            _psn[0] += 1
            return stack.enter_context(nc.psum_tensor(f"Cps{_psn[0]}", s, F32))

        x1 = [sb(f"x1_{i}", [128, TC], F32) for i in range(NDT)]
        xnbf = [sb(f"xnbf{i}", [128, TC], BF16) for i in range(NDT)]
        sq = [sb("sq0", [128, TC], F32)]
        tscr = sb("tscr", [128, TC], F32)
        st_s = sb("st_s", [1, TC], F32)
        st_q = sb("st_q", [1, TC], F32)
        r_rstd = sb("r_rstd", [1, TC], F32)
        r_mrstd = sb("r_mrstd", [1, TC], F32)
        r_var = sb("r_var", [1, TC], F32)
        rstd_b = sb("rstd_b", [128, TC], F32)
        mrstd_b = sb("mrstd_b", [128, TC], F32)
        sg_sb = sb("sg_sb", [128, 64], F32)
        su_sb = sb("su_sb", [128, 64], F32)
        sd_sb = sb("sd_sb", [128, 16], F32)
        silu_t = sb("silu_t", [128, 512], BF16)
        up_t = sb("up_t", [128, 512], BF16)
        fuse = sb("fuse", [128, GRP, TC], BF16)
        bqs = sb("bqs", [128, 16], F32)
        bks = sb("bks", [128, 16], F32)
        bvs = sb("bvs", [128, 16], F32)
        bgs = sb("bgs", [128, 16], F32)
        ost = [sb(f"ost{i}", [128, 512], F32) for i in range(2)]
        idn = env["idn"] if env is not None else alloc_idents(nc, stack)

        ld = sem("ld"); mm = sem("mm"); ve = sem("ve"); po = sem("po")
        ac = sem("ac"); st = sem("st"); gld = sem("gld"); la = sem("la")
        lx2 = [sem("lxA"), sem("lxB")]
        la2 = [sem("laA"), sem("laB")]
        st2 = [sem("stA"), sem("stB")]
        cst2 = [Ctr(), Ctr()]
        cld = Ctr(); cmm = Ctr(); cve = Ctr(); cpo = Ctr()
        cac = Ctr(); cst = Ctr(); cgld = Ctr(); cla = Ctr(); clxa = Ctr()

        def vop(veg, fn, *waits):
            for (s, v) in waits:
                veg.wait_ge(s, v)
            veg.wait_ge(ve, cve.n)
            fn().then_inc(ve, 1)
            cve.add()

        def aop(sc, fn, *waits):
            for (s, v) in waits:
                sc.wait_ge(s, v)
            sc.wait_ge(ac, cac.n)
            fn().then_inc(ac, 1)
            cac.add()

        with nc.Block() as b:
            @b.sync
            def _(sy):
                sy.dma_start(sg_sb[:], sg_l[:]).then_inc(ld, 16); cld.add()
                sy.dma_start(su_sb[:], su_l[:]).then_inc(ld, 16); cld.add()
                sy.dma_start(sd_sb[:], sd_l[:]).then_inc(ld, 16); cld.add()
                sy.dma_start(bqs[:], bq[:]).then_inc(ld, 16); cld.add()
                sy.dma_start(bks[:], bk[:]).then_inc(ld, 16); cld.add()
                sy.dma_start(bvs[:], bv[:]).then_inc(ld, 16); cld.add()
                sy.dma_start(bgs[:], bg[:]).then_inc(ld, 16); cld.add()

            @b.gpsimd
            def _(gp):
                if env is None:
                    emit_idents(gp, idn, po, cpo)

        tmp_stack = ExitStack()
        xtm = [tmp_stack.enter_context(nc.sbuf_tensor(f"Cxtm{i}", [128, D], F32))
               for i in range(2)]
        a0t = [tmp_stack.enter_context(nc.sbuf_tensor(f"Ca0t{i}", [128, TC], F32))
               for i in range(2)]
        ptr = [tmp_stack.enter_context(nc.psum_tensor(f"Cptr{i}", [128, 128], F32))
               for i in range(4)]
        ld0, mm0, ve0, la0 = cld.n, cmm.n, cve.n, cla.n
        with nc.Block() as b:
            @b.sync
            def _(sy):
                for tt in range(8):
                    if tt >= 2:
                        sy.wait_ge(mm, mm0 + (tt - 1) * NDT)
                    sy.dma_start(xtm[tt % 2][:], xa[tt * 128:(tt + 1) * 128, :]
                                 ).then_inc(lx2[tt % 2], 16)

            @b.tensor
            def _(te):
                for tt in range(8):
                    te.wait_ge(lx2[tt % 2], 16 * (tt // 2 + 1))
                    for dt in range(NDT):
                        k = tt * NDT + dt
                        if k >= 4:
                            te.wait_ge(ve, ve0 + k - 3)
                        te.transpose(ptr[k % 4][:],
                                     xtm[tt % 2][:, dt * 128:(dt + 1) * 128],
                                     idn["ident128"][:]).then_inc(mm, 1)
                        cmm.add()

            @b.vector
            def _(veg):
                for tt in range(8):
                    for dt in range(NDT):
                        k = tt * NDT + dt
                        vop(veg, lambda dt=dt, tt=tt, k=k: veg.tensor_copy(
                            x1[dt][:, tt * 128:(tt + 1) * 128], ptr[k % 4][:]),
                            (mm, mm0 + k + 1))

        ve0, la0 = cve.n, cla.n
        with nc.Block() as b:
            @b.scalar
            def _(sc):
                for dt in range(NDT):
                    if dt >= 2:
                        sc.wait_ge(ve, ve0 + (dt - 1))
                    sc.dma_start(a0t[dt % 2][:],
                                 attn0_t[dt * 128:(dt + 1) * 128, :]
                                 ).then_inc(la2[dt % 2], 16)

            @b.vector
            def _(veg):
                for dt in range(NDT):
                    vop(veg, lambda dt=dt: veg.tensor_tensor(
                        x1[dt][:], x1[dt][:], a0t[dt % 2][:], ALU.add),
                        (la2[dt % 2], 16 * (dt // 2 + 1)))

        tmp_stack_pending = tmp_stack

        def emit_ln(src_tiles, dst_bf):
            pstat_stack = ExitStack()
            pstat = pstat_stack.enter_context(
                nc.psum_tensor(f"pstat{_psn[0]}", [1, TC], F32))
            _psn[0] += 1
            pstatq = pstat_stack.enter_context(
                nc.psum_tensor(f"pstatq{_psn[0]}", [1, TC], F32))
            _psn[0] += 1
            ac0, mm0, ve0, ld0 = cac.n, cmm.n, cve.n, cld.n
            with nc.Block() as b:
                @b.scalar
                def _(sc):
                    for dt in range(NDT):
                        if dt >= 1:
                            sc.wait_ge(mm, mm0 + dt)
                        aop(sc, lambda dt=dt: sc.activation(
                            sq[0][:], src_tiles[dt][:], AF.Square))

                @b.tensor
                def _(te):
                    for dt in range(NDT):
                        te.wait_ge(ac, ac0 + dt + 1)
                        for tchunk in range(2):
                            t0 = tchunk * 512
                            te.matmul(pstat[:, t0:t0 + 512], idn["ones128"][:],
                                      src_tiles[dt][:, t0:t0 + 512],
                                      start=(dt == 0), stop=(dt == NDT - 1))
                            last = te.matmul(pstatq[:, t0:t0 + 512],
                                             idn["ones128"][:],
                                             sq[0][:, t0:t0 + 512],
                                             start=(dt == 0),
                                             stop=(dt == NDT - 1))
                        last.then_inc(mm, 1); cmm.add()

            ve0, ac0 = cve.n, cac.n
            with nc.Block() as b:
                @b.vector
                def _(veg):
                    vop(veg, lambda: veg.tensor_scalar_mul(st_s[:], pstat[:],
                                                           1.0 / D))
                    vop(veg, lambda: veg.tensor_scalar_mul(st_q[:], pstatq[:],
                                                           1.0 / D))
                    vop(veg, lambda: veg.tensor_tensor(r_var[:], st_s[:],
                                                       st_s[:], ALU.mult))
                    vop(veg, lambda: veg.tensor_tensor(r_var[:], st_q[:],
                                                       r_var[:], ALU.subtract))

                @b.scalar
                def _(sc):
                    aop(sc, lambda: sc.activation(r_rstd[:], r_var[:], AF.Ln,
                                                  bias=idn["eps1"][:]),
                        (ve, ve0 + 4))
                    aop(sc, lambda: sc.activation(r_rstd[:], r_rstd[:], AF.Exp,
                                                  scale=-0.5))

            ve0, ld0 = cve.n, cld.n
            with nc.Block() as b:
                @b.vector
                def _(veg):
                    vop(veg, lambda: veg.tensor_tensor(r_mrstd[:], st_s[:],
                                                       r_rstd[:], ALU.mult))

                @b.sync
                def _(sy):
                    sy.wait_ge(ve, ve0 + 1)
                    sy.dma_start(stats_d[0:1, :], r_rstd[:]).then_inc(ld, 16)
                    cld.add()
                    sy.dma_start(stats_d[1:2, :], r_mrstd[:]).then_inc(ld, 16)
                    cld.add()
                    sy.wait_ge(ld, 16 * cld.n)
                    sy.dma_start(rstd_b[:], dram_bcast_ap(stats_d, 0, 0, TC)
                                 ).then_inc(ld, 16); cld.add()
                    sy.dma_start(mrstd_b[:], dram_bcast_ap(stats_d, 1, 0, TC)
                                 ).then_inc(ld, 16); cld.add()

            ve0 = cve.n
            with nc.Block() as b:
                @b.vector
                def _(veg):
                    veg.wait_ge(ld, 16 * cld.n)
                    for dt in range(NDT):
                        vop(veg, lambda dt=dt: veg.tensor_tensor(
                            tscr[:], src_tiles[dt][:], rstd_b[:], ALU.mult))
                        vop(veg, lambda dt=dt: veg.tensor_tensor(
                            dst_bf[dt][:], tscr[:], mrstd_b[:], ALU.subtract))
            pstat_stack.close()

        tmp_stack_pending.close()
        ffnw_stack = ExitStack()
        wgbf = ffnw_stack.enter_context(
            nc.sbuf_tensor("Cwgbf", [128, NDT, GRP * 128], BF16))
        wubf = ffnw_stack.enter_context(
            nc.sbuf_tensor("Cwubf", [128, NDT, GRP * 128], BF16))
        wdbf = ffnw_stack.enter_context(
            nc.sbuf_tensor("Cwdbf", [128, GRP, D // 2], BF16))
        emit_ln(x1, xnbf)

        ffn_psum_stack = ExitStack()
        def ps2(s):
            _psn[0] += 1
            return ffn_psum_stack.enter_context(
                nc.psum_tensor(f"Cps{_psn[0]}", s, F32))
        pga = [ps2([128, 512]) for i in range(2)]
        pgb = [ps2([128, 512]) for i in range(2)]
        pdn = [ps2([128, 512]) for i in range(2)]
        gfw = sem("gfw")
        gdw = sem("gdw")

        def wgwu_load(gp, g):
            f0_ = g * GRP * 128
            for (w8, dst) in ((wg8, wgbf), (wu8, wubf)):
                gp.dma_start(dst[:], bass.AP(
                    tensor=w8.tensor, offset=f0_,
                    ap=[[FF, 128], [FF * 128, NDT], [1, GRP * 128]])
                ).then_inc(gfw, 16)

        with nc.Block() as b:
            @b.gpsimd
            def _(gp):
                wgwu_load(gp, 0)

        mmGU = {}
        mmDN = {}
        veGU = {}
        veDN = {}
        acGU = {}
        mm0F, ve0F, ac0F = cmm.n, cve.n, cac.n

        for g in range(C_NGRP):
            base = mm0F + g * (GRP * 2 + 32)
            mmGU[g] = base + GRP * 2
            mmDN[(g, 0)] = mmGU[g] + 16
            mmDN[(g, 1)] = mmGU[g] + 32
            veGU[g] = ve0F + g * (GRP * 2 * 2 + 32) + GRP * 2 * 2
            veDN[g] = veGU[g] + 32
            acGU[g] = ac0F + (g + 1) * GRP * 2

        with nc.Block() as b:
            @b.gpsimd
            def _(gp):
                for g in range(C_NGRP):
                    f0 = g * GRP * 128
                    if g > 0:
                        gp.wait_ge(mm, mmDN[(g - 1, 1)])
                    gp.dma_start(wdbf[:], bass.AP(
                        tensor=wd8.tensor, offset=f0 * D,
                        ap=[[D, 128], [D * 128, GRP], [1, D // 2]])
                    ).then_inc(gdw, 16)
                    if g + 1 < C_NGRP:
                        gp.wait_ge(mm, mmGU[g])
                        wgwu_load(gp, g + 1)
                    gp.wait_ge(mm, mmDN[(g, 0)])
                    gp.dma_start(wdbf[:], bass.AP(
                        tensor=wd8.tensor, offset=f0 * D + D // 2,
                        ap=[[D, 128], [D * 128, GRP], [1, D // 2]])
                    ).then_inc(gdw, 16)

            @b.tensor
            def _(te):
                for g in range(C_NGRP):
                    te.wait_ge(gfw, 16 * 2 * (g + 1))
                    for fr in range(GRP):
                        for tchunk in range(2):
                            t0 = tchunk * 512
                            k = fr * 2 + tchunk
                            if k >= 2:
                                te.wait_ge(ac, ac0F + g * GRP * 2 + k - 1)
                                te.wait_ge(ve, ve0F + g * (GRP * 4 + 32)
                                           + 2 * (k - 2) + 1)
                            elif g > 0:
                                te.wait_ge(ac, acGU[g - 1])
                                te.wait_ge(ve, veGU[g - 1])
                            for dt in range(NDT):
                                te.matmul(pga[tchunk][:],
                                          wgbf[:, dt, fr * 128:(fr + 1) * 128],
                                          xnbf[dt][:, t0:t0 + 512],
                                          start=(dt == 0), stop=(dt == NDT - 1))
                                last = te.matmul(
                                    pgb[tchunk][:],
                                    wubf[:, dt, fr * 128:(fr + 1) * 128],
                                    xnbf[dt][:, t0:t0 + 512],
                                    start=(dt == 0), stop=(dt == NDT - 1))
                            last.then_inc(mm, 1); cmm.add()
                    for dh in range(2):
                        te.wait_ge(gdw, 16 * (2 * g + dh + 1))
                        te.wait_ge(ve, veGU[g] + dh * 16)
                        for odt in range(NDT // 2):
                            for tchunk in range(2):
                                t0 = tchunk * 512
                                k = odt * 2 + tchunk
                                if k >= 2:
                                    te.wait_ge(ve, veGU[g] + dh * 16 + k - 1)
                                for fr in range(GRP):
                                    last = te.matmul(
                                        pdn[k % 2][:],
                                        wdbf[:, fr, odt * 128:(odt + 1) * 128],
                                        fuse[:, fr, t0:t0 + 512],
                                        start=(fr == 0), stop=(fr == GRP - 1))
                                last.then_inc(mm, 1); cmm.add()

            @b.scalar
            def _(sc):
                for g in range(C_NGRP):
                    for fr in range(GRP):
                        gf = g * GRP + fr
                        for tchunk in range(2):
                            k = fr * 2 + tchunk
                            aop(sc, lambda gf=gf, tchunk=tchunk:
                                sc.activation(silu_t[:], pga[tchunk][:],
                                              AF.Silu,
                                              scale=sg_sb[:, gf:gf + 1]),
                                (mm, mm0F + g * (GRP * 2 + 32) + k + 1),
                                (ve, ve0F + g * (GRP * 4 + 32) + 2 * k))

            @b.vector
            def _(veg):
                for g in range(C_NGRP):
                    mgu = mm0F + g * (GRP * 2 + 32)
                    for fr in range(GRP):
                        gf = g * GRP + fr
                        for tchunk in range(2):
                            t0 = tchunk * 512
                            k = fr * 2 + tchunk
                            waits = [(mm, mgu + k + 1)]
                            if k == 0 and g > 0:
                                waits.append((mm, mmDN[(g - 1, 1)]))
                            vop(veg, lambda gf=gf, tchunk=tchunk:
                                veg.tensor_scalar(up_t[:], pgb[tchunk][:],
                                                  su_sb[:, gf:gf + 1], None,
                                                  ALU.mult), *waits)
                            vop(veg, lambda fr=fr, t0=t0: veg.tensor_tensor(
                                fuse[:, fr, t0:t0 + 512], silu_t[:], up_t[:],
                                ALU.mult),
                                (ac, ac0F + g * GRP * 2 + k + 1))
                    for dh in range(2):
                        for odt in range(NDT // 2):
                            godt = dh * (NDT // 2) + odt
                            for tchunk in range(2):
                                t0 = tchunk * 512
                                k = odt * 2 + tchunk
                                vop(veg, lambda godt=godt, t0=t0, k=k:
                                    veg.scalar_tensor_tensor(
                                        x1[godt][:, t0:t0 + 512], pdn[k % 2][:],
                                        sd_sb[:, godt:godt + 1],
                                        x1[godt][:, t0:t0 + 512],
                                        ALU.mult, ALU.add),
                                    (mm, mmGU[g] + dh * 16 + k + 1))

        ffn_psum_stack.close()
        ffnw_stack.close()
        emit_ln(x1, xnbf)

        pqk = [ps([128, 512]) for i in range(2)]
        wbig2 = [sb(f"wbig{i}", [128, NDT, 512], BF16) for i in range(2)]
        qw2 = [sem("qwA"), sem("qwB")]
        wseq = [(wi_, h_) for wi_ in range(C_NW) for h_ in range(4)]
        wsrcs = [wq, wk, wv, wgt]

        def wbig_load(gp, s):
            wi_, h_ = wseq[s]
            gp.dma_start(wbig2[s % 2][:], bass.AP(
                tensor=wsrcs[wi_].tensor, offset=h_ * 512,
                ap=[[D, 128], [D * 128, NDT], [1, 512]])
            ).then_inc(qw2[s % 2], 16)

        if wseq:
            with nc.Block() as b:
                @b.gpsimd
                def _(gp):
                    wbig_load(gp, 0)

        for wi, (wsrc, bias_sb, dst, mode) in enumerate((
                (wq, bqs, q_out, "q"), (wk, bks, k_out, "k"),
                (wv, bvs, v_out, "k"), (wgt, bgs, gate_out, "g"))[:C_NW]):
            for half in range(4):
                c0 = half * 512
                s = wi * 4 + half
                wbig = wbig2[s % 2]
                gld0, mm0, ve0, ac0 = cgld.n, cmm.n, cve.n, cac.n
                sb0 = [cst2[0].n, cst2[1].n]
                with nc.Block() as b:
                    @b.gpsimd
                    def _(gp):
                        if s + 1 < len(wseq):
                            wbig_load(gp, s + 1)

                    @b.tensor
                    def _(te):
                        te.wait_ge(qw2[s % 2], 16 * (s // 2 + 1))
                        for ft in range(4):
                            for tchunk in range(2):
                                t0 = tchunk * 512
                                k = ft * 2 + tchunk
                                if k >= 2:
                                    if mode == "g":
                                        te.wait_ge(ac, ac0 + k - 1)
                                    else:
                                        te.wait_ge(ve, ve0 + k - 1)
                                for dt in range(NDT):
                                    last = te.matmul(
                                        pqk[k % 2][:],
                                        wbig[:, dt, ft * 128:(ft + 1) * 128],
                                        xnbf[dt][:, t0:t0 + 512],
                                        start=(dt == 0), stop=(dt == NDT - 1))
                                last.then_inc(mm, 1); cmm.add()

                    if mode != "g":
                        @b.vector
                        def _(veg):
                            for ft in range(4):
                                fcol = (c0 + ft * 128) // 128
                                for tchunk in range(2):
                                    t0 = tchunk * 512
                                    k = ft * 2 + tchunk
                                    if mode == "q":
                                        vop(veg, lambda fcol=fcol, k=k:
                                            veg.tensor_scalar(
                                                ost[k % 2][:], pqk[k % 2][:],
                                                bias_sb[:, fcol:fcol + 1],
                                                SCALING, ALU.add, ALU.mult),
                                            (mm, mm0 + k + 1),
                                            (st2[k % 2],
                                             16 * (sb0[k % 2] + k // 2)))
                                    else:
                                        vop(veg, lambda fcol=fcol, k=k:
                                            veg.tensor_scalar(
                                                ost[k % 2][:], pqk[k % 2][:],
                                                bias_sb[:, fcol:fcol + 1],
                                                None, ALU.add),
                                            (mm, mm0 + k + 1),
                                            (st2[k % 2],
                                             16 * (sb0[k % 2] + k // 2)))
                    else:
                        @b.scalar
                        def _(sc):
                            for ft in range(4):
                                fcol = (c0 + ft * 128) // 128
                                for tchunk in range(2):
                                    k = ft * 2 + tchunk
                                    aop(sc, lambda fcol=fcol, k=k:
                                        sc.activation(
                                            ost[k % 2][:], pqk[k % 2][:],
                                            AF.Silu,
                                            bias=bias_sb[:, fcol:fcol + 1]),
                                        (mm, mm0 + k + 1),
                                        (st2[k % 2],
                                         16 * (sb0[k % 2] + k // 2)))

                    @b.sync
                    def _(sy):
                        for ft in range(4):
                            for tchunk in range(2):
                                t0 = tchunk * 512
                                k = ft * 2 + tchunk
                                if mode == "g":
                                    sy.wait_ge(ac, ac0 + k + 1)
                                else:
                                    sy.wait_ge(ve, ve0 + k + 1)
                                sy.dma_start(
                                    dst[c0 + ft * 128:c0 + (ft + 1) * 128,
                                        t0:t0 + 512],
                                    ost[k % 2][:]).then_inc(st2[k % 2], 16)
                                cst2[k % 2].add()

        st0 = cst.n
        with nc.Block() as b:
            @b.sync
            def _(sy):
                for dt in range(NDT):
                    sy.dma_start(x2_out[dt * 128:(dt + 1) * 128, :],
                                 x1[dt][:]).then_inc(st, 16)
                    cst.add()
                sy.wait_ge(st, 16 * cst.n)
                sy.wait_ge(st2[0], 16 * cst2[0].n)
                sy.wait_ge(st2[1], 16 * cst2[1].n)

    return nc



DW = 4992


def build_launch_D(env=None):
    if env is None:
        nc = bass.Bass("TRN2", target_bir_lowering=False)
        qT = nc.dram_tensor("qT", [4, 128, L], F32, kind="ExternalInput").ap()
        kT = nc.dram_tensor("kT", [4, 128, L], F32, kind="ExternalInput").ap()
        vtm = nc.dram_tensor("vtm", [4, L, HD], F32, kind="ExternalInput").ap()
        decay_d = nc.dram_tensor("decay_d", [1, 2], F32,
                                 kind="ExternalInput").ap()
        gnw_d = nc.dram_tensor("gnw_d", [128, 2], F32, kind="ExternalInput").ap()
        gnb_d = nc.dram_tensor("gnb_d", [128, 2], F32, kind="ExternalInput").ap()
        g_out = nc.dram_tensor("g_out", [4, 128, L], F32,
                               kind="ExternalOutput").ap()
        scr = nc.dram_tensor("scr", [4, L], F32).ap()
        mksem = None
        q_a2a = k_a2a = v_a2a = g_stage = None
    else:
        nc = env["nc"]
        decay_d = env["decay_d"]; gnw_d = env["gnw_d"]; gnb_d = env["gnb_d"]
        q_a2a = env["q_a2a"]; k_a2a = env["k_a2a"]; v_a2a = env["v_a2a"]
        g_stage = env["g_stage"]
        scr = env["scr"]
        mksem = env["mksem"]
        qT = kT = vtm = g_out = None

    def pair_ap(src_a2a, pair):
        b, hloc = pair % 2, pair // 2
        return bass.AP(tensor=src_a2a.tensor,
                       offset=(b * 4 * 256 + hloc * 128) * TC,
                       ap=[[TC, 128], [256 * TC, 4], [1, TC]])

    stack = ExitStack()
    with stack:
        sb = lambda n_, s, d: stack.enter_context(nc.sbuf_tensor("D" + n_, s, d))
        if mksem is None:
            sem = lambda n_: stack.enter_context(nc.semaphore(n_))
        else:
            sem = lambda n_: mksem("D" + n_)
        _psn = [0]

        def ps(shape):
            _psn[0] += 1
            return stack.enter_context(nc.psum_tensor(f"Dps{_psn[0]}", shape, F32))

        dfull = [sb(f"dfull{i}", [128, DW], F32) for i in range(2)]
        iot = sb("iot", [128, DW], F32)
        qp = sb("qp", [128, L], F32)
        kp = sb("kp", [128, L], F32)
        qs_bf = sb("qs_bf", [128, L], BF16)
        kbf = sb("kbf", [128, L], BF16)
        vt_bf = sb("vt_bf", [128, 32, HD], BF16)
        ww = sb("ww", [128, L], F32)
        dect = sb("dect", [128, L], F32)
        dcol = sb("dcol", [128, 1], F32)
        ed = sb("ed", [128, 1], F32)
        rsum = sb("rsum", [1, L], F32)
        rb = sb("rb", [128, L], F32)
        vfm = rb
        retbf = [sb(f"retbf{i}", [128, 512], BF16) for i in range(2)]
        at_f = sb("at_f", [128, 512], F32)
        sq2 = sb("sq2", [128, 512], F32)
        s_m = sb("s_m", [1, 512], F32)
        s_q = sb("s_q", [1, 512], F32)
        s_v = sb("s_v", [1, 512], F32)
        mrb = sb("mrb", [128, 512], F32)
        rrb = sb("rrb", [128, 512], F32)
        gt = [sb(f"gt{i}", [128, 512], F32) for i in range(2)]
        gnw_sb = sb("gnw_sb", [128, 2], F32)
        gnb_sb = sb("gnb_sb", [128, 2], F32)
        idn = env["idn"] if env is not None else alloc_idents(nc, stack)

        psA = [ps([128, 512]) for _ in range(2)]
        psB = [ps([128, 512]) for _ in range(2)]
        psC = ps([1, 512])
        psD = ps([1, 512])
        pbc1 = ps([128, 512])
        pbc2 = ps([128, 512])
        acT = {}
        veT = {}

        ld = sem("ld"); mm = sem("mm"); ve = sem("ve"); po = sem("po")
        ac = sem("ac"); gld = sem("gld"); ly = sem("ly")
        st2 = [sem("stA"), sem("stB")]
        cst2 = [Ctr(), Ctr()]
        cld = Ctr(); cmm = Ctr(); cve = Ctr(); cpo = Ctr()
        cac = Ctr(); cgld = Ctr(); cly = Ctr()

        def vop(veg, fn, *waits):
            for (s, v) in waits:
                veg.wait_ge(s, v)
            veg.wait_ge(ve, cve.n)
            fn().then_inc(ve, 1)
            cve.add()

        def aop(sc, fn, *waits):
            for (s, v) in waits:
                sc.wait_ge(s, v)
            sc.wait_ge(ac, cac.n)
            fn().then_inc(ac, 1)
            cac.add()

        with nc.Block() as b:
            @b.gpsimd
            def _(gp):
                if env is None:
                    emit_idents(gp, idn, po, cpo)
                gp.iota(iot[:], pattern=[[1, DW]], base=-384,
                        channel_multiplier=-1,
                        allow_small_or_imprecise_dtypes=True
                        ).then_inc(po, 1); cpo.add()

            @b.sync
            def _(sy):
                sy.dma_start(gnw_sb[:], gnw_d[:]).then_inc(ld, 16); cld.add()
                sy.dma_start(gnb_sb[:], gnb_d[:]).then_inc(ld, 16); cld.add()

        for pair in range(4):
            hloc = pair // 2
            newhead = (pair % 2 == 0)

            ld0, ve0, ac0, gld0 = cld.n, cve.n, cac.n, cgld.n
            with nc.Block() as b:
                @b.sync
                def _(sy):
                    if env is None:
                        sy.dma_start(qp[:], qT[pair]).then_inc(ld, 16)
                        cld.add()
                        sy.dma_start(kp[:], kT[pair]).then_inc(ld, 16)
                        cld.add()
                    else:
                        sy.dma_start(qp[:], pair_ap(q_a2a, pair)
                                     ).then_inc(ld, 16); cld.add()
                        sy.dma_start(kp[:], pair_ap(k_a2a, pair)
                                     ).then_inc(ld, 16); cld.add()
                        sy.dma_start(vfm[:], pair_ap(v_a2a, pair)
                                     ).then_inc(ld, 16); cld.add()
                    if newhead:
                        sy.dma_start(dcol[:], dram_bcast_ap(decay_d, 0, hloc, 1)
                                     ).then_inc(ld, 16); cld.add()

                @b.gpsimd
                def _(gp):
                    if env is None:
                        src = bass.AP(tensor=vtm.tensor, offset=pair * L * HD,
                                      ap=[[HD, 128], [HD * 128, 32], [1, HD]])
                        gp.dma_start(vt_bf[:], src).then_inc(gld, 16)
                        cgld.add()
                        src2 = bass.AP(tensor=kT.tensor, offset=pair * 128 * L,
                                       ap=[[L, 128], [1, L]])
                        gp.dma_start(kbf[:], src2).then_inc(gld, 16)
                        cgld.add()
                    else:
                        gp.dma_start(kbf[:], pair_ap(k_a2a, pair)
                                     ).then_inc(gld, 16); cgld.add()

                @b.vector
                def _(veg):
                    if newhead:
                        vop(veg, lambda: veg.tensor_scalar(
                            dfull[hloc][:], iot[:], dcol[:], None, ALU.mult),
                            (ld, 16 * cld.n))
                    vop(veg, lambda: veg.tensor_tensor_scan(
                        ww[:], dect[:], kp[:], 0.0, ALU.mult, ALU.add),
                        (ac, ac0 + 2))
                    vop(veg, lambda: veg.tensor_tensor(
                        ww[:], qp[:], ww[:], ALU.mult))

                @b.scalar
                def _(sc):
                    if newhead:
                        aop(sc, lambda: sc.activation(ed[:], dcol[:], AF.Exp),
                            (ld, 16 * cld.n))
                    else:
                        aop(sc, lambda: sc.activation(ed[:], dcol[:], AF.Exp))
                    aop(sc, lambda: sc.activation(dect[:], qp[:], AF.Abs,
                                                  bias=ed[:], scale=0.0),
                        (ld, 16 * cld.n))

            if env is not None:
                mmv0 = cmm.n
                vev0 = cve.n
                with nc.Block() as b:
                    @b.tensor
                    def _(te):
                        te.wait_ge(ld, 16 * cld.n)
                        for sj in range(32):
                            if sj >= 2:
                                te.wait_ge(ve, vev0 + sj - 1)
                            te.transpose(
                                psA[sj % 2][0:128, 0:128],
                                vfm[:, sj * 128:(sj + 1) * 128],
                                idn["ident128"][:]).then_inc(mm, 1)
                            cmm.add()

                    @b.vector
                    def _(veg):
                        for sj in range(32):
                            vop(veg, lambda sj=sj: veg.tensor_copy(
                                vt_bf[:, sj, :], psA[sj % 2][0:128, 0:128]),
                                (mm, mmv0 + sj + 1))

            if newhead:
                ve0, po0 = cve.n, cpo.n
                with nc.Block() as b:
                    @b.gpsimd
                    def _(gp):
                        gp.affine_select(
                            dfull[hloc][:], dfull[hloc][:],
                            pattern=[[1, DW]], compare_op=ALU.is_ge,
                            fill=-1e38, base=-384, channel_multiplier=-1
                        ).then_inc(po, 1); cpo.add()

                    @b.scalar
                    def _(sc):
                        aop(sc, lambda: sc.activation(
                            dfull[hloc][:], dfull[hloc][:], AF.Exp),
                            (po, po0 + 1))

            mm0, ve0, ac0, ly0 = cmm.n, cve.n, cac.n, cly.n
            with nc.Block() as b:
                @b.tensor
                def _(te):
                    for ch in range(8):
                        if ch >= 2:
                            te.wait_ge(ve, ve0 + ch - 1)
                        te.matmul(psC[:] if ch % 2 == 0 else psD[:],
                                  idn["ones128"][:],
                                  ww[:, ch * 512:(ch + 1) * 512],
                                  start=True, stop=True).then_inc(mm, 1)
                        cmm.add()

                @b.vector
                def _(veg):
                    for ch in range(8):
                        vop(veg, lambda ch=ch: veg.tensor_copy(
                            rsum[:, ch * 512:(ch + 1) * 512],
                            (psC[:] if ch % 2 == 0 else psD[:])),
                            (mm, mm0 + ch + 1))
                    vop(veg, lambda: veg.tensor_scalar(
                        rb[0:1, :], rsum[:], -1.0, None, ALU.mult))
                    vop(veg, lambda: veg.tensor_tensor(
                        rsum[:], rsum[:], rb[0:1, :], ALU.max))
                    vop(veg, lambda: veg.tensor_scalar(
                        rsum[:], rsum[:], 1.0, None, ALU.max))
                    vop(veg, lambda: veg.reciprocal(rsum[:], rsum[:]))

                @b.sync
                def _(sy):
                    sy.wait_ge(ve, ve0 + 12)
                    sy.dma_start(scr[0:1, :], rsum[:]).then_inc(ly, 16)
                    cly.add()
                    sy.wait_ge(ly, 16 * cly.n)
                    sy.dma_start(rb[:], dram_bcast_ap(scr, 0, 0, L)
                                 ).then_inc(ly, 16); cly.add()

            ve0, ly0 = cve.n, cly.n
            with nc.Block() as b:
                @b.vector
                def _(veg):
                    vop(veg, lambda: veg.tensor_tensor(
                        qs_bf[:], qp[:], rb[:], ALU.mult), (ly, 16 * cly.n))

            for tich in range(8):
                t0 = tich * 512
                nsj = 4 * (tich + 1)
                mm0, ve0, ac0 = cmm.n, cve.n, cac.n
                sb0 = [cst2[0].n, cst2[1].n]
                acT[(pair, tich)] = ac0
                veT[(pair, tich)] = ve0
                with nc.Block() as b:
                    @b.tensor
                    def _(te):
                        te.wait_ge(gld, 16 * cgld.n)
                        te.wait_ge(ve, ve0)
                        te.wait_ge(ac, ac0)
                        for sjb in range(nsj):
                            if sjb >= 2:
                                te.wait_ge(ve, ve0 + sjb - 1)
                            te.matmul(psA[sjb % 2][:],
                                      kbf[:, sjb * 128:(sjb + 1) * 128],
                                      qs_bf[:, t0:t0 + 512],
                                      start=True, stop=True).then_inc(mm, 1)
                            cmm.add()
                            te.wait_ge(ve, ve0 + sjb + 1)
                            te.matmul(psB[tich % 2][:],
                                      vt_bf[:, sjb, :], retbf[sjb % 2][:],
                                      start=(sjb == 0), stop=(sjb == nsj - 1)
                                      ).then_inc(mm, 1)
                            cmm.add()
                        te.wait_ge(ac, ac0 + 2)
                        te.matmul(psC[:], idn["ones128"][:], at_f[:],
                                  start=True, stop=True).then_inc(mm, 1)
                        cmm.add()
                        te.matmul(psD[:], idn["ones128"][:], sq2[:],
                                  start=True, stop=True).then_inc(mm, 1)
                        cmm.add()
                        te.wait_ge(ac, ac0 + 4)
                        te.matmul(pbc1[:], idn["ones1r"][:], s_v[:],
                                  start=True, stop=True).then_inc(mm, 1)
                        cmm.add()
                        te.wait_ge(ve, ve0 + nsj + 5)
                        te.matmul(pbc2[:], idn["ones1r"][:], s_m[:],
                                  start=True, stop=True).then_inc(mm, 1)
                        cmm.add()

                    @b.scalar
                    def _(sc):
                        w = [(mm, mm0 + 2 * nsj)]
                        if tich >= 1:
                            pnsj = 4 * tich
                            w.append((ve, veT[(pair, tich - 1)] + pnsj + 8))
                        aop(sc, lambda: sc.activation(at_f[:],
                                                      psB[tich % 2][:],
                                                      AF.Copy), *w)
                        aop(sc, lambda: sc.activation(sq2[:], at_f[:],
                                                      AF.Square))
                        aop(sc, lambda: sc.activation(s_v[:], s_q[:], AF.Ln,
                                                      bias=idn["eps1"][:]),
                            (ve, ve0 + nsj + 4))
                        aop(sc, lambda: sc.activation(s_v[:], s_v[:], AF.Exp,
                                                      scale=-0.5))

                    @b.vector
                    def _(veg):
                        for sjb in range(nsj):
                            doff = 384 + t0 - sjb * 128
                            vop(veg, lambda sjb=sjb, doff=doff:
                                veg.tensor_tensor(
                                    retbf[sjb % 2][:], psA[sjb % 2][:],
                                    dfull[hloc][:, doff:doff + 512], ALU.mult),
                                (mm, mm0 + 2 * sjb + 1))
                        vop(veg, lambda: veg.tensor_scalar_mul(
                            s_m[:], psC[:], 1.0 / HD),
                            (mm, mm0 + 2 * nsj + 1))
                        vop(veg, lambda: veg.tensor_scalar_mul(
                            s_q[:], psD[:], 1.0 / HD),
                            (mm, mm0 + 2 * nsj + 2))
                        vop(veg, lambda: veg.tensor_tensor(
                            rb[0:1, 0:512], s_m[:], s_m[:], ALU.mult))
                        vop(veg, lambda: veg.tensor_tensor(
                            s_q[:], s_q[:], rb[0:1, 0:512], ALU.subtract))
                        vop(veg, lambda: veg.tensor_tensor(
                            s_m[:], s_m[:], s_v[:], ALU.mult), (ac, ac0 + 4))
                        kbuf = tich % 2
                        vop(veg, lambda kbuf=kbuf: veg.tensor_tensor(
                            gt[kbuf][:], at_f[:], pbc1[:], ALU.mult),
                            (mm, mm0 + 2 * nsj + 3),
                            (st2[kbuf], 16 * cst2[kbuf].n))
                        vop(veg, lambda kbuf=kbuf: veg.tensor_tensor(
                            gt[kbuf][:], gt[kbuf][:], pbc2[:], ALU.subtract),
                            (mm, mm0 + 2 * nsj + 4))
                        vop(veg, lambda kbuf=kbuf: veg.tensor_scalar(
                            gt[kbuf][:], gt[kbuf][:],
                            gnw_sb[:, hloc:hloc + 1],
                            gnb_sb[:, hloc:hloc + 1],
                            ALU.mult, ALU.add))

                    @b.sync
                    def _(sy):
                        kbuf = tich % 2
                        sy.wait_ge(ve, cve.n)
                        if env is None:
                            tgt = bass.AP(tensor=g_out.tensor,
                                          offset=pair * 128 * L + t0,
                                          ap=[[L, 128], [1, 512]])
                        else:
                            b_, hloc_ = pair % 2, pair // 2
                            r0 = (b_ * 4 + tich // 2) * 256 + hloc_ * 128
                            tgt = bass.AP(tensor=g_stage.tensor,
                                          offset=r0 * TC + (tich % 2) * 512,
                                          ap=[[TC, 128], [1, 512]])
                        sy.dma_start(tgt, gt[kbuf][:]).then_inc(st2[kbuf], 16)
                        cst2[kbuf].add()

        with nc.Block() as b:
            @b.sync
            def _(sy):
                sy.wait_ge(st2[0], 16 * cst2[0].n)
                sy.wait_ge(st2[1], 16 * cst2[1].n)

    return nc



def build_launch_E(env=None):
    if env is None:
        nc = bass.Bass("TRN2", target_bir_lowering=False)
        x2_t = nc.dram_tensor("x2_t", [D, TC], F32, kind="ExternalInput").ap()
        g_t = nc.dram_tensor("g_t", [D, TC], F32, kind="ExternalInput").ap()
        gate_t = nc.dram_tensor("gate_t", [D, TC], F32,
                                kind="ExternalInput").ap()
        wo = nc.dram_tensor("wo", [D, D], F32, kind="ExternalInput").ap()
        bo_l = nc.dram_tensor("bo_l", [128, 16], F32, kind="ExternalInput").ap()
        rw = nc.dram_tensor("rw", [NEXP, D], F32, kind="ExternalInput").ap()
        gup = nc.dram_tensor("gup", [NEXP, D, 2 * FI], F32,
                             kind="ExternalInput").ap()
        dwn = nc.dram_tensor("dwn", [NEXP, FI, D], F32,
                             kind="ExternalInput").ap()
        out_t = nc.dram_tensor("out_t", [TC, D], F32, kind="ExternalOutput").ap()
        stats_d = nc.dram_tensor("stats_d", [2, TC], F32).ap()
        comb_d = nc.dram_tensor("comb_d", [NEXP, TC], F32).ap()
        mksem = None
    else:
        nc = env["nc"]
        x2_t = env["x2_t"]; g_t = env["g_t"]; gate_t = env["gate_t"]
        wo = env["wo"]; bo_l = env["bo_l"]; rw = env["rw"]
        gup = env["gup"]; dwn = env["dwn"]; out_t = env["out_t"]
        stats_d = env["stats_d"]; comb_d = env["comb_d"]
        mksem = env["mksem"]

    def g_row(h):
        if env is None:
            return h * 128
        return (h // 2) * 256 + (h % 2) * 128

    stack = ExitStack()
    with stack:
        sb = lambda n_, s, d: stack.enter_context(nc.sbuf_tensor("E" + n_, s, d))
        if mksem is None:
            sem = lambda n_: stack.enter_context(nc.semaphore(n_))
        else:
            sem = lambda n_: mksem("E" + n_)
        _psn = [0]

        def ps_on(st_, shape):
            _psn[0] += 1
            return st_.enter_context(nc.psum_tensor(f"Eps{_psn[0]}", shape, F32))

        x3 = [sb(f"x3_{i}", [128, TC], F32) for i in range(NDT)]
        xnbf = [sb(f"xnbf{i}", [128, TC], BF16) for i in range(NDT)]
        bos = sb("bos", [128, 16], F32)
        idn = env["idn"] if env is not None else alloc_idents(nc, stack)

        ld = sem("ld"); mm = sem("mm"); ve = sem("ve"); po = sem("po")
        ac = sem("ac"); gld = sem("gld"); ly = sem("ly")
        lg2 = [sem("lgA"), sem("lgB")]
        lt2 = [sem("ltA"), sem("ltB")]
        st2 = [sem("stA"), sem("stB")]
        cst2 = [Ctr(), Ctr()]
        cld = Ctr(); cmm = Ctr(); cve = Ctr(); cpo = Ctr()
        cac = Ctr(); cgld = Ctr(); cly = Ctr()

        def vop(veg, fn, *waits):
            for (s, v) in waits:
                veg.wait_ge(s, v)
            veg.wait_ge(ve, cve.n)
            fn().then_inc(ve, 1)
            cve.add()

        def aop(sc, fn, *waits):
            for (s, v) in waits:
                sc.wait_ge(s, v)
            sc.wait_ge(ac, cac.n)
            fn().then_inc(ac, 1)
            cac.add()

        gg_stack = ExitStack()
        gated = [gg_stack.enter_context(
            nc.sbuf_tensor(f"Egated{i}", [128, TC], BF16)) for i in range(NDT)]
        gg = [gg_stack.enter_context(nc.sbuf_tensor(f"Egg{i}", [128, TC], F32))
              for i in range(2)]
        gt2 = [gg_stack.enter_context(nc.sbuf_tensor(f"Egt2{i}", [128, TC], F32))
               for i in range(2)]
        with nc.Block() as b:
            @b.sync
            def _(sy):
                sy.dma_start(bos[:], bo_l[:]).then_inc(ld, 16); cld.add()

            @b.gpsimd
            def _(gp):
                if env is None:
                    emit_idents(gp, idn, po, cpo)

        ve0 = cve.n
        with nc.Block() as b:
            @b.sync
            def _(sy):
                for dt in range(NDT):
                    if dt >= 2:
                        sy.wait_ge(ve, ve0 + dt - 1)
                    sy.dma_start(gg[dt % 2][:],
                                 g_t[g_row(dt):g_row(dt) + 128, :]
                                 ).then_inc(lg2[dt % 2], 16)

            @b.scalar
            def _(sc):
                for dt in range(NDT):
                    if dt >= 2:
                        sc.wait_ge(ve, ve0 + dt - 1)
                    sc.dma_start(gt2[dt % 2][:],
                                 gate_t[dt * 128:(dt + 1) * 128, :]
                                 ).then_inc(lt2[dt % 2], 16)

            @b.gpsimd
            def _(gp):
                for dt in range(NDT):
                    gp.dma_start(x3[dt][:], x2_t[dt * 128:(dt + 1) * 128, :]
                                 ).then_inc(gld, 16); cgld.add()

            @b.vector
            def _(veg):
                for dt in range(NDT):
                    vop(veg, lambda dt=dt: veg.tensor_tensor(
                        gated[dt][:], gt2[dt % 2][:], gg[dt % 2][:], ALU.mult),
                        (lg2[dt % 2], 16 * (dt // 2 + 1)),
                        (lt2[dt % 2], 16 * (dt // 2 + 1)))

        wo_stack = ExitStack()
        wbig2 = [wo_stack.enter_context(
            nc.sbuf_tensor(f"Ewbig{i}", [128, NDT, 512], BF16))
            for i in range(2)]
        pqk = [ps_on(wo_stack, [128, 512]) for _ in range(2)]
        wq2 = [sem("EwqA"), sem("EwqB")]

        def wo_load(gp, q_):
            gp.dma_start(wbig2[q_ % 2][:], bass.AP(
                tensor=wo.tensor, offset=q_ * 512,
                ap=[[D, 128], [D * 128, NDT], [1, 512]])
            ).then_inc(wq2[q_ % 2], 16)

        with nc.Block() as b:
            @b.gpsimd
            def _(gp):
                wo_load(gp, 0)

        for quarter in range(4):
            c0 = quarter * 512
            wbig = wbig2[quarter % 2]
            gld0, mm0, ve0 = cgld.n, cmm.n, cve.n
            with nc.Block() as b:
                @b.gpsimd
                def _(gp):
                    if quarter + 1 < 4:
                        wo_load(gp, quarter + 1)

                @b.tensor
                def _(te):
                    te.wait_ge(wq2[quarter % 2], 16 * (quarter // 2 + 1))
                    for ft in range(4):
                        for tchunk in range(2):
                            t0 = tchunk * 512
                            k = ft * 2 + tchunk
                            if k >= 2:
                                te.wait_ge(ve, ve0 + k - 1)
                            for dt in range(NDT):
                                last = te.matmul(
                                    pqk[k % 2][:],
                                    wbig[:, dt, ft * 128:(ft + 1) * 128],
                                    gated[dt][:, t0:t0 + 512],
                                    start=(dt == 0), stop=(dt == NDT - 1))
                            last.then_inc(mm, 1); cmm.add()

                @b.vector
                def _(veg):
                    for ft in range(4):
                        odt = (c0 + ft * 128) // 128
                        for tchunk in range(2):
                            t0 = tchunk * 512
                            k = ft * 2 + tchunk
                            vop(veg, lambda odt=odt, t0=t0, k=k:
                                veg.scalar_tensor_tensor(
                                    x3[odt][:, t0:t0 + 512], pqk[k % 2][:],
                                    bos[:, odt:odt + 1],
                                    x3[odt][:, t0:t0 + 512],
                                    ALU.add, ALU.add),
                                (mm, mm0 + k + 1),
                                (ld, 16 * cld.n),
                                (gld, 16 * cgld.n))
        wo_stack.close()
        gg_stack.close()

        ln_stack = ExitStack()
        lnsb = lambda n_, s: ln_stack.enter_context(
            nc.sbuf_tensor("E" + n_, s, F32))
        tscr = lnsb("tscr", [128, TC])
        st_s = lnsb("st_s", [1, TC])
        st_q = lnsb("st_q", [1, TC])
        r_rstd = lnsb("r_rstd", [1, TC])
        r_mrstd = lnsb("r_mrstd", [1, TC])
        r_var = lnsb("r_var", [1, TC])
        rstd_b = lnsb("rstd_b", [128, TC])
        mrstd_b = lnsb("mrstd_b", [128, TC])

        def emit_ln(src_tiles, dst_bf):
            pst_stack = ExitStack()
            pstat = ps_on(pst_stack, [1, TC])
            pstatq = ps_on(pst_stack, [1, TC])
            ac0, mm0, ve0 = cac.n, cmm.n, cve.n
            with nc.Block() as b:
                @b.scalar
                def _(sc):
                    for dt in range(NDT):
                        if dt >= 1:
                            sc.wait_ge(mm, mm0 + dt)
                        aop(sc, lambda dt=dt: sc.activation(
                            tscr[:], src_tiles[dt][:], AF.Square))

                @b.tensor
                def _(te):
                    for dt in range(NDT):
                        te.wait_ge(ac, ac0 + dt + 1)
                        for tchunk in range(2):
                            t0 = tchunk * 512
                            te.matmul(pstat[:, t0:t0 + 512], idn["ones128"][:],
                                      src_tiles[dt][:, t0:t0 + 512],
                                      start=(dt == 0), stop=(dt == NDT - 1))
                            last = te.matmul(pstatq[:, t0:t0 + 512],
                                             idn["ones128"][:],
                                             tscr[:, t0:t0 + 512],
                                             start=(dt == 0),
                                             stop=(dt == NDT - 1))
                        last.then_inc(mm, 1); cmm.add()

            ve0, ac0 = cve.n, cac.n
            with nc.Block() as b:
                @b.vector
                def _(veg):
                    vop(veg, lambda: veg.tensor_scalar_mul(st_s[:], pstat[:],
                                                           1.0 / D))
                    vop(veg, lambda: veg.tensor_scalar_mul(st_q[:], pstatq[:],
                                                           1.0 / D))
                    vop(veg, lambda: veg.tensor_tensor(r_var[:], st_s[:],
                                                       st_s[:], ALU.mult))
                    vop(veg, lambda: veg.tensor_tensor(r_var[:], st_q[:],
                                                       r_var[:], ALU.subtract))

                @b.scalar
                def _(sc):
                    aop(sc, lambda: sc.activation(r_rstd[:], r_var[:], AF.Ln,
                                                  bias=idn["eps1"][:]),
                        (ve, ve0 + 4))
                    aop(sc, lambda: sc.activation(r_rstd[:], r_rstd[:],
                                                  AF.Exp, scale=-0.5))

            ve0 = cve.n
            with nc.Block() as b:
                @b.vector
                def _(veg):
                    vop(veg, lambda: veg.tensor_tensor(r_mrstd[:], st_s[:],
                                                       r_rstd[:], ALU.mult))

                @b.sync
                def _(sy):
                    sy.wait_ge(ve, ve0 + 1)
                    sy.dma_start(stats_d[0:1, :], r_rstd[:]).then_inc(ld, 16)
                    cld.add()
                    sy.dma_start(stats_d[1:2, :], r_mrstd[:]).then_inc(ld, 16)
                    cld.add()
                    sy.wait_ge(ld, 16 * cld.n)
                    sy.dma_start(rstd_b[:], dram_bcast_ap(stats_d, 0, 0, TC)
                                 ).then_inc(ld, 16); cld.add()
                    sy.dma_start(mrstd_b[:], dram_bcast_ap(stats_d, 1, 0, TC)
                                 ).then_inc(ld, 16); cld.add()

            with nc.Block() as b:
                @b.vector
                def _(veg):
                    veg.wait_ge(ld, 16 * cld.n)
                    for dt in range(NDT):
                        vop(veg, lambda dt=dt: veg.tensor_tensor(
                            tscr[:], src_tiles[dt][:], rstd_b[:], ALU.mult))
                        vop(veg, lambda dt=dt: veg.tensor_tensor(
                            dst_bf[dt][:], tscr[:], mrstd_b[:], ALU.subtract))
            pst_stack.close()

        emit_ln(x3, xnbf)
        ln_stack.close()

        rt_stack = ExitStack()
        rws = rt_stack.enter_context(nc.sbuf_tensor("Erws", [NEXP, D], F32))
        rwT = rt_stack.enter_context(
            nc.sbuf_tensor("ErwT", [128, NDT, NEXP], BF16))
        lgits = rt_stack.enter_context(
            nc.sbuf_tensor("Elgits", [128, NEXP], F32))
        pr_m = rt_stack.enter_context(nc.sbuf_tensor("Epr_m", [128, 1], F32))
        pr_s = rt_stack.enter_context(nc.sbuf_tensor("Epr_s", [128, 1], F32))
        probs = rt_stack.enter_context(nc.sbuf_tensor("Eprobs", [128, NEXP], F32))
        m8 = rt_stack.enter_context(nc.sbuf_tensor("Em8", [128, 8], F32))
        mrep = rt_stack.enter_context(nc.sbuf_tensor("Emrep", [128, NEXP], F32))
        cmb = rt_stack.enter_context(nc.sbuf_tensor("Ecmb", [128, NEXP], F32))
        combT = rt_stack.enter_context(nc.sbuf_tensor("EcombT", [16, TC], F32))
        ptr16 = ps_on(rt_stack, [128, 16])
        ptc = ps_on(rt_stack, [16, 128])
        prr = [ps_on(rt_stack, [128, NEXP]) for _ in range(2)]

        ld0, mm0, ve0, ac0 = cld.n, cmm.n, cve.n, cac.n
        with nc.Block() as b:
            @b.sync
            def _(sy):
                sy.dma_start(rws[:], rw[:]).then_inc(ld, 16); cld.add()

            @b.tensor
            def _(te):
                te.wait_ge(ld, 16 * cld.n)
                for dt in range(NDT):
                    if dt >= 1:
                        te.wait_ge(ve, ve0 + dt)
                    te.transpose(ptr16[:], rws[:, dt * 128:(dt + 1) * 128],
                                 idn["ident16"][:]).then_inc(mm, 1); cmm.add()

            @b.vector
            def _(veg):
                for dt in range(NDT):
                    vop(veg, lambda dt=dt: veg.tensor_copy(
                        rwT[:, dt, :], ptr16[:]), (mm, mm0 + dt + 1))

        for tb in range(8):
            tb0 = tb * 128
            mm0, ve0, ac0, ly0 = cmm.n, cve.n, cac.n, cly.n
            with nc.Block() as b:
                @b.tensor
                def _(te):
                    for dt in range(NDT):
                        last = te.matmul(prr[tb % 2][:], xnbf[dt][:, tb0:tb0 + 128],
                                         rwT[:, dt, :],
                                         start=(dt == 0), stop=(dt == NDT - 1))
                    last.then_inc(mm, 1); cmm.add()

                @b.vector
                def _(veg):
                    vop(veg, lambda: veg.reduce_max(pr_m[:], prr[tb % 2][:],
                                                    axis=mybir.AxisListType.X,
                                                    negate=True),
                        (mm, mm0 + 1))
                    vop(veg, lambda: veg.tensor_scalar(
                        lgits[:], prr[tb % 2][:], pr_m[:], None, ALU.add))

                @b.scalar
                def _(sc):
                    aop(sc, lambda: sc.activation(probs[:], lgits[:], AF.Exp),
                        (ve, ve0 + 2))

                @b.vector
                def _(veg):
                    vop(veg, lambda: veg.reduce_sum(pr_s[:], probs[:],
                                                    axis=mybir.AxisListType.X),
                        (ac, ac0 + 1))
                    vop(veg, lambda: veg.reciprocal(pr_s[:], pr_s[:]))
                    vop(veg, lambda: veg.tensor_scalar(
                        probs[:], probs[:], pr_s[:], None, ALU.mult))
                    vop(veg, lambda: veg.max(out=m8[:], in_=probs[:]))
                    vop(veg, lambda: veg.memset(m8[:, 4:8], -1e30))
                    vop(veg, lambda: veg.match_replace(
                        out=mrep[:], in_to_replace=m8[:], in_values=probs[:],
                        imm_value=-7.0))
                    vop(veg, lambda: veg.tensor_scalar(
                        mrep[:], mrep[:], -7.0, None, ALU.is_equal))
                    vop(veg, lambda: veg.tensor_tensor(
                        cmb[:], probs[:], mrep[:], ALU.mult))
                    vop(veg, lambda: veg.reduce_sum(pr_s[:], m8[:, 0:4],
                                                    axis=mybir.AxisListType.X))
                    vop(veg, lambda: veg.reciprocal(pr_s[:], pr_s[:]))
                    vop(veg, lambda: veg.tensor_scalar(
                        cmb[:], cmb[:], pr_s[:], None, ALU.mult))

            mm0, ve0 = cmm.n, cve.n
            with nc.Block() as b:
                @b.tensor
                def _(te):
                    te.transpose(ptc[:], cmb[:], idn["ident128"][:]
                                 ).then_inc(mm, 1); cmm.add()

                @b.vector
                def _(veg):
                    vop(veg, lambda: veg.tensor_copy(
                        combT[:, tb0:tb0 + 128], ptc[:]), (mm, mm0 + 1))

        ve0, ly0 = cve.n, cly.n
        with nc.Block() as b:
            @b.sync
            def _(sy):
                sy.dma_start(comb_d[:], combT[:]).then_inc(ly, 16); cly.add()
        comb_store_cnt = cly.n
        rt_stack.close()

        def gup_src(e):
            return bass.AP(tensor=gup.tensor, offset=e * D * 2 * FI,
                           ap=[[2 * FI, 128], [2 * FI * 128, NDT],
                               [1, 2 * FI]])

        def dwn_src(e):
            return bass.AP(tensor=dwn.tensor, offset=e * FI * D,
                           ap=[[D, 128], [D * 128, NDT_FI], [1, D]])

        moe_stack = ExitStack()
        gupbf = moe_stack.enter_context(
            nc.sbuf_tensor("Egupbf", [128, NDT, 2 * FI], BF16))
        dwnbf = moe_stack.enter_context(
            nc.sbuf_tensor("Edwnbf", [128, FI // 128, D], BF16))
        comb_b = moe_stack.enter_context(
            nc.sbuf_tensor("Ecomb_b", [128, TC], F32))
        silu_t = moe_stack.enter_context(
            nc.sbuf_tensor("Esilu_t", [128, 512], BF16))
        up_t = moe_stack.enter_context(
            nc.sbuf_tensor("Eup_t", [128, 512], BF16))
        fuse_e = moe_stack.enter_context(
            nc.sbuf_tensor("Efuse_e", [128, FI // 128, TC], BF16))
        pg = [ps_on(moe_stack, [128, 512]) for _ in range(2)]
        pu = [ps_on(moe_stack, [128, 512]) for _ in range(2)]
        pdn = [ps_on(moe_stack, [128, 512]) for _ in range(2)]
        NFI = FI // 128
        NDT_FI = NFI

        gldG = sem("gldG"); gldD = sem("gldD")
        with nc.Block() as b:
            @b.gpsimd
            def _(gp):
                gp.wait_ge(ly, 16 * comb_store_cnt)
                gp.dma_start(gupbf[:], gup_src(0)).then_inc(gldG, 16)

        mm0E, ve0E, ac0E, ly0E = cmm.n, cve.n, cac.n, cly.n
        mmGU_E = {}
        mmDN_E = {}
        veGU_E = {}
        for e in range(NEXP):
            mmGU_E[e] = mm0E + e * 44 + 12
            mmDN_E[e] = mm0E + (e + 1) * 44
            veGU_E[e] = ve0E + e * 56 + 24
        with nc.Block() as b:
            @b.gpsimd
            def _(gp):
                for e in range(NEXP):
                    if e > 0:
                        gp.wait_ge(mm, mmDN_E[e - 1])
                    gp.dma_start(dwnbf[:], dwn_src(e)).then_inc(gldD, 16)
                    if e + 1 < NEXP:
                        gp.wait_ge(mm, mmGU_E[e])
                        gp.dma_start(gupbf[:], gup_src(e + 1)).then_inc(gldG, 16)

            @b.sync
            def _(sy):
                for e in range(NEXP):
                    if e > 0:
                        sy.wait_ge(ve, veGU_E[e - 1])
                    sy.dma_start(comb_b[:], dram_bcast_ap(comb_d, e, 0, TC)
                                 ).then_inc(ly, 16); cly.add()

            @b.tensor
            def _(te):
                for e in range(NEXP):
                    te.wait_ge(gldG, 16 * (e + 1))
                    for fr in range(NFI):
                        for tchunk in range(2):
                            t0 = tchunk * 512
                            k = fr * 2 + tchunk
                            if k >= 2:
                                te.wait_ge(ve, ve0E + e * 56 + 2 * (k - 2) + 2)
                                te.wait_ge(ac, ac0E + e * 12 + k - 1)
                            elif e > 0:
                                te.wait_ge(ve, veGU_E[e - 1])
                                te.wait_ge(ac, ac0E + e * 12)
                            for dt in range(NDT):
                                te.matmul(pg[tchunk][:],
                                          gupbf[:, dt, fr * 128:(fr + 1) * 128],
                                          xnbf[dt][:, t0:t0 + 512],
                                          start=(dt == 0), stop=(dt == NDT - 1))
                                last = te.matmul(
                                    pu[tchunk][:],
                                    gupbf[:, dt,
                                          FI + fr * 128:FI + (fr + 1) * 128],
                                    xnbf[dt][:, t0:t0 + 512],
                                    start=(dt == 0), stop=(dt == NDT - 1))
                            last.then_inc(mm, 1); cmm.add()
                    te.wait_ge(gldD, 16 * (e + 1))
                    te.wait_ge(ve, veGU_E[e])
                    for odt in range(NDT):
                        for tchunk in range(2):
                            t0 = tchunk * 512
                            k = odt * 2 + tchunk
                            if k >= 2:
                                te.wait_ge(ve, veGU_E[e] + k - 1)
                            for fr in range(NFI):
                                last = te.matmul(
                                    pdn[k % 2][:],
                                    dwnbf[:, fr, odt * 128:(odt + 1) * 128],
                                    fuse_e[:, fr, t0:t0 + 512],
                                    start=(fr == 0), stop=(fr == NFI - 1))
                            last.then_inc(mm, 1); cmm.add()

            @b.scalar
            def _(sc):
                for e in range(NEXP):
                    for fr in range(NFI):
                        for tchunk in range(2):
                            k = fr * 2 + tchunk
                            aop(sc, lambda tchunk=tchunk: sc.activation(
                                silu_t[:], pg[tchunk][:], AF.Silu),
                                (mm, mm0E + e * 44 + k + 1),
                                (ve, ve0E + e * 56 + 2 * k))

            @b.vector
            def _(veg):
                for e in range(NEXP):
                    for fr in range(NFI):
                        for tchunk in range(2):
                            t0 = tchunk * 512
                            k = fr * 2 + tchunk
                            waits = [(mm, mm0E + e * 44 + k + 1),
                                     (ly, 16 * (ly0E + e + 1))]
                            if k == 0 and e > 0:
                                waits.append((mm, mmDN_E[e - 1]))
                            vop(veg, lambda tchunk=tchunk, t0=t0:
                                veg.tensor_tensor(
                                    up_t[:], pu[tchunk][:],
                                    comb_b[:, t0:t0 + 512], ALU.mult),
                                *waits)
                            vop(veg, lambda fr=fr, t0=t0: veg.tensor_tensor(
                                fuse_e[:, fr, t0:t0 + 512], silu_t[:],
                                up_t[:], ALU.mult),
                                (ac, ac0E + e * 12 + k + 1))
                    for odt in range(NDT):
                        for tchunk in range(2):
                            t0 = tchunk * 512
                            k = odt * 2 + tchunk
                            vop(veg, lambda odt=odt, t0=t0, k=k:
                                veg.tensor_tensor(
                                    x3[odt][:, t0:t0 + 512], pdn[k % 2][:],
                                    x3[odt][:, t0:t0 + 512], ALU.add),
                                (mm, mmGU_E[e] + k + 1))

        moe_stack.close()

        fin_stack = ExitStack()
        otile = [fin_stack.enter_context(
            nc.sbuf_tensor(f"Eotile{i}", [128, D], F32)) for i in range(2)]
        ptro = [ps_on(fin_stack, [128, 128]) for _ in range(4)]
        mm0, ve0 = cmm.n, cve.n
        with nc.Block() as b:
            @b.tensor
            def _(te):
                for tt in range(8):
                    for dt in range(NDT):
                        k = tt * NDT + dt
                        if k >= 4:
                            te.wait_ge(ve, ve0 + k - 3)
                        te.transpose(ptro[k % 4][:],
                                     x3[dt][:, tt * 128:(tt + 1) * 128],
                                     idn["ident128"][:]).then_inc(mm, 1)
                        cmm.add()

            @b.vector
            def _(veg):
                for tt in range(8):
                    for dt in range(NDT):
                        k = tt * NDT + dt
                        w = [(mm, mm0 + k + 1)]
                        if dt == 0 and tt >= 2:
                            w.append((st2[tt % 2], 16 * (tt // 2)))
                        vop(veg, lambda tt=tt, dt=dt, k=k: veg.tensor_copy(
                            otile[tt % 2][:, dt * 128:(dt + 1) * 128],
                            ptro[k % 4][:]), *w)

            @b.sync
            def _(sy):
                for tt in range(8):
                    sy.wait_ge(ve, ve0 + (tt + 1) * NDT)
                    sy.dma_start(out_t[tt * 128:(tt + 1) * 128, :],
                                 otile[tt % 2][:]).then_inc(st2[tt % 2], 16)
                    cst2[tt % 2].add()
                sy.wait_ge(st2[0], 16 * cst2[0].n)
                sy.wait_ge(st2[1], 16 * cst2[1].n)
        fin_stack.close()

    return nc



def _col16(v):
    return np.ascontiguousarray(v.reshape(16, 128).T)


def _col64(v):
    return np.ascontiguousarray(v.reshape(64, 128).T)


def kernel_unfused(**inp):

    cores = list(range(NCORES))
    x = np.ascontiguousarray(inp["L0_attn_x"].reshape(TOK, D))

    nc = build_launch_A()
    xpw = np.ascontiguousarray(inp["L0_attn_x_proj_w"])
    maps = [dict(xa=np.ascontiguousarray(x[c * TC:(c + 1) * TC]), xpw=xpw)
            for c in cores]
    res = run_bass_kernel_spmd(nc, maps, core_ids=cores)
    xn_full = np.concatenate([res.results[c]["xn_out"] for c in cores], axis=1)
    xp_full = np.concatenate([res.results[c]["xp_out"] for c in cores], axis=1)

    nc = build_launch_B()
    dtw = inp["L0_attn_dt_proj_w"]
    dtb = inp["L0_attn_dt_proj_b"].reshape(-1, 1)
    alog = inp["L0_attn_A_log"]
    dp = inp["L0_attn_D_param"].reshape(-1, 1)
    maps = []
    for c in cores:
        ds = slice(c * DS, (c + 1) * DS)
        maps.append(dict(
            xn_ds=np.ascontiguousarray(xn_full[ds]), xp_full=xp_full,
            dtw_sl=np.ascontiguousarray(dtw[ds]),
            dtb_sl=np.ascontiguousarray(dtb[ds]),
            alog_sl=np.ascontiguousarray(alog[ds]),
            dp_sl=np.ascontiguousarray(dp[ds])))
    res = run_bass_kernel_spmd(nc, maps, core_ids=cores)
    attn0_full = np.concatenate([res.results[c]["attn0_out"] for c in cores],
                                axis=0)

    nc = build_launch_C()
    common = dict(
        wg8=inp["L0_ffn_w_gate"], wu8=inp["L0_ffn_w_up"],
        wd8=inp["L0_ffn_w_down"],
        sg_l=_col64(inp["L0_ffn_s_gate"]), su_l=_col64(inp["L0_ffn_s_up"]),
        sd_l=_col16(inp["L0_ffn_s_down"]),
        wq=inp["L1_attn_Wq"], wk=inp["L1_attn_Wk"], wv=inp["L1_attn_Wv"],
        wgt=inp["L1_attn_Wg"],
        bq=_col16(inp["L1_attn_bq"]), bk=_col16(inp["L1_attn_bk"]),
        bv=_col16(inp["L1_attn_bv"]), bg=_col16(inp["L1_attn_bg"]))
    maps = []
    for c in cores:
        ts = slice(c * TC, (c + 1) * TC)
        m = dict(common)
        m["xa"] = np.ascontiguousarray(x[ts])
        m["attn0_t"] = np.ascontiguousarray(attn0_full[:, ts])
        maps.append(m)
    res = run_bass_kernel_spmd(nc, maps, core_ids=cores)
    x2_full = np.concatenate([res.results[c]["x2_out"] for c in cores], axis=1)
    q_full = np.concatenate([res.results[c]["q_out"] for c in cores], axis=1)
    k_full = np.concatenate([res.results[c]["k_out"] for c in cores], axis=1)
    v_full = np.concatenate([res.results[c]["v_out"] for c in cores], axis=1)
    gate_full = np.concatenate([res.results[c]["gate_out"] for c in cores],
                               axis=1)

    nc = build_launch_D()
    decay = inp["L1_attn_decay"]
    gnw = inp["L1_attn_gn_w"].reshape(H, HD)
    gnb = inp["L1_attn_gn_b"].reshape(H, HD)
    maps = []
    pair_lists = []
    for c in cores:
        hA, hB = 2 * c, 2 * c + 1
        pairs = [(0, hA), (1, hA), (0, hB), (1, hB)]
        pair_lists.append(pairs)
        qT = np.stack([q_full[h * HD:(h + 1) * HD, b * L:(b + 1) * L]
                       for (b, h) in pairs])
        kT = np.stack([k_full[h * HD:(h + 1) * HD, b * L:(b + 1) * L]
                       for (b, h) in pairs])
        vtm = np.stack([v_full[h * HD:(h + 1) * HD, b * L:(b + 1) * L].T
                        for (b, h) in pairs])
        maps.append(dict(
            qT=np.ascontiguousarray(qT), kT=np.ascontiguousarray(kT),
            vtm=np.ascontiguousarray(vtm),
            decay_d=np.ascontiguousarray(decay[[hA, hB]].reshape(1, 2)),
            gnw_d=np.ascontiguousarray(gnw[[hA, hB]].T),
            gnb_d=np.ascontiguousarray(gnb[[hA, hB]].T)))
    res = run_bass_kernel_spmd(nc, maps, core_ids=cores)
    g_full = np.empty((D, TOK), np.float32)
    for c in cores:
        go = res.results[c]["g_out"].reshape(4, HD, L)
        for pi, (b, h) in enumerate(pair_lists[c]):
            g_full[h * HD:(h + 1) * HD, b * L:(b + 1) * L] = go[pi]

    nc = build_launch_E()
    common = dict(
        wo=inp["L1_attn_Wo"], bo_l=_col16(inp["L1_attn_bo"]),
        rw=inp["L1_ffn_router_weight"],
        gup=inp["L1_ffn_gate_up_proj"], dwn=inp["L1_ffn_down_proj"])
    maps = []
    for c in cores:
        ts = slice(c * TC, (c + 1) * TC)
        m = dict(common)
        m["x2_t"] = np.ascontiguousarray(x2_full[:, ts])
        m["g_t"] = np.ascontiguousarray(g_full[:, ts])
        m["gate_t"] = np.ascontiguousarray(gate_full[:, ts])
        maps.append(m)
    res = run_bass_kernel_spmd(nc, maps, core_ids=cores)
    out = np.concatenate([res.results[c]["out_t"] for c in cores], axis=0)
    return out.reshape(B, L, D)



def build_fused():
    import os as _os
    _os.environ["TRN_ALLOW_ICI_ALL_TO_ALL"] = "1"
    nc = bass.Bass("TRN2", target_bir_lowering=False)
    ein = lambda n_, s, d: nc.dram_tensor(n_, s, d, kind="ExternalInput").ap()
    intl = lambda n_, s, d=F32: nc.dram_tensor(n_, s, d).ap()

    xa = ein("xa", [TC, D], F32)
    xpw = ein("xpw", [RK + 2 * NST, D], F32)
    dtw_sl = ein("dtw_sl", [DS, RK], F32)
    dtb_sl = ein("dtb_sl", [DS, 1], F32)
    alog_sl = ein("alog_sl", [DS, NST], F32)
    dp_sl = ein("dp_sl", [DS, 1], F32)
    wg8 = ein("wg8", [D, FF], I8)
    wu8 = ein("wu8", [D, FF], I8)
    wd8 = ein("wd8", [FF, D], I8)
    sg_l = ein("sg_l", [128, 64], F32)
    su_l = ein("su_l", [128, 64], F32)
    sd_l = ein("sd_l", [128, 16], F32)
    wq = ein("wq", [D, D], F32)
    wk = ein("wk", [D, D], F32)
    wv = ein("wv", [D, D], F32)
    wgt = ein("wgt", [D, D], F32)
    bq = ein("bq", [128, 16], F32)
    bk = ein("bk", [128, 16], F32)
    bv = ein("bv", [128, 16], F32)
    bg = ein("bg", [128, 16], F32)
    decay_d = ein("decay_d", [1, 2], F32)
    gnw_d = ein("gnw_d", [128, 2], F32)
    gnb_d = ein("gnb_d", [128, 2], F32)
    wo = ein("wo", [D, D], F32)
    bo_l = ein("bo_l", [128, 16], F32)
    rw = ein("rw", [NEXP, D], F32)
    gup = ein("gup", [NEXP, D, 2 * FI], F32)
    dwn = ein("dwn", [NEXP, FI, D], F32)
    out_t = nc.dram_tensor("out_t", [TC, D], F32, kind="ExternalOutput").ap()

    xn_loc = intl("xn_loc", [D, TC])
    xn_a2a = intl("xn_a2a", [D, TC])
    xp_loc = intl("xp_loc", [160, TC])
    xp_ag = intl("xp_ag", [8 * 160, TC])
    attn0_pre = intl("attn0_pre", [D, TC])
    attn0_t = intl("attn0_t2", [D, TC])
    y_dram = intl("y_dram", [DS, TOK])
    q_loc = intl("q_loc", [D, TC])
    k_loc = intl("k_loc", [D, TC])
    v_loc = intl("v_loc", [D, TC])
    q_a2a = intl("q_a2a", [D, TC])
    k_a2a = intl("k_a2a", [D, TC])
    v_a2a = intl("v_a2a", [D, TC])
    x2_loc = intl("x2_loc", [D, TC])
    gate_loc = intl("gate_loc", [D, TC])
    g_stage = intl("g_stage", [D, TC])
    g_a2a = intl("g_a2a", [D, TC])
    stats_dA = intl("stats_dA", [2, TC])
    stats_dC = intl("stats_dC", [2, TC])
    stats_dE = intl("stats_dE", [2, TC])
    comb_d = intl("comb_d", [NEXP, TC])
    scrD = intl("scrD", [4, L])

    RGRP = [list(range(NCORES))]

    gstack = ExitStack()
    with gstack:
        idn = alloc_idents(nc, gstack)
        semnames = {}

        def mksem(name):
            assert name not in semnames
            s = gstack.enter_context(nc.semaphore(name))
            semnames[name] = s
            return s

        ccsem = mksem("ccsem")
        ncc = [0]

        def collective(kind, pairs_list):
            with nc.Block() as b:
                @b.gpsimd
                def _(gp):
                    for (src, dst) in pairs_list:
                        gp.collective_compute(
                            kind, ALU.bypass, replica_groups=RGRP,
                            ins=[src.opt()], outs=[dst.opt()],
                        ).then_inc(ccsem)
                        ncc[0] += 1
                    gp.wait_ge(ccsem, ncc[0])

        with nc.Block() as b:
            @b.gpsimd
            def _(gp):
                emit_idents(gp, idn, mksem("identpo"), Ctr())

        env = dict(nc=nc, mksem=mksem, idn=idn)

        with nc.named_scope("phA"):
            build_launch_A(dict(env, xa=xa, xpw=xpw, xn_out=xn_loc,
                                xp_out=xp_loc, stats_d=stats_dA))
        with nc.named_scope("ccA"):
            with nc.Block() as b:
                @b.gpsimd
                def _(gp):
                    gp.collective_compute(
                        "AllToAll", ALU.bypass, replica_groups=RGRP,
                        ins=[xn_loc.opt()], outs=[xn_a2a.opt()],
                    ).then_inc(ccsem)
                    ncc[0] += 1
                    gp.collective_compute(
                        "AllGather", ALU.bypass, replica_groups=RGRP,
                        ins=[xp_loc.opt()], outs=[xp_ag.opt()],
                    ).then_inc(ccsem)
                    ncc[0] += 1
                    gp.wait_ge(ccsem, ncc[0])

        with nc.named_scope("phB"):
            build_launch_B(dict(env, dtw_sl=dtw_sl, dtb_sl=dtb_sl,
                                alog_sl=alog_sl, dp_sl=dp_sl, y_dram=y_dram,
                                xn_a2a=xn_a2a, xp_ag=xp_ag,
                                attn0_stage=attn0_pre))
        with nc.named_scope("ccB"):
            collective("AllToAll", [(attn0_pre, attn0_t)])

        with nc.named_scope("phC"):
            build_launch_C(dict(env, xa=xa, attn0_t=attn0_t, wg8=wg8, wu8=wu8,
                                wd8=wd8, sg_l=sg_l, su_l=su_l, sd_l=sd_l,
                                wq=wq, wk=wk, wv=wv, wgt=wgt, bq=bq, bk=bk,
                                bv=bv, bg=bg, x2_out=x2_loc, q_out=q_loc,
                                k_out=k_loc, v_out=v_loc, gate_out=gate_loc,
                                stats_d=stats_dC))
        with nc.named_scope("ccC"):
            collective("AllToAll", [(q_loc, q_a2a), (k_loc, k_a2a),
                                    (v_loc, v_a2a)])

        with nc.named_scope("phD"):
            build_launch_D(dict(env, decay_d=decay_d, gnw_d=gnw_d, gnb_d=gnb_d,
                                q_a2a=q_a2a, k_a2a=k_a2a, v_a2a=v_a2a,
                                g_stage=g_stage, scr=scrD))
        with nc.named_scope("ccD"):
            collective("AllToAll", [(g_stage, g_a2a)])

        with nc.named_scope("phE"):
            build_launch_E(dict(env, x2_t=x2_loc, g_t=g_a2a, gate_t=gate_loc,
                                wo=wo, bo_l=bo_l, rw=rw, gup=gup, dwn=dwn,
                                out_t=out_t, stats_d=stats_dE, comb_d=comb_d))

    return nc


def kernel(**inp):
    import os as _os
    _os.environ["TRN_ALLOW_ICI_ALL_TO_ALL"] = "1"
    cores = list(range(NCORES))
    x = np.ascontiguousarray(inp["L0_attn_x"].reshape(TOK, D))
    dtw = inp["L0_attn_dt_proj_w"]
    dtb = inp["L0_attn_dt_proj_b"].reshape(-1, 1)
    alog = inp["L0_attn_A_log"]
    dp = inp["L0_attn_D_param"].reshape(-1, 1)
    decay = inp["L1_attn_decay"]
    gnw = inp["L1_attn_gn_w"].reshape(H, HD)
    gnb = inp["L1_attn_gn_b"].reshape(H, HD)
    common = dict(
        xpw=np.ascontiguousarray(inp["L0_attn_x_proj_w"]),
        wg8=inp["L0_ffn_w_gate"], wu8=inp["L0_ffn_w_up"],
        wd8=inp["L0_ffn_w_down"],
        sg_l=_col64(inp["L0_ffn_s_gate"]), su_l=_col64(inp["L0_ffn_s_up"]),
        sd_l=_col16(inp["L0_ffn_s_down"]),
        wq=inp["L1_attn_Wq"], wk=inp["L1_attn_Wk"], wv=inp["L1_attn_Wv"],
        wgt=inp["L1_attn_Wg"],
        bq=_col16(inp["L1_attn_bq"]), bk=_col16(inp["L1_attn_bk"]),
        bv=_col16(inp["L1_attn_bv"]), bg=_col16(inp["L1_attn_bg"]),
        wo=inp["L1_attn_Wo"], bo_l=_col16(inp["L1_attn_bo"]),
        rw=inp["L1_ffn_router_weight"],
        gup=inp["L1_ffn_gate_up_proj"], dwn=inp["L1_ffn_down_proj"])
    nc = build_fused()
    maps = []
    for c in cores:
        ds = slice(c * DS, (c + 1) * DS)
        hA, hB = 2 * c, 2 * c + 1
        m = dict(common)
        m["xa"] = np.ascontiguousarray(x[c * TC:(c + 1) * TC])
        m["dtw_sl"] = np.ascontiguousarray(dtw[ds])
        m["dtb_sl"] = np.ascontiguousarray(dtb[ds])
        m["alog_sl"] = np.ascontiguousarray(alog[ds])
        m["dp_sl"] = np.ascontiguousarray(dp[ds])
        m["decay_d"] = np.ascontiguousarray(decay[[hA, hB]].reshape(1, 2))
        m["gnw_d"] = np.ascontiguousarray(gnw[[hA, hB]].T)
        m["gnb_d"] = np.ascontiguousarray(gnb[[hA, hB]].T)
        maps.append(m)
    res = run_bass_kernel_spmd(nc, maps, core_ids=cores)
    out = np.concatenate([res.results[c]["out_t"] for c in cores], axis=0)
    return out.reshape(B, L, D)


